# version 03
# baseline (speedup 1.0000x reference)
import os
import numpy as np
import ml_dtypes
from contextlib import ExitStack
import concourse.bass as bass
import concourse.mybir as mybir
from concourse.bass_utils import run_bass_kernel_spmd

F32 = mybir.dt.float32
BF16 = mybir.dt.bfloat16
AF = mybir.ActivationFunctionType
ALU = mybir.AluOpType
AX = mybir.AxisListType
ENGS = ("sync", "gpsimd", "tensor", "scalar", "vector")

N = 8192
D = 1024
NT = 16
TT = 512


class _SkipRest(Exception):
    pass


class Ctx:
    def __init__(self, nc):
        self.nc = nc
        self.streams = {e: [] for e in ENGS}
        self.sems = {}
        self.counts = {}

    def sem(self, es, name):
        h = es.enter_context(self.nc.semaphore(name))
        self.sems[name] = h
        self.counts[name] = 0

    SELFSEM = {"vector": "v", "scalar": "s", "gpsimd": "g"}

    def op(self, eng, fn, inc=None, n=1):
        if not getattr(self, "enabled", True):
            return self.counts.get(inc, 0) if inc else None
        ss = self.SELFSEM.get(eng)
        if ss is not None and self.counts.get(ss, 0) > 0:
            self.streams[eng].append(("wait", self.sems[ss], self.counts[ss]))
        if inc is not None:
            self.counts[inc] += n
            pair = (self.sems[inc], n)
        else:
            pair = None
        self.streams[eng].append(("op", fn, pair))
        return self.counts[inc] if inc else None

    def wait(self, eng, sem_name, val=None):
        if not getattr(self, "enabled", True):
            return
        v = self.counts[sem_name] if val is None else val
        if v > 0:
            self.streams[eng].append(("wait", self.sems[sem_name], v))

    def barrier(self):
        snap = dict(self.counts)
        for e in ENGS:
            for s, v in snap.items():
                self.wait(e, s, v)

    def finalize(self, block):
        for e in ENGS:
            stream = self.streams[e]

            def make(stream):
                def run(eng):
                    for item in stream:
                        if item[0] == "op":
                            _, fn, pair = item
                            inst = fn(eng)
                            if pair is not None and inst is not None:
                                inst.then_inc(pair[0], pair[1])
                        else:
                            _, semh, v = item
                            eng.wait_ge(semh, v)
                return run
            getattr(block, e)(make(stream))




def build_l1():
    PH = os.environ.get("L1PHASE", "D")
    CUT = int(os.environ.get("L1CUT", "9"))
    NORANK1 = os.environ.get("L1NORANK1", "0") == "1"
    nc = bass.Bass("TRN2", target_bir_lowering=False, debug=False)
    xT = nc.dram_tensor("xT", [D, N], F32, kind="ExternalInput").ap()
    wqkvT = nc.dram_tensor("wqkvT", [D, 768], F32, kind="ExternalInput").ap()
    gamma_c = nc.dram_tensor("gamma_c", [128, 8], F32, kind="ExternalInput").ap()
    beta_cb = nc.dram_tensor("beta_cb", [128, 8], BF16, kind="ExternalInput").ap()
    identb_i = nc.dram_tensor("identb", [128, 128], BF16, kind="ExternalInput").ap()
    onesb_i = nc.dram_tensor("onesb", [128, 128], BF16, kind="ExternalInput").ap()
    onesrow_i = nc.dram_tensor("onesrow", [1, 512], BF16, kind="ExternalInput").ap()
    onesrowf_i = nc.dram_tensor("onesrow_f", [1, 64], F32, kind="ExternalInput").ap()
    masks_i = nc.dram_tensor("masks", [128, 5 * 512], BF16, kind="ExternalInput").ap()
    csts_i = nc.dram_tensor("csts", [128, 4], F32, kind="ExternalInput").ap()
    oneshh_i = nc.dram_tensor("oneshh", [128, 2], BF16, kind="ExternalInput").ap()
    aout = nc.dram_tensor("attnoutT", [256, N], BF16, kind="ExternalOutput").ap()
    negm_dram = nc.dram_tensor("negm_scratch", [1, 64], BF16, kind="Internal").ap()

    es = ExitStack()
    sb = lambda name, shape, dt: es.enter_context(nc.sbuf_tensor(name, shape, dt))
    with es:
        c = Ctx(nc)
        for s in ("xc", "xw", "d0", "d1", "g", "t", "s", "v", "o0", "o1"):
            c.sem(es, s)

        xt = [sb(f"xt{i}", [128, 8 * TT], F32) for i in range(2)]
        xb = sb("xb", [128, 8 * TT], BF16)
        xs = sb("xs", [128, 8 * TT], BF16)
        wstage = sb("wstage", [128, 768], F32)
        wraw = sb("wraw", [128, 768], BF16)
        wT = sb("wT", [128, 8 * 768], BF16)
        gam = sb("gam", [128, 8], F32)
        betab = sb("betab", [128, 8], BF16)
        identb = sb("identb_s", [128, 128], BF16)
        onesb = sb("onesb_s", [128, 128], BF16)
        onesrow = sb("onesrow_s", [1, 512], BF16)
        onesrow_f = sb("onesrowf_s", [1, 64], F32)
        maskt = sb("maskt", [128, 5 * 512], BF16)
        colsum_row = sb("colsum_row", [1, 768], BF16)
        bias_row = sb("bias_row", [1, 768], BF16)
        shift_row = sb("shift_row", [1, 512], BF16)
        st0 = sb("st0", [128, TT], F32)
        st1 = sb("st1", [128, TT], F32)
        st2 = sb("st2", [128, TT], F32)
        st3 = sb("st3", [128, TT], F32)
        s_bc = sb("s_bc", [128, TT], F32)
        shift_bc = sb("shift_bc", [128, TT], F32)
        Qp = [sb(f"Qp{i}", [128, N], BF16) for i in range(2)]
        Kp = [sb(f"Kp{i}", [128, N], BF16) for i in range(2)]
        V_km = sb("V_km", [128, 64 * 260], BF16)
        vstage = sb("vstage", [128, 1024], BF16)
        PT = [sb(f"PT{i}", [128, 1536], BF16) for i in range(2)]
        biasneg = sb("biasneg", [128, 64], F32)
        recip_row = sb("recip_row", [1, 512], F32)
        accst = [sb(f"accst{i}", [65, 512], F32) for i in range(2)]
        recip_bc = sb("recip_bc", [64, 512], F32)
        aout_sb = [sb(f"aout{i}", [64, 512], BF16) for i in range(2)]
        cst = sb("cst", [128, 4], F32)
        oneshh = sb("oneshh_s", [128, 2], BF16)
        negm2 = sb("negm2", [2, 32], F32)
        negm2b = sb("negm2b", [2, 32], BF16)
        negmrow = sb("negmrow", [1, 64], BF16)
        nc.const_aps.register(F32, 1.0 / 1024.0, cst[:, 0:1])
        nc.const_aps.register(F32, 1e-5, cst[:, 1:2])
        nc.const_aps.register(F32, -0.5, cst[:, 2:3])
        nc.const_aps.register(F32, -1.0, cst[:, 3:4])

        es_b = ExitStack()
        pb = lambda name, shape, dt: es_b.enter_context(nc.psum_tensor(name, shape, dt))
        p_sumx = pb("p_sumx", [128, TT], F32)
        p_sumx2 = pb("p_sumx2", [128, TT], F32)
        p_qkv = pb("p_qkv", [128, 3 * 512], F32)
        p_vtr = pb("p_vtr", [128, 2 * 1024], BF16)
        p_nb = p_sumx

        for dst, src in [(gam, gamma_c), (betab, beta_cb), (identb, identb_i),
                         (onesb, onesb_i), (onesrow, onesrow_i),
                         (onesrow_f, onesrowf_i), (maskt, masks_i), (cst, csts_i),
                         (oneshh, oneshh_i)]:
            c.op("sync", lambda e, d=dst, s=src: e.dma_start(out=d[:], in_=s), inc="xc", n=16)
        vko = V_km[:].rearrange("p (c h d) -> p c h d", h=4, d=65)
        c.op("vector", lambda e: e.memset(vko[:, :, :, 64:65], 1.0), inc="v")
        c.wait("vector", "xc")
        for dc in range(8):
            wv = c.counts["v"]
            c.wait("sync", "v", wv)
            if dc > 0:
                c.wait("sync", "t")
            c.op("sync", lambda e, dc=dc: e.dma_start(
                out=wstage[:], in_=wqkvT[dc * 128:(dc + 1) * 128, :]), inc="xw", n=16)
            c.wait("vector", "xw")
            c.op("vector", lambda e, dc=dc: e.tensor_scalar(
                wT[:, dc * 768:(dc + 1) * 768], wstage[:], gam[:, dc:dc + 1], None,
                ALU.mult), inc="v")
            c.op("vector", lambda e: e.tensor_copy(wraw[:], wstage[:]), inc="v")
            wr_v = c.counts["v"]
            c.wait("tensor", "v", wr_v)
            c.op("tensor", lambda e, dc=dc: e.matmul(
                p_sumx[0:1, 0:512], betab[:, dc:dc + 1], wraw[:, 0:512],
                start=(dc == 0), stop=(dc == 7)), inc="t")
            c.op("tensor", lambda e, dc=dc: e.matmul(
                p_sumx2[0:1, 0:256], betab[:, dc:dc + 1], wraw[:, 512:768],
                start=(dc == 0), stop=(dc == 7)), inc="t")

        c.wait("tensor", "v")
        for (o0, o1) in ((0, 512), (512, 768)):
            for dc in range(8):
                c.op("tensor", lambda e, dc=dc, o0=o0, o1=o1: e.matmul(
                    p_qkv[0:1, o0:o1], onesb[:, 0:1],
                    wT[:, dc * 768 + o0: dc * 768 + o1],
                    start=(dc == 0), stop=(dc == 7)),
                    inc=("t" if (dc == 7 and o0 == 512) else None))
        c.wait("vector", "t")
        c.op("vector", lambda e: e.tensor_copy(colsum_row[:], p_qkv[0:1, 0:768]), inc="v")
        c.op("vector", lambda e: e.tensor_copy(bias_row[0:1, 0:512], p_sumx[0:1, 0:512]), inc="v")
        c.op("vector", lambda e: e.tensor_copy(bias_row[0:1, 512:768], p_sumx2[0:1, 0:256]), inc="v")

        NTILES = NT if PH != "A" else 0
        xt_free_v = [0, 0]
        qkv_evac_v = {}
        vtr_free_v = {}
        for t in range(NTILES):
            buf = t % 2
            c.wait("sync", "v", xt_free_v[buf])
            dsem = "d0" if buf == 0 else "d1"
            c.op("sync", lambda e, t=t, buf=buf: e.dma_start(
                out=xt[buf][:].rearrange("p (c n) -> p c n", n=TT),
                in_=xT.rearrange("(c p) n -> p c n", p=128)[:, :, t * TT:(t + 1) * TT]),
                inc=dsem, n=16)
            c.wait("vector", dsem)
            c.op("vector", lambda e, buf=buf: e.tensor_copy(xb[:], xt[buf][:]), inc="v")
            c.op("vector", lambda e: e.tensor_tensor(xs[:], xb[:], xb[:], ALU.mult), inc="v")
            xb_ready = c.counts["v"]
            c.enabled = CUT >= 1 and PH != "A"
            c.wait("tensor", "v", xb_ready)
            if t > 0:
                c.wait("tensor", "s", stats_consumed_s)
                c.wait("tensor", "v", stats_consumed_v)
            for dc in range(8):
                c.op("tensor", lambda e, dc=dc: e.matmul(
                    p_sumx[:], onesb[:], xb[:, dc * TT:(dc + 1) * TT],
                    start=(dc == 0), stop=(dc == 7)))
            for dc in range(8):
                c.op("tensor", lambda e, dc=dc: e.matmul(
                    p_sumx2[:], onesb[:], xs[:, dc * TT:(dc + 1) * TT],
                    start=(dc == 0), stop=(dc == 7)),
                    inc=("t" if dc == 7 else None))
            stats_t = c.counts["t"]
            c.wait("scalar", "t", stats_t)
            c.op("scalar", lambda e: e.activation(st0[:], p_sumx[:], AF.Square,
                                                  scale=1.0 / 1024.0), inc="s")
            stats_consumed_s = c.counts["s"]
            c.wait("vector", "s")
            c.op("vector", lambda e: e.scalar_tensor_tensor(
                st1[:], p_sumx2[:], 1.0 / 1024.0, st0[:], ALU.mult, ALU.subtract), inc="v")
            c.wait("scalar", "v")
            c.op("scalar", lambda e: e.activation(st2[:], st1[:], AF.Ln, bias=1e-5), inc="s")
            c.op("scalar", lambda e: e.activation(st0[:], st2[:], AF.Exp, scale=-0.5), inc="s")
            c.op("scalar", lambda e: e.activation(st3[:], st2[:], AF.Exp, scale=-1.0), inc="s")
            c.wait("vector", "s")
            c.op("vector", lambda e: e.tensor_scalar(
                st1[:], st3[:], -1e-5, 1.0 + 1e-5, ALU.mult, ALU.add), inc="v")
            c.wait("scalar", "v")
            c.op("scalar", lambda e: e.activation(st2[:], st1[:], AF.Ln), inc="s")
            c.op("scalar", lambda e: e.activation(st3[:], st2[:], AF.Exp, scale=-0.5), inc="s")
            c.wait("vector", "s")
            c.op("vector", lambda e: e.tensor_tensor(s_bc[:], st0[:], st3[:], ALU.mult), inc="v")
            c.op("vector", lambda e: e.scalar_tensor_tensor(
                shift_bc[:], p_sumx[:], -1.0 / 1024.0, s_bc[:], ALU.mult, ALU.mult), inc="v")
            stats_consumed_v = c.counts["v"]
            c.op("vector", lambda e: e.tensor_copy(shift_row[:], shift_bc[0:1, :]), inc="v")
            c.enabled = CUT >= 2 and PH != "A"
            sbc3 = s_bc[:].rearrange("p (a n) -> p a n", a=1).broadcast_to([128, 8, TT])
            c.op("vector", lambda e, buf=buf: e.tensor_tensor(
                xs[:].rearrange("p (c n) -> p c n", n=TT),
                xt[buf][:].rearrange("p (c n) -> p c n", n=TT), sbc3, ALU.mult), inc="v")
            xt_free_v[buf] = c.counts["v"]
            xs_ready = c.counts["v"]

            c.enabled = CUT >= 3 and PH != "A"
            c.wait("tensor", "v", xs_ready)
            for ec in range(6):
                bank = ec % 3
                if qkv_evac_v.get(bank):
                    c.wait("tensor", "v", qkv_evac_v[bank])
                for dc in range(8):
                    c.op("tensor", lambda e, dc=dc, ec=ec, bank=bank: e.matmul(
                        p_qkv[:, bank * 512:(bank + 1) * 512],
                        wT[:, dc * 768 + ec * 128: dc * 768 + (ec + 1) * 128],
                        xs[:, dc * TT:(dc + 1) * TT],
                        start=(dc == 0), stop=(NORANK1 and dc == 7)),
                        inc=("t" if (NORANK1 and dc == 7) else None))
                if not NORANK1:
                    c.op("tensor", lambda e, ec=ec, bank=bank: e.matmul(
                        p_qkv[:, bank * 512:(bank + 1) * 512],
                        colsum_row[0:1, ec * 128:(ec + 1) * 128], shift_row[:],
                        start=False, stop=False))
                    c.op("tensor", lambda e, ec=ec, bank=bank: e.matmul(
                        p_qkv[:, bank * 512:(bank + 1) * 512],
                        bias_row[0:1, ec * 128:(ec + 1) * 128], onesrow[:],
                        start=False, stop=True), inc="t")
                mm_t = c.counts["t"]
                c.wait("vector", "t", mm_t)
                pslice = (lambda bank=bank: p_qkv[:, bank * 512:(bank + 1) * 512])()
                if ec < 2:
                    c.op("vector", lambda e, ec=ec, t=t, ps=pslice: e.tensor_scalar(
                        Qp[ec][:, t * TT:(t + 1) * TT], ps, 0.125, None, ALU.mult), inc="v")
                elif ec < 4:
                    c.op("vector", lambda e, ec=ec, t=t, ps=pslice: e.tensor_copy(
                        Kp[ec - 2][:, t * TT:(t + 1) * TT], ps), inc="v")
                else:
                    c.op("vector", lambda e, ec=ec, ps=pslice: e.tensor_copy(
                        vstage[:, (ec - 4) * 512:(ec - 3) * 512], ps), inc="v")
                qkv_evac_v[bank] = c.counts["v"]
            vstage_ready = c.counts["v"]

            c.enabled = CUT >= 4 and PH != "A"
            c.wait("tensor", "v", vstage_ready)
            for ec in range(2):
                for sc in range(4):
                    tb = (ec * 4 + sc) % 2
                    if t > 0 or ec > 0 or sc > 1:
                        c.wait("tensor", "v", vtr_free_v.get(tb, 0))
                    c.op("tensor", lambda e, ec=ec, sc=sc, tb=tb: e.transpose(
                        p_vtr[:, tb * 1024: tb * 1024 + 128],
                        vstage[:, ec * 512 + sc * 128: ec * 512 + (sc + 1) * 128],
                        identb[:]), inc="t")
                    tr_t = c.counts["t"]
                    cc = 4 * t + sc
                    c.wait("vector", "t", tr_t)
                    c.op("vector", lambda e, cc=cc, ec=ec, tb=tb: e.tensor_copy(
                        V_km[:, cc * 260 + ec * 130: cc * 260 + (ec + 1) * 130]
                        .rearrange("p (h d) -> p h d", d=65)[:, :, 0:64],
                        p_vtr[:, tb * 1024: tb * 1024 + 128]
                        .rearrange("p (h d) -> p h d", d=64)), inc="v")
                    vtr_free_v[tb] = c.counts["v"]

        c.enabled = True
        for p in (range(2) if PH not in ("A", "B") else range(0)):
            for half in range(2):
                c.op("vector", lambda e, p=p, half=half: e.tensor_tensor(
                    xb[:], Qp[p][:, half * 4096:(half + 1) * 4096],
                    Kp[p][:, half * 4096:(half + 1) * 4096], ALU.mult), inc="v")
                qk_v = c.counts["v"]
                c.wait("tensor", "v", qk_v)
                for t8 in range(8):
                    c.op("tensor", lambda e, t8=t8: e.matmul(
                        p_nb[0:2, 0:512], oneshh[:, 0:2], xb[:, t8 * 512:(t8 + 1) * 512],
                        start=True, stop=True), inc="t")
                    mm_t = c.counts["t"]
                    c.wait("vector", "t", mm_t)
                    qt = half * 8 + t8
                    c.op("vector", lambda e, p=p, qt=qt: e.reduce_max(
                        negm2[0:2, p * 16 + qt: p * 16 + qt + 1],
                        p_nb[0:2, 0:512], axis=AX.X), inc="v")
                    c.wait("tensor", "v")
        c.enabled = PH not in ("A", "B")
        c.op("vector", lambda e: e.tensor_scalar(
            negm2b[:], negm2[:], -1.0, None, ALU.mult), inc="v")
        c.wait("gpsimd", "v")
        c.op("gpsimd", lambda e: e.dma_start(
            out=negm_dram.rearrange("o (r pq) -> (o r) pq", r=2), in_=negm2b[:]),
            inc="g", n=16)
        c.wait("gpsimd", "g")
        c.op("gpsimd", lambda e: e.dma_start(
            out=negmrow[:].rearrange("o (p r q) -> o p r q", p=2, r=2),
            in_=negm_dram.rearrange("o (r p q) -> o p r q", r=2, p=2)),
            inc="g", n=16)
        c.wait("tensor", "g")
        for h in range(4):
            for qt in range(NT):
                c.op("tensor", lambda e, h=h, qt=qt: e.matmul(
                    p_nb[:, 64 + h * 16 + qt: 65 + h * 16 + qt],
                    onesrow[0:1, 0:128], negmrow[0:1, h * 16 + qt: h * 16 + qt + 1],
                    start=True, stop=True),
                    inc=("t" if (h == 3 and qt == NT - 1) else None))
        c.wait("vector", "t")
        c.op("vector", lambda e: e.tensor_copy(biasneg[:], p_nb[:, 64:128]), inc="v")

        c.enabled = PH not in ("A", "B", "C")
        c.barrier()
        es_b.close()
        p_sim = [es.enter_context(nc.psum_tensor(f"p_sim{i}", [128, 1536], F32))
                 for i in range(2)]
        p_acc = es.enter_context(nc.psum_tensor("p_acc", [65, 512], F32))
        p_bcast = es.enter_context(nc.psum_tensor("p_bcast", [64, 512], F32))

        sim_free_s = [0, 0]
        pt_free_t = [0, 0]
        acc_free_v = 0
        bcast_free_v = 0
        aout_free_o = [0, 0]
        nqt_done = 0
        NH = 1 if PH == "D1" else 4
        NQT = 2 if PH == "D1" else NT
        for h in range(NH):
            p = h // 2
            rb = 64 * (h % 2)
            for qt in range(NQT):
                nk = 4 * qt + 4
                ng = (nk + 2) // 3
                exp_s = []
                av_count = 0
                total_av = 3 * ng
                for g in range(ng):
                    jb = g % 2
                    c.wait("tensor", "s", sim_free_s[jb])
                    simops = []
                    for j in range(3):
                        kc = 3 * g + j
                        kce = min(kc, 63)
                        patt = None
                        if kc > 4 * qt + 3:
                            patt = 4
                        elif kc >= 4 * qt:
                            patt = kc - 4 * qt
                        simops.append(lambda e, jb=jb, j=j, kce=kce, p=p, rb=rb, qt=qt, patt=patt: e.matmul(
                            p_sim[jb][:, j * 512:(j + 1) * 512],
                            Kp[p][rb:rb + 64, kce * 128:(kce + 1) * 128],
                            Qp[p][rb:rb + 64, qt * 512:(qt + 1) * 512],
                            start=True, stop=(patt is None)))
                        if patt is not None:
                            simops.append(lambda e, jb=jb, j=j, patt=patt: e.matmul(
                                p_sim[jb][:, j * 512:(j + 1) * 512],
                                identb[:], maskt[:, patt * 512:(patt + 1) * 512],
                                start=False, stop=True))
                    for fi, f in enumerate(simops):
                        c.op("tensor", f, inc=("t" if fi == len(simops) - 1 else None))
                    sim_t = c.counts["t"]
                    c.wait("scalar", "t", sim_t)
                    c.op("scalar", lambda e, jb=jb, h=h, qt=qt: e.activation(
                        PT[jb][:], p_sim[jb][:], AF.Exp,
                        bias=biasneg[:, h * 16 + qt: h * 16 + qt + 1], scale=1.0), inc="s")
                    exp_s.append(c.counts["s"])
                    sim_free_s[jb] = c.counts["s"]
                    if g >= 1:
                        gp = g - 1
                        jp = gp % 2
                        c.wait("tensor", "s", exp_s[gp])
                        if g == 1:
                            c.wait("tensor", "v", acc_free_v)
                        for j in range(3):
                            kc = 3 * gp + j
                            kce = min(kc, 63)
                            c.op("tensor", lambda e, jp=jp, j=j, kce=kce, h=h,
                                 st=(av_count == 0), sp=False: e.matmul(
                                p_acc[:], V_km[:, kce * 260 + h * 65: kce * 260 + h * 65 + 65],
                                PT[jp][:, j * 512:(j + 1) * 512],
                                start=st, stop=sp), inc=("t" if j == 2 else None))
                            av_count += 1
                        pt_free_t[jp] = c.counts["t"]
                gp = ng - 1
                jp = gp % 2
                c.wait("tensor", "s", exp_s[gp])
                if ng == 1:
                    c.wait("tensor", "v", acc_free_v)
                for j in range(3):
                    kc = 3 * gp + j
                    kce = min(kc, 63)
                    c.op("tensor", lambda e, jp=jp, j=j, kce=kce, h=h,
                         st=(av_count == 0), sp=(j == 2): e.matmul(
                        p_acc[:], V_km[:, kce * 260 + h * 65: kce * 260 + h * 65 + 65],
                        PT[jp][:, j * 512:(j + 1) * 512],
                        start=st, stop=sp), inc=("t" if j == 2 else None))
                    av_count += 1
                pt_free_t[jp] = c.counts["t"]
                av_done_t = c.counts["t"]

                ob = nqt_done % 2
                c.wait("vector", "t", av_done_t)
                c.op("vector", lambda e: e.reciprocal(recip_row[:], p_acc[64:65, :]), inc="v")
                recip_v = c.counts["v"]
                c.wait("tensor", "v", max(recip_v, bcast_free_v))
                c.op("tensor", lambda e: e.matmul(
                    p_bcast[:], onesrow_f[:], recip_row[:], start=True, stop=True), inc="t")
                bc_t = c.counts["t"]
                c.wait("vector", "t", bc_t)
                c.op("vector", lambda e: e.tensor_copy(recip_bc[:], p_bcast[:]), inc="v")
                bcast_free_v = c.counts["v"]
                c.wait("vector", "o0" if ob == 0 else "o1", aout_free_o[ob])
                c.op("vector", lambda e, ob=ob: e.scalar_tensor_tensor(
                    aout_sb[ob][:], p_acc[0:64, :], 1.0, recip_bc[:],
                    ALU.mult, ALU.mult), inc="v")
                acc_free_v = c.counts["v"]
                c.wait("sync", "v", acc_free_v)
                osem = "o0" if ob == 0 else "o1"
                c.op("sync", lambda e, h=h, qt=qt, ob=ob: e.dma_start(
                    out=aout[64 * h: 64 * (h + 1), qt * 512:(qt + 1) * 512],
                    in_=aout_sb[ob][:]), inc=osem, n=16)
                aout_free_o[ob] = c.counts[osem]
                nqt_done += 1

        c.wait("sync", "o0", c.counts["o0"])
        c.wait("sync", "o1", c.counts["o1"])

        with nc.Block() as block:
            c.finalize(block)
    return nc





def make_masks_l1():
    m = np.zeros((128, 5 * 512), dtype=np.float32)
    for pnum in range(4):
        row = np.arange(128)[:, None] + 128 * pnum
        qq = np.arange(512)[None, :]
        m[:, pnum * 512:(pnum + 1) * 512] = np.where(row <= qq, 0.0, -1e30)
    m[:, 4 * 512:] = -1e30
    return m.astype(ml_dtypes.bfloat16)


def l1_in_maps(inp):
    x = np.asarray(inp["L0_attn_x"], dtype=np.float32)
    w_qkv = np.asarray(inp["L0_attn_w_qkv"], dtype=np.float32)
    gamma = np.asarray(inp["L0_attn_ln_gamma"], dtype=np.float32)
    beta = np.asarray(inp["L0_attn_ln_beta"], dtype=np.float32)
    gamma_c = np.ascontiguousarray(gamma.reshape(8, 128).T)
    beta_cb = np.ascontiguousarray(beta.reshape(8, 128).T).astype(ml_dtypes.bfloat16)
    identb = np.eye(128, dtype=ml_dtypes.bfloat16)
    onesb = np.ones((128, 128), dtype=ml_dtypes.bfloat16)
    onesrow = np.ones((1, 512), dtype=ml_dtypes.bfloat16)
    onesrow_f = np.ones((1, 64), dtype=np.float32)
    masks = make_masks_l1()
    in_maps = []
    for core in range(8):
        b, hg = core // 4, core % 4
        xTb = np.ascontiguousarray(x[b].T)
        rows = []
        for part in range(3):
            rows.extend(range(part * 1024 + hg * 256, part * 1024 + (hg + 1) * 256))
        wqkvT = np.ascontiguousarray(w_qkv[rows, :].T)
        in_maps.append({
            "xT": xTb, "wqkvT": wqkvT, "gamma_c": gamma_c, "beta_cb": beta_cb,
            "identb": identb, "onesb": onesb, "onesrow": onesrow,
            "onesrow_f": onesrow_f, "masks": masks,
            "csts": np.broadcast_to(np.array([1.0 / 1024.0, 1e-5, -0.5, -1.0],
                                             dtype=np.float32), (128, 4)).copy(),
            "oneshh": np.concatenate([
                np.concatenate([np.ones((64, 1)), np.zeros((64, 1))], axis=0),
                np.concatenate([np.zeros((64, 1)), np.ones((64, 1))], axis=0)],
                axis=1).astype(ml_dtypes.bfloat16),
        })
    return in_maps




TD = 3584
QS = [640, 1408]
KS = [128, 896]
CS = [768, 1536]
NQB = [(TD - QS[0]) // 128, (TD - QS[1]) // 128]
NBND = [11, 5]
MAXQB = 56
H = 8


def ln_chain(c, st0, st1, st2, s_out, sh_out, p_sumx, p_sumx2):
    c.wait("scalar", "t")
    c.op("scalar", lambda e: e.activation(st0[:], p_sumx[:], AF.Square,
                                          scale=1.0 / 1024.0), inc="s")
    c.wait("vector", "s")
    c.op("vector", lambda e: e.scalar_tensor_tensor(
        st1[:], p_sumx2[:], 1.0 / 1024.0, st0[:], ALU.mult, ALU.subtract), inc="v")
    c.wait("scalar", "v")
    c.op("scalar", lambda e: e.activation(st2[:], st1[:], AF.Ln, bias=1e-5), inc="s")
    c.op("scalar", lambda e: e.activation(s_out[:], st2[:], AF.Exp, scale=-0.5), inc="s")
    c.wait("vector", "s")
    c.op("vector", lambda e: e.scalar_tensor_tensor(
        sh_out[:], p_sumx[:], -1.0 / 1024.0, s_out[:], ALU.mult, ALU.mult), inc="v")


def rep8(ap):
    return ap.rearrange("p (a n) -> p a n", a=1).broadcast_to([128, 8, 512])


class L2:
    def __init__(self):
        nc = bass.Bass("TRN2", target_bir_lowering=False, debug=False)
        self.nc = nc
        di = lambda n, sh, dt: nc.dram_tensor(n, sh, dt, kind="ExternalInput").ap()
        self.attnT = di("attnT", [1024, TD], BF16)
        self.xT = di("xT", [1024, TD], F32)
        self.woutT = di("woutT", [1024, 1024], F32)
        self.boutrow = di("boutrow", [1, 1024], F32)
        self.wqkv_i = di("wqkv", [2, 1024, 1536], F32)
        self.wo_i = di("wo", [2, 512, 1024], F32)
        self.w1_i = di("w1r", [2, 1024, 5632], F32)
        self.b1_i = di("b1r", [2, 1, 5632], F32)
        self.w2_i = di("w2p", [2, 2816, 1024], F32)
        self.b2_i = di("b2row", [2, 1, 1024], F32)
        self.lng_i = di("lng", [128, 16], F32)
        self.lnb_i = di("lnb", [128, 16], F32)
        self.qs8_i = di("qs8", [128, 2], F32)
        self.ks_i = di("ks1", [128, 2], F32)
        self.tabs_i = di("tabs", [128, 4 * TD], BF16)
        self.RT_i = di("RT", [128, 128], BF16)
        self.identb_i = di("identb", [128, 128], BF16)
        self.onesb_i = di("onesb", [128, 128], BF16)
        self.oneshh_i = di("oneshh", [128, 2], BF16)
        self.selT_i = di("selT", [2, 128], BF16)
        self.onesrow_i = di("onesrow", [1, 512], BF16)
        self.onesrowf_i = di("onesrow_f", [1, 64], F32)
        self.mreg_i = di("mreg", [128, 2 * 128], BF16)
        self.mbnd_i = di("mbnd", [sum(NBND) * 5, 128, 128], BF16)
        self.flag_i = di("flag", [128, 1], F32)
        self.csts_i = di("csts", [128, 8], F32)
        self.outT = nc.dram_tensor("outT", [1024, 2048], F32, kind="ExternalOutput").ap()
        ds = lambda n: nc.dram_tensor(n, [1024, TD], F32, kind="Internal").ap()
        self.sc_xtld = ds("sc_xtld")
        self.sc_xin = ds("sc_xin")
        self.sc_xmid = [ds(f"sc_xmid{l}") for l in range(2)]
        self.sc_xout = [ds(f"sc_xout{l}") for l in range(2)]
        self.negm_dram = nc.dram_tensor(
            "negm_scratch", [1, 2 * 4 * MAXQB], BF16, kind="Internal").ap()
        self.sc_o = nc.dram_tensor("sc_o", [512, TD], BF16, kind="Internal").ap()

    def rr(self, ap):
        return ap.rearrange("(c p) n -> p c n", p=128)

    def build(self):
        nc = self.nc
        STAGE = os.environ.get("L2STAGE", "Z")
        order = ["A", "B0", "C0", "B1", "C1", "D"]
        upto = order.index(STAGE) if STAGE in order else len(order) - 1

        es = ExitStack()
        self.es = es
        sb = lambda n, sh, dt: es.enter_context(nc.sbuf_tensor(n, sh, dt))
        with es:
            c = Ctx(nc)
            self.c = c
            for s in ("xc", "xw", "d0", "d1", "e0", "e1", "g", "g0", "g1", "t",
                      "s", "v", "o0", "o1"):
                c.sem(es, s)

            self.identb = sb("identb_s", [128, 128], BF16)
            self.onesb = sb("onesb_s", [128, 128], BF16)
            self.oneshh = sb("oneshh_s", [128, 2], BF16)
            self.selT = sb("selT_s", [2, 128], BF16)
            self.onesrow = sb("onesrow_s", [1, 512], BF16)
            self.onesrow_f = sb("onesrowf_s", [1, 64], F32)
            self.RT = sb("RT_s", [128, 128], BF16)
            self.mreg = sb("mreg_s", [128, 2 * 128], BF16)
            self.lng = sb("lng_s", [128, 16], F32)
            self.lnb = sb("lnb_s", [128, 16], F32)
            self.qs8 = sb("qs8_s", [128, 2], F32)
            self.ks1 = sb("ks1_s", [128, 2], F32)
            self.flag = sb("flag_s", [128, 1], F32)
            cst = sb("cst_s", [128, 8], F32)
            nc.const_aps.register(F32, 1.0 / 1024.0, cst[:, 0:1])
            nc.const_aps.register(F32, 1e-5, cst[:, 1:2])
            nc.const_aps.register(F32, -0.5, cst[:, 2:3])
            nc.const_aps.register(F32, -1.0, cst[:, 3:4])
            nc.const_aps.register(F32, 1e-30, cst[:, 4:5])
            self.st0 = sb("st0", [128, 512], F32)
            self.st1 = sb("st1", [128, 512], F32)
            self.st2 = sb("st2", [128, 512], F32)
            self.s_bc = sb("s_bc", [128, 512], F32)
            self.sh_bc = sb("sh_bc", [128, 512], F32)

            for dst, src in [
                (self.identb, self.identb_i), (self.onesb, self.onesb_i),
                (self.oneshh, self.oneshh_i), (self.selT, self.selT_i),
                (self.onesrow, self.onesrow_i), (self.onesrow_f, self.onesrowf_i),
                (self.RT, self.RT_i), (self.mreg, self.mreg_i),
                (self.lng, self.lng_i), (self.lnb, self.lnb_i),
                (self.qs8, self.qs8_i), (self.ks1, self.ks_i),
                (self.flag, self.flag_i), (cst, self.csts_i),
            ]:
                c.op("sync", lambda e, d=dst, s=src: e.dma_start(out=d[:], in_=s),
                     inc="xc", n=16)
            for eng in ("vector", "tensor", "scalar", "gpsimd"):
                c.wait(eng, "xc")

            self.stageA()
            if upto >= 1:
                self.attn_part(0, self.sc_xin, self.sc_xmid[0])
            if upto >= 2:
                self.ffn_part(0, self.sc_xmid[0], self.sc_xout[0])
            if upto >= 3:
                self.attn_part(1, self.sc_xout[0], self.sc_xmid[1])
            if upto >= 4:
                self.ffn_part(1, self.sc_xmid[1], self.sc_xout[1])
            if upto >= 5:
                self.final_stage()

            c.wait("sync", "o0", c.counts["o0"])
            c.wait("sync", "o1", c.counts["o1"])
            with nc.Block() as block:
                c.finalize(block)
        return nc

    def stageA(self):
        nc, c = self.nc, self.c
        esa = ExitStack()
        sba = lambda n, sh, dt: esa.enter_context(nc.sbuf_tensor(n, sh, dt))
        pba = lambda n, sh, dt: esa.enter_context(nc.psum_tensor(n, sh, dt))
        woT = sba("woT", [128, 8 * 1024], BF16)
        brow = sba("browA", [1, 1024], BF16)
        at_t = [sba(f"atA{i}", [128, 8 * 512], BF16) for i in range(2)]
        xf_t = [sba(f"xfA{i}", [128, 8 * 512], F32) for i in range(2)]
        yf = sba("yfA", [128, 8 * 512], F32)
        ybf = sba("ybfA", [128, 8 * 512], BF16)
        y2b = sba("y2bA", [128, 8 * 512], BF16)
        xtl = sba("xtlA", [128, 8 * 512], F32)
        xin_t = sba("xinA", [128, 8 * 512], F32)
        self.s2_bc = esa.enter_context(nc.sbuf_tensor("s2_bcA", [128, 512], F32))
        self.sh2_bc = esa.enter_context(nc.sbuf_tensor("sh2_bcA", [128, 512], F32))
        p_y = pba("p_yA", [128, 3 * 512], F32)
        p_s1 = pba("p_s1A", [128, 512], F32)
        p_s2 = pba("p_s2A", [128, 512], F32)

        c.op("gpsimd", lambda e: e.dma_start(out=brow[:], in_=self.boutrow),
             inc="g", n=16)
        c.op("gpsimd", lambda e: e.dma_start(
            out=woT[:].rearrange("p (c n) -> p c n", n=1024),
            in_=self.woutT.rearrange("(c p) n -> p c n", p=128)), inc="g", n=16)
        c.wait("tensor", "g")

        atr = self.rr(self.attnT)
        xtr = self.rr(self.xT)
        xtld_r = self.rr(self.sc_xtld)
        xin_r = self.rr(self.sc_xin)
        st0, st1, st2 = self.st0, self.st1, self.st2

        tile_free = [0, 0]
        tile_free_t = [0, 0]
        bank_free = {}
        statA_s = 0
        statA_v = 0
        xtl_dma = [0, 0]
        xin_dma = [0, 0]
        for t in range(7):
            bf = t % 2
            dsem = "d0" if bf == 0 else "d1"
            c.wait("sync", "v", tile_free[bf])
            c.wait("sync", "t", tile_free_t[bf])
            c.op("sync", lambda e, t=t, bf=bf: e.dma_start(
                out=at_t[bf][:].rearrange("p (c n) -> p c n", n=512),
                in_=atr[:, :, t * 512:(t + 1) * 512]), inc=dsem, n=16)
            c.op("sync", lambda e, t=t, bf=bf: e.dma_start(
                out=xf_t[bf][:].rearrange("p (c n) -> p c n", n=512),
                in_=xtr[:, :, t * 512:(t + 1) * 512]), inc=dsem, n=16)
            c.wait("tensor", dsem)
            for dc in range(8):
                bank = dc % 3
                if bank_free.get(bank):
                    c.wait("tensor", "v", bank_free[bank])
                for ec in range(8):
                    c.op("tensor", lambda e, dc=dc, ec=ec, bf=bf, bank=bank: e.matmul(
                        p_y[:, bank * 512:(bank + 1) * 512],
                        woT[:, ec * 1024 + dc * 128: ec * 1024 + (dc + 1) * 128],
                        at_t[bf][:, ec * 512:(ec + 1) * 512],
                        start=(ec == 0), stop=False))
                c.op("tensor", lambda e, dc=dc, bank=bank: e.matmul(
                    p_y[:, bank * 512:(bank + 1) * 512],
                    brow[0:1, dc * 128:(dc + 1) * 128], self.onesrow[:],
                    start=False, stop=True), inc="t")
                c.wait("vector", "t")
                c.op("vector", lambda e, dc=dc, bank=bank: e.tensor_copy(
                    yf[:, dc * 512:(dc + 1) * 512],
                    p_y[:, bank * 512:(bank + 1) * 512]), inc="v")
                bank_free[bank] = c.counts["v"]
            tile_free_t[bf] = c.counts["t"]
            c.op("vector", lambda e: e.tensor_copy(ybf[:], yf[:]), inc="v")
            c.op("vector", lambda e: e.tensor_tensor(y2b[:], ybf[:], ybf[:],
                                                     ALU.mult), inc="v")
            yb_v = c.counts["v"]
            c.wait("tensor", "v", yb_v)
            c.wait("tensor", "s", statA_s)
            c.wait("tensor", "v", statA_v)
            for dc in range(8):
                c.op("tensor", lambda e, dc=dc: e.matmul(
                    p_s1[:], self.onesb[:], ybf[:, dc * 512:(dc + 1) * 512],
                    start=(dc == 0), stop=(dc == 7)))
            for dc in range(8):
                c.op("tensor", lambda e, dc=dc: e.matmul(
                    p_s2[:], self.onesb[:], y2b[:, dc * 512:(dc + 1) * 512],
                    start=(dc == 0), stop=(dc == 7)),
                    inc=("t" if dc == 7 else None))
            ln_chain(c, st0, st1, st2, self.s_bc, self.sh_bc, p_s1, p_s2)
            statA_s = c.counts["s"]
            c.wait("vector", "e0" if bf == 1 else "e1", xtl_dma[1 - bf])
            c.op("vector", lambda e: e.scalar_tensor_tensor(
                xtl[:].rearrange("p (c n) -> p c n", n=512),
                yf[:].rearrange("p (c n) -> p c n", n=512), 1.0, rep8(self.s_bc[:]),
                ALU.mult, ALU.mult), inc="v")
            c.op("vector", lambda e: e.tensor_tensor(
                xtl[:].rearrange("p (c n) -> p c n", n=512),
                xtl[:].rearrange("p (c n) -> p c n", n=512), rep8(self.sh_bc[:]),
                ALU.add), inc="v")
            c.op("vector", lambda e, bf=bf: e.tensor_tensor(
                xtl[:], xtl[:], xf_t[bf][:], ALU.add), inc="v")
            tile_free[bf] = c.counts["v"]
            esem = "e0" if bf == 0 else "e1"
            c.wait("sync", "v")
            c.op("sync", lambda e, t=t: e.dma_start(
                out=xtld_r[:, :, t * 512:(t + 1) * 512],
                in_=xtl[:].rearrange("p (c n) -> p c n", n=512)), inc=esem, n=16)
            c.op("vector", lambda e: e.tensor_copy(ybf[:], xtl[:]), inc="v")
            c.op("vector", lambda e: e.tensor_tensor(y2b[:], ybf[:], ybf[:],
                                                     ALU.mult), inc="v")
            x2_v = c.counts["v"]
            c.wait("tensor", "v", x2_v)
            c.wait("tensor", "s", statA_s)
            for dc in range(8):
                c.op("tensor", lambda e, dc=dc: e.matmul(
                    p_s1[:], self.onesb[:], ybf[:, dc * 512:(dc + 1) * 512],
                    start=(dc == 0), stop=(dc == 7)))
            for dc in range(8):
                c.op("tensor", lambda e, dc=dc: e.matmul(
                    p_s2[:], self.onesb[:], y2b[:, dc * 512:(dc + 1) * 512],
                    start=(dc == 0), stop=(dc == 7)),
                    inc=("t" if dc == 7 else None))
            ln_chain(c, st0, st1, st2, self.s2_bc, self.sh2_bc, p_s1, p_s2)
            statA_s = c.counts["s"]
            statA_v = c.counts["v"]
            c.wait("vector", "e0" if bf == 1 else "e1", xin_dma[1 - bf])
            c.op("vector", lambda e: e.scalar_tensor_tensor(
                xin_t[:].rearrange("p (c n) -> p c n", n=512),
                xtl[:].rearrange("p (c n) -> p c n", n=512), 1.0, rep8(self.s2_bc[:]),
                ALU.mult, ALU.mult), inc="v")
            c.op("vector", lambda e: e.tensor_tensor(
                xin_t[:].rearrange("p (c n) -> p c n", n=512),
                xin_t[:].rearrange("p (c n) -> p c n", n=512), rep8(self.sh2_bc[:]),
                ALU.add), inc="v")
            if t == 2:
                c.op("vector", lambda e: e.tensor_scalar(
                    xin_t[:].rearrange("p (c n) -> p c n", n=512)[:, :, 511:512],
                    xin_t[:].rearrange("p (c n) -> p c n", n=512)[:, :, 511:512],
                    self.flag[:, 0:1], None, ALU.mult), inc="v")
            c.wait("sync", "v")
            c.op("sync", lambda e, t=t: e.dma_start(
                out=xin_r[:, :, t * 512:(t + 1) * 512],
                in_=xin_t[:].rearrange("p (c n) -> p c n", n=512)), inc=esem, n=16)
            xtl_dma[bf] = c.counts[esem]
            xin_dma[bf] = c.counts[esem]
        c.wait("sync", "e0", c.counts["e0"])
        c.wait("sync", "e1", c.counts["e1"])
        c.barrier()
        esa.close()

    def attn_part(self, l, z_dram, xmid_dram):
        nc, c = self.nc, self.c
        qs_off, ks_off = QS[l], KS[l]
        Tq, Tk = TD - qs_off, TD - ks_off
        nqb = NQB[l]
        nkt = (Tk + 511) // 512
        nbnd = NBND[l]
        mb_base = 5 * sum(NBND[:l])

        e2 = ExitStack()
        sb2 = lambda n, sh, dt: e2.enter_context(nc.sbuf_tensor(f"{n}_l{l}", sh, dt))
        pb2 = lambda n, sh, dt: e2.enter_context(nc.psum_tensor(f"{n}_l{l}", sh, dt))

        wq = sb2("wq", [128, 8 * 1536], BF16)
        zb = [sb2(f"zb{i}", [128, 8 * 513], BF16) for i in range(2)]
        tbt = [sb2(f"tbt{i}", [128, 4 * 512], BF16) for i in range(2)]
        Qf = sb2("Qf", [128, 4 * Tq], BF16)
        Kf = sb2("Kf", [128, 4 * Tk], BF16)
        Vkm = sb2("Vkm", [128, (Tk // 128) * 520], BF16)
        vstage = sb2("vstage", [128, 4 * 512], BF16)
        qraw = sb2("qraw", [128, 512], BF16)
        sqb_ = sb2("sqb", [128, 512], BF16)
        rnb = sb2("rnb", [2, 512], BF16)
        qn = sb2("qn", [128, 512], BF16)
        t1f = sb2("t1f", [128, 512], F32)
        uf = sb2("uf", [128, 512], F32)
        negm2 = sb2("negm2", [2, 4 * MAXQB], F32)
        negm2b = sb2("negm2b", [2, 4 * MAXQB], BF16)
        negmrow = sb2("negmrow", [1, 8 * MAXQB], BF16)
        biasneg = sb2("biasneg", [128, 8 * MAXQB], F32)
        dqb = sb2("dqb", [128, 512], BF16)
        mstr = [sb2(f"mstr{i}", [128, 5 * 128], BF16) for i in range(2)]
        PT = [sb2(f"PTl{i}", [128, 640], BF16) for i in range(2)]
        accst = sb2("accst", [65, 8 * 128], F32)
        lrec = sb2("lrec", [1, 8 * 128], F32)
        lntmp = sb2("lntmp", [1, 8 * 128], F32)
        rb_sb = sb2("rb_sb", [64, 8 * 128], F32)
        ost = [sb2(f"ost{i}", [64, 8 * 128], BF16) for i in range(2)]

        e2p = ExitStack()
        pb2p = lambda n, sh, dt: e2p.enter_context(nc.psum_tensor(f"{n}_l{l}", sh, dt))
        p_qkv = pb2p("p_qkv", [128, 3 * 512], F32)
        p_nrm = pb2p("p_nrm", [2, 512], F32)
        p_nb = pb2p("p_nb", [128, 512], F32)
        p_rot = pb2p("p_rot", [128, 512], F32)
        p_vt2 = pb2p("p_vt2", [128, 2 * 1024], BF16)

        c.op("gpsimd", lambda e, l=l: e.dma_start(
            out=wq[:].rearrange("p (c n) -> p c n", n=1536),
            in_=self.wqkv_i[l].rearrange("(c p) n -> p c n", p=128)), inc="g", n=16)
        c.wait("tensor", "g")

        zr = self.rr(z_dram)
        tabr = self.tabs_i.rearrange("p (f n) -> p f n", f=4)
        vko = Vkm[:].rearrange("p (k h d) -> p k h d", h=8, d=65)
        c.op("vector", lambda e: e.memset(vko[:, :, :, 64:65], 1.0), inc="v")
        c.op("vector", lambda e: e.memset(negm2[:], 0.0), inc="v")

        bank_free = {}
        zb_free = [0, 0]
        vt_free = [0, 0]
        for kt in range(nkt):
            t0 = ks_off + kt * 512
            tlen = min(512, TD - t0)
            bf = kt % 2
            gsem = "g0" if bf == 0 else "g1"
            c.wait("gpsimd", "v", zb_free[bf])
            c.op("gpsimd", lambda e, t0=t0, tlen=tlen, bf=bf: e.dma_start(
                out=zb[bf][:].rearrange("p (c n) -> p c n", n=513)[:, :, 0:tlen + 1],
                in_=zr[:, :, t0 - 1: t0 + tlen]), inc=gsem, n=16)
            c.op("gpsimd", lambda e, t0=t0, tlen=tlen, bf=bf: e.dma_start(
                out=tbt[bf][:].rearrange("p (f n) -> p f n", f=4)[:, :, 0:tlen],
                in_=tabr[:, :, t0: t0 + tlen]), inc=gsem, n=16)
            c.wait("tensor", gsem)
            for ec in range(12):
                bank = ec % 3
                if bank_free.get(bank):
                    c.wait("tensor", "v", bank_free[bank])
                for dc in range(8):
                    off = 1 if dc < 4 else 0
                    c.op("tensor", lambda e, ec=ec, dc=dc, bf=bf, bank=bank, off=off,
                         tlen=tlen: e.matmul(
                        p_qkv[:, bank * 512: bank * 512 + tlen],
                        wq[:, dc * 1536 + ec * 128: dc * 1536 + (ec + 1) * 128],
                        zb[bf][:, dc * 513 + off: dc * 513 + off + tlen],
                        start=(dc == 0), stop=(dc == 7)),
                        inc=("t" if dc == 7 else None))
                mm_t = c.counts["t"]
                c.wait("vector", "t", mm_t)
                ch = ec % 4
                pslice = p_qkv[:, bank * 512: bank * 512 + tlen]
                if ec >= 8:
                    c.op("vector", lambda e, ps=pslice, ch=ch, tlen=tlen: e.tensor_copy(
                        vstage[:, ch * 512: ch * 512 + tlen], ps), inc="v")
                    bank_free[bank] = c.counts["v"]
                    continue
                isq = ec < 4
                doff = (t0 - qs_off) if isq else (t0 - ks_off)
                c.op("vector", lambda e, ps=pslice, tlen=tlen: e.tensor_copy(
                    qraw[:, 0:tlen], ps), inc="v")
                bank_free[bank] = c.counts["v"]
                if isq and doff < 0:
                    continue
                c.op("vector", lambda e, tlen=tlen: e.tensor_tensor(
                    sqb_[:, 0:tlen], qraw[:, 0:tlen], qraw[:, 0:tlen], ALU.mult),
                    inc="v")
                sq_v = c.counts["v"]
                c.wait("tensor", "v", sq_v)
                c.op("tensor", lambda e, tlen=tlen: e.matmul(
                    p_nrm[:, 0:tlen], self.oneshh[:, 0:2], sqb_[:, 0:tlen],
                    start=True, stop=True), inc="t")
                c.wait("scalar", "t")
                c.op("scalar", lambda e, tlen=tlen: e.activation(
                    self.st0[0:2, 0:tlen], p_nrm[:, 0:tlen], AF.Ln, bias=1e-30),
                    inc="s")
                c.op("scalar", lambda e, tlen=tlen: e.activation(
                    rnb[:, 0:tlen], self.st0[0:2, 0:tlen], AF.Exp, scale=-0.5),
                    inc="s")
                c.wait("tensor", "s")
                c.op("tensor", lambda e, tlen=tlen: e.matmul(
                    p_nb[:, 0:tlen], self.selT[:, :], rnb[:, 0:tlen],
                    start=True, stop=True), inc="t")
                nb_t = c.counts["t"]
                c.wait("vector", "t", nb_t)
                sc_ap = self.qs8 if isq else self.ks1
                c.op("vector", lambda e, tlen=tlen, sc_ap=sc_ap, l=l: e.scalar_tensor_tensor(
                    qn[:, 0:tlen], qraw[:, 0:tlen], sc_ap[:, l:l + 1], p_nb[:, 0:tlen],
                    ALU.mult, ALU.mult), inc="v")
                qn_v = c.counts["v"]
                c.wait("tensor", "v", qn_v)
                c.op("tensor", lambda e, tlen=tlen: e.matmul(
                    p_rot[:, 0:tlen], self.RT[:, :], qn[:, 0:tlen],
                    start=True, stop=True), inc="t")
                rot_t = c.counts["t"]
                ci = 0 if isq else 2
                si = 1 if isq else 3
                c.wait("vector", "g0" if bf == 0 else "g1")
                c.op("vector", lambda e, bf=bf, ci=ci, tlen=tlen: e.tensor_tensor(
                    t1f[:, 0:tlen], qn[:, 0:tlen],
                    tbt[bf][:, ci * 512: ci * 512 + tlen], ALU.mult), inc="v")
                c.wait("vector", "t", rot_t)
                c.op("vector", lambda e, bf=bf, si=si, tlen=tlen: e.scalar_tensor_tensor(
                    uf[:, 0:tlen], p_rot[:, 0:tlen], 1.0,
                    tbt[bf][:, si * 512: si * 512 + tlen], ALU.mult, ALU.mult),
                    inc="v")
                dst, dT = (Qf, Tq) if isq else (Kf, Tk)
                c.op("vector", lambda e, ch=ch, dst=dst, dT=dT, doff=doff,
                     tlen=tlen: e.tensor_tensor(
                    dst[:, ch * dT + doff: ch * dT + doff + tlen],
                    t1f[:, 0:tlen], uf[:, 0:tlen], ALU.add), inc="v")
            zb_free[bf] = c.counts["v"]
            vs_v = c.counts["v"]
            c.wait("tensor", "v", vs_v)
            nsc = (tlen + 127) // 128
            for ch in range(4):
                for scn in range(nsc):
                    tbuf = (ch * nsc + scn) % 2
                    c.wait("tensor", "v", vt_free[tbuf])
                    c.op("tensor", lambda e, ch=ch, scn=scn, tbuf=tbuf: e.transpose(
                        p_vt2[:, tbuf * 1024: tbuf * 1024 + 128],
                        vstage[:, ch * 512 + scn * 128: ch * 512 + (scn + 1) * 128],
                        self.identb[:]), inc="t")
                    tr_t = c.counts["t"]
                    cc = kt * 4 + scn
                    c.wait("vector", "t", tr_t)
                    c.op("vector", lambda e, cc=cc, ch=ch, tbuf=tbuf: e.tensor_copy(
                        Vkm[:, cc * 520 + ch * 130: cc * 520 + (ch + 1) * 130]
                        .rearrange("p (h d) -> p h d", d=65)[:, :, 0:64],
                        p_vt2[:, tbuf * 1024: tbuf * 1024 + 128]
                        .rearrange("p (h d) -> p h d", d=64)), inc="v")
                    vt_free[tbuf] = c.counts["v"]

        ndt = (Tq + 511) // 512
        for ch in range(4):
            for dt_ in range(ndt):
                doff = dt_ * 512
                dlen = min(512, Tq - doff)
                nb = dlen // 128
                c.op("vector", lambda e, ch=ch, doff=doff, dlen=dlen: e.tensor_tensor(
                    dqb[:, 0:dlen], Qf[:, ch * Tq + doff: ch * Tq + doff + dlen],
                    Kf[:, ch * Tk + 512 + doff: ch * Tk + 512 + doff + dlen],
                    ALU.mult), inc="v")
                dq_v = c.counts["v"]
                c.wait("tensor", "v", dq_v)
                c.op("tensor", lambda e, dlen=dlen: e.matmul(
                    p_nrm[:, 0:dlen], self.oneshh[:, 0:2], dqb[:, 0:dlen],
                    start=True, stop=True), inc="t")
                mm_t = c.counts["t"]
                c.wait("vector", "t", mm_t)
                c.op("vector", lambda e, ch=ch, dt_=dt_, nb=nb: e.reduce_max(
                    negm2[0:2, ch * MAXQB + dt_ * 4: ch * MAXQB + dt_ * 4 + nb],
                    p_nrm[:, 0:nb * 128].rearrange("p (b n) -> p b n", n=128),
                    axis=AX.X), inc="v")
                c.wait("tensor", "v")
        c.op("vector", lambda e: e.tensor_scalar(
            negm2b[:], negm2[:], -1.0, None, ALU.mult), inc="v")
        c.wait("gpsimd", "v")
        c.op("gpsimd", lambda e: e.dma_start(
            out=self.negm_dram.rearrange("o (r pq) -> (o r) pq", r=2),
            in_=negm2b[:]), inc="g", n=16)
        c.wait("gpsimd", "g")
        c.op("gpsimd", lambda e: e.dma_start(
            out=negmrow[:].rearrange("o (p r q) -> o p r q", p=4, r=2),
            in_=self.negm_dram.rearrange("o (r p q) -> o p r q", r=2, p=4)),
            inc="g", n=16)
        c.wait("tensor", "g")
        c.op("tensor", lambda e: e.matmul(
            p_nb[:, 0:448], self.onesrow[0:1, 0:128], negmrow[0:1, 0:448],
            start=True, stop=True), inc="t")
        bt = c.counts["t"]
        c.wait("vector", "t", bt)
        c.op("vector", lambda e: e.tensor_copy(biasneg[:, 0:448], p_nb[:, 0:448]),
             inc="v")
        c.wait("tensor", "v")

        c.barrier()
        e2p.close()
        e3 = ExitStack()
        p_sml = [e3.enter_context(nc.psum_tensor(f"p_sml{i}_l{l}", [128, 640], F32))
                 for i in range(2)]
        p_acc = e3.enter_context(nc.psum_tensor(f"p_accl_l{l}", [65, 128], F32))
        p_rb = e3.enter_context(nc.psum_tensor(f"p_rbl_l{l}", [64, 1024], F32))

        sim_free = [0, 0]
        acc_free = 0
        rb_free = 0
        mb_free = [0, 0]
        ost_free = [0, 0]
        for qb in range(nqb):
            qoff = qb * 128
            mb = qb % 2
            if qb < nbnd:
                dsem = "d0" if mb == 0 else "d1"
                c.wait("sync", "t", mb_free[mb])
                c.op("sync", lambda e, l=l, qb=qb, mb=mb: e.dma_start(
                    out=mstr[mb][:].rearrange("p (k n) -> p k n", k=5),
                    in_=self.mbnd_i[mb_base + qb * 5: mb_base + (qb + 1) * 5]
                    .rearrange("k p n -> p k n")), inc=dsem, n=16)
                c.wait("tensor", dsem)
            for h in range(H):
                ch, rb = h // 2, 64 * (h % 2)
                jb = (qb * H + h) % 2
                c.wait("tensor", "s", sim_free[jb])
                simops = []
                for ko in range(5):
                    koff = qoff + ko * 128
                    masked = qb < nbnd or ko == 0 or ko == 4
                    simops.append(lambda e, jb=jb, ko=ko, ch=ch, rb=rb, koff=koff,
                                  qoff=qoff, masked=masked: e.matmul(
                        p_sml[jb][:, ko * 128:(ko + 1) * 128],
                        Kf[rb:rb + 64, ch * Tk + koff: ch * Tk + koff + 128],
                        Qf[rb:rb + 64, ch * Tq + qoff: ch * Tq + qoff + 128],
                        start=True, stop=(not masked)))
                    if qb < nbnd:
                        simops.append(lambda e, jb=jb, ko=ko, mb=mb: e.matmul(
                            p_sml[jb][:, ko * 128:(ko + 1) * 128],
                            self.identb[:], mstr[mb][:, ko * 128:(ko + 1) * 128],
                            start=False, stop=True))
                    elif ko == 0 or ko == 4:
                        mi = 0 if ko == 0 else 1
                        simops.append(lambda e, jb=jb, ko=ko, mi=mi: e.matmul(
                            p_sml[jb][:, ko * 128:(ko + 1) * 128],
                            self.identb[:], self.mreg[:, mi * 128:(mi + 1) * 128],
                            start=False, stop=True))
                for fi, f in enumerate(simops):
                    c.op("tensor", f, inc=("t" if fi == len(simops) - 1 else None))
                sim_t = c.counts["t"]
                if qb < nbnd:
                    mb_free[mb] = sim_t
                c.wait("scalar", "t", sim_t)
                c.op("scalar", lambda e, jb=jb, h=h, qb=qb: e.activation(
                    PT[jb][:], p_sml[jb][:], AF.Exp,
                    bias=biasneg[:, h * MAXQB + qb: h * MAXQB + qb + 1],
                    scale=1.0), inc="s")
                exp_s = c.counts["s"]
                sim_free[jb] = exp_s
                c.wait("tensor", "s", exp_s)
                c.wait("tensor", "v", acc_free)
                for ko in range(5):
                    kc = qb + ko
                    c.op("tensor", lambda e, jb=jb, ko=ko, h=h, kc=kc,
                         st=(ko == 0), sp=(ko == 4): e.matmul(
                        p_acc[:], Vkm[:, kc * 520 + h * 65: kc * 520 + h * 65 + 65],
                        PT[jb][:, ko * 128:(ko + 1) * 128],
                        start=st, stop=sp), inc=("t" if ko == 4 else None))
                av_t = c.counts["t"]
                c.wait("vector", "t", av_t)
                c.op("vector", lambda e, h=h: e.tensor_copy(
                    accst[:, h * 128:(h + 1) * 128], p_acc[:]), inc="v")
                acc_free = c.counts["v"]
            c.op("vector", lambda e: e.tensor_copy(
                lrec[:], accst[64:65, :]), inc="v")
            lr_v = c.counts["v"]
            c.wait("scalar", "v", lr_v)
            c.op("scalar", lambda e: e.activation(lntmp[:], lrec[:], AF.Ln), inc="s")
            c.op("scalar", lambda e: e.activation(lrec[:], lntmp[:], AF.Exp,
                                                  scale=-1.0), inc="s")
            rl_s = c.counts["s"]
            c.wait("tensor", "s", rl_s)
            c.wait("tensor", "v", rb_free)
            c.op("tensor", lambda e: e.matmul(
                p_rb[:, 0:512], self.onesrow_f[:], lrec[0:1, 0:512],
                start=True, stop=True))
            c.op("tensor", lambda e: e.matmul(
                p_rb[:, 512:1024], self.onesrow_f[:], lrec[0:1, 512:1024],
                start=True, stop=True), inc="t")
            bc_t = c.counts["t"]
            c.wait("vector", "t", bc_t)
            c.op("vector", lambda e: e.tensor_copy(rb_sb[:], p_rb[:]), inc="v")
            rb_free = c.counts["v"]
            ob2 = qb % 2
            c.wait("vector", "e0" if ob2 == 0 else "e1", ost_free[ob2])
            c.op("vector", lambda e, ob2=ob2: e.scalar_tensor_tensor(
                ost[ob2][0:64, :].rearrange("p (h n) -> p h n", h=8),
                accst[0:64, :].rearrange("p (h n) -> p h n", h=8),
                1.0,
                rb_sb[:].rearrange("p (h n) -> p h n", h=8),
                ALU.mult, ALU.mult), inc="v")
            acc_free = c.counts["v"]
            c.wait("sync", "v")
            esm = "e0" if ob2 == 0 else "e1"
            c.op("sync", lambda e, qoff=qoff, ob2=ob2: e.dma_start(
                out=self.sc_o.rearrange("(h d) n -> d h n", d=64)[
                    :, :, qs_off + qoff: qs_off + qoff + 128],
                in_=ost[ob2][0:64, :].rearrange("p (h n) -> p h n", h=8)),
                inc=esm, n=16)
            ost_free[ob2] = c.counts[esm]

        c.wait("sync", "e0", c.counts["e0"])
        c.wait("sync", "e1", c.counts["e1"])
        c.barrier()
        e3.close()

        e2.close()

        e4 = ExitStack()
        sb4 = lambda n, sh, dt: e4.enter_context(nc.sbuf_tensor(f"{n}_w{l}", sh, dt))
        zq = sb4("zq", [128, 8 * 512], F32)
        otile = sb4("otile", [128, 4 * 512], BF16)
        xmo = sb4("xmo", [128, 8 * 512], F32)
        woB = sb4("woB", [128, 4 * 1024], BF16)
        p_wo = e4.enter_context(nc.psum_tensor(f"p_wo_l{l}", [128, 3 * 512], F32))
        c.op("gpsimd", lambda e, l=l: e.dma_start(
            out=woB[:].rearrange("p (c n) -> p c n", n=1024),
            in_=self.wo_i[l].rearrange("(c p) n -> p c n", p=128)), inc="g", n=16)
        c.wait("tensor", "g")
        wo_bank_free = {}
        zq_free = 0
        ot_free = 0
        xmo_dma = [0, 0]
        nwt = (Tq + 511) // 512
        flag_tile = (1535 - qs_off) // 512
        flag_col = (1535 - qs_off) % 512
        o_pk = self.sc_o.rearrange("(c p) n -> p c n", p=128)
        for wt in range(nwt):
            woff = wt * 512
            wlen = min(512, Tq - woff)
            c.wait("sync", "v", zq_free)
            c.op("sync", lambda e, woff=woff, wlen=wlen: e.dma_start(
                out=zq[:].rearrange("p (c n) -> p c n", n=512)[:, :, 0:wlen],
                in_=zr[:, :, qs_off + woff: qs_off + woff + wlen]), inc="d0", n=16)
            c.wait("sync", "t", ot_free)
            c.op("sync", lambda e, woff=woff, wlen=wlen: e.dma_start(
                out=otile[:].rearrange("p (c n) -> p c n", n=512)[:, :, 0:wlen],
                in_=o_pk[:, :, qs_off + woff: qs_off + woff + wlen]), inc="d1", n=16)
            c.wait("vector", "d0")
            c.wait("tensor", "d0")
            c.wait("tensor", "d1")
            c.wait("vector", "e0" if wt % 2 == 1 else "e1", xmo_dma[1 - wt % 2])
            for dc in range(8):
                bank = dc % 3
                if wo_bank_free.get(bank):
                    c.wait("tensor", "v", wo_bank_free[bank])
                for ch in range(4):
                    c.op("tensor", lambda e, dc=dc, ch=ch, wlen=wlen,
                         bank=bank: e.matmul(
                        p_wo[:, bank * 512: bank * 512 + wlen],
                        woB[:, ch * 1024 + dc * 128: ch * 1024 + (dc + 1) * 128],
                        otile[:, ch * 512: ch * 512 + wlen],
                        start=(ch == 0), stop=(ch == 3)),
                        inc=("t" if ch == 3 else None))
                mm_t = c.counts["t"]
                c.wait("vector", "t", mm_t)
                c.op("vector", lambda e, dc=dc, bank=bank, wlen=wlen: e.scalar_tensor_tensor(
                    xmo[:, dc * 512: dc * 512 + wlen],
                    p_wo[:, bank * 512: bank * 512 + wlen], 1.0,
                    zq[:, dc * 512: dc * 512 + wlen], ALU.mult, ALU.add), inc="v")
                wo_bank_free[bank] = c.counts["v"]
            zq_free = c.counts["v"]
            ot_free = c.counts["t"]
            if wt == flag_tile:
                c.op("vector", lambda e, flag_col=flag_col: e.tensor_scalar(
                    xmo[:].rearrange("p (c n) -> p c n", n=512)[:, :, flag_col:flag_col + 1],
                    xmo[:].rearrange("p (c n) -> p c n", n=512)[:, :, flag_col:flag_col + 1],
                    self.flag[:, 0:1], None, ALU.mult), inc="v")
            c.wait("sync", "v")
            esem = "e0" if wt % 2 == 0 else "e1"
            c.op("sync", lambda e, woff=woff, wlen=wlen: e.dma_start(
                out=self.rr(xmid_dram)[:, :, qs_off + woff: qs_off + woff + wlen],
                in_=xmo[:].rearrange("p (c n) -> p c n", n=512)[:, :, 0:wlen]),
                inc=esem, n=16)
            xmo_dma[wt % 2] = c.counts[esem]
        c.wait("sync", "e0", c.counts["e0"])
        c.wait("sync", "e1", c.counts["e1"])
        c.barrier()
        e4.close()

    def ffn_part(self, l, xmid_dram, xout_dram):
        nc, c = self.nc, self.c
        cs_off = CS[l]
        Tc = TD - cs_off
        nt = (Tc + 511) // 512

        e2 = ExitStack()
        sb2 = lambda n, sh, dt: e2.enter_context(nc.sbuf_tensor(f"{n}_f{l}", sh, dt))
        pb2 = lambda n, sh, dt: e2.enter_context(nc.psum_tensor(f"{n}_f{l}", sh, dt))
        w1 = sb2("w1", [128, 8 * 5632], BF16)
        w2 = sb2("w2", [128, 22 * 1024], BF16)
        b1row = sb2("b1row", [1, 5632], BF16)
        b2row = sb2("b2row", [1, 1024], BF16)
        xmb = sb2("xmb", [128, 8 * 513], BF16)
        xmf = sb2("xmf", [128, 512], F32)
        h2b = sb2("h2b", [128, 512], BF16)
        xnb = sb2("xnb", [128, 8 * 512], BF16)
        gbuf = sb2("gbuf", [128, 512], BF16)
        h4 = sb2("h4", [128, 22 * 512], BF16)
        xob = sb2("xob", [128, 512], F32)
        shg = sb2("shg", [128, 512], F32)
        p_a = pb2("p_a", [128, 2 * 512], F32)
        p_g = pb2("p_g", [128, 2 * 512], F32)
        p_s1 = pb2("p_s1f", [128, 512], F32)
        p_s2 = pb2("p_s2f", [128, 512], F32)
        p_o = pb2("p_of", [128, 2 * 512], F32)

        c.op("gpsimd", lambda e, l=l: e.dma_start(out=b1row[:], in_=self.b1_i[l]),
             inc="g", n=16)
        c.op("gpsimd", lambda e, l=l: e.dma_start(out=b2row[:], in_=self.b2_i[l]),
             inc="g", n=16)
        c.op("gpsimd", lambda e, l=l: e.dma_start(
            out=w1[:].rearrange("p (c n) -> p c n", n=5632),
            in_=self.w1_i[l].rearrange("(c p) n -> p c n", p=128)), inc="g", n=16)
        c.op("gpsimd", lambda e, l=l: e.dma_start(
            out=w2[:].rearrange("p (c n) -> p c n", n=1024),
            in_=self.w2_i[l].rearrange("(c p) n -> p c n", p=128)), inc="g", n=16)
        c.wait("tensor", "g")

        xmr = self.rr(xmid_dram)
        xor_ = self.rr(xout_dram)
        st0, st1, st2 = self.st0, self.st1, self.st2
        g_ = self.lng
        b_ = self.lnb

        tile_free = 0
        tile_free_t = 0
        stat_s = 0
        stat_v = 0
        ab_free = [0, 0]
        gb_free = [0, 0]
        o_free = {}
        xmf_free = [0, 0]
        xob_dma = [0, 0]
        last_h4_v = 0
        for t in range(nt):
            toff = cs_off + t * 512
            tlen = min(512, TD - toff)
            c.wait("gpsimd", "v", tile_free)
            c.wait("gpsimd", "t", tile_free_t)
            c.op("gpsimd", lambda e, toff=toff, tlen=tlen: e.dma_start(
                out=xmb[:].rearrange("p (c n) -> p c n", n=513)[:, :, 0:tlen + 1],
                in_=xmr[:, :, toff - 1: toff + tlen]), inc="g0", n=16)
            c.wait("vector", "g0")
            c.wait("tensor", "g0")
            c.wait("tensor", "s", stat_s)
            c.wait("tensor", "v", stat_v)
            for dc in range(8):
                off = 1 if dc < 4 else 0
                c.op("tensor", lambda e, dc=dc, off=off, tlen=tlen: e.matmul(
                    p_s1[:, 0:tlen], self.onesb[:],
                    xmb[:, dc * 513 + off: dc * 513 + off + tlen],
                    start=(dc == 0), stop=(dc == 7)), inc=("t" if dc == 7 else None))
            for dc in range(8):
                off = 1 if dc < 4 else 0
                if dc > 0:
                    c.wait("vector", "t")
                c.op("vector", lambda e, dc=dc, off=off, tlen=tlen: e.tensor_tensor(
                    h2b[:, 0:tlen],
                    xmb[:, dc * 513 + off: dc * 513 + off + tlen],
                    xmb[:, dc * 513 + off: dc * 513 + off + tlen], ALU.mult), inc="v")
                h2_v = c.counts["v"]
                c.wait("tensor", "v", h2_v)
                c.op("tensor", lambda e, dc=dc, tlen=tlen: e.matmul(
                    p_s2[:, 0:tlen], self.onesb[:], h2b[:, 0:tlen],
                    start=(dc == 0), stop=(dc == 7)), inc="t")
            ln_chain(c, st0, st1, st2, self.s_bc, self.sh_bc, p_s1, p_s2)
            stat_s = c.counts["s"]
            for dc in range(8):
                off = 1 if dc < 4 else 0
                c.op("vector", lambda e, dc=dc, l=l: e.tensor_scalar(
                    shg[:], self.sh_bc[:],
                    g_[:, l * 8 + dc: l * 8 + dc + 1],
                    b_[:, l * 8 + dc: l * 8 + dc + 1],
                    ALU.mult, ALU.add), inc="v")
                c.op("vector", lambda e, dc=dc, off=off, l=l: e.scalar_tensor_tensor(
                    xnb[:, dc * 512:(dc + 1) * 512],
                    xmb[:, dc * 513 + off: dc * 513 + off + 512],
                    g_[:, l * 8 + dc: l * 8 + dc + 1], self.s_bc[:],
                    ALU.mult, ALU.mult), inc="v")
                c.op("vector", lambda e, dc=dc: e.tensor_tensor(
                    xnb[:, dc * 512:(dc + 1) * 512], xnb[:, dc * 512:(dc + 1) * 512],
                    shg[:], ALU.add), inc="v")
            xn_v = c.counts["v"]
            stat_v = xn_v
            tile_free = c.counts["v"]
            tile_free_t = c.counts["t"]
            c.wait("tensor", "v", xn_v)
            for pr in range(22):
                pa = pr % 2
                if ab_free[pa]:
                    c.wait("tensor", "s", gb_free[pa])
                    c.wait("tensor", "v", ab_free[pa])
                for dc in range(8):
                    c.op("tensor", lambda e, pr=pr, dc=dc, pa=pa, tlen=tlen: e.matmul(
                        p_a[:, pa * 512: pa * 512 + tlen],
                        w1[:, dc * 5632 + pr * 256: dc * 5632 + pr * 256 + 128],
                        xnb[:, dc * 512: dc * 512 + tlen],
                        start=(dc == 0), stop=False))
                c.op("tensor", lambda e, pr=pr, pa=pa, tlen=tlen: e.matmul(
                    p_a[:, pa * 512: pa * 512 + tlen],
                    b1row[0:1, pr * 256: pr * 256 + 128], self.onesrow[0:1, 0:tlen],
                    start=False, stop=True))
                for dc in range(8):
                    c.op("tensor", lambda e, pr=pr, dc=dc, pa=pa, tlen=tlen: e.matmul(
                        p_g[:, pa * 512: pa * 512 + tlen],
                        w1[:, dc * 5632 + pr * 256 + 128: dc * 5632 + pr * 256 + 256],
                        xnb[:, dc * 512: dc * 512 + tlen],
                        start=(dc == 0), stop=False))
                c.op("tensor", lambda e, pr=pr, pa=pa, tlen=tlen: e.matmul(
                    p_g[:, pa * 512: pa * 512 + tlen],
                    b1row[0:1, pr * 256 + 128: pr * 256 + 256],
                    self.onesrow[0:1, 0:tlen],
                    start=False, stop=True), inc="t")
                mm_t = c.counts["t"]
                c.wait("scalar", "t", mm_t)
                c.wait("scalar", "v", last_h4_v)
                c.op("scalar", lambda e, pa=pa, tlen=tlen: e.activation(
                    gbuf[:, 0:tlen], p_g[:, pa * 512: pa * 512 + tlen], AF.Gelu),
                    inc="s")
                gb_free[pa] = c.counts["s"]
                gl_s = c.counts["s"]
                c.wait("vector", "s", gl_s)
                c.op("vector", lambda e, pr=pr, pa=pa, tlen=tlen: e.scalar_tensor_tensor(
                    h4[:, pr * 512: pr * 512 + tlen],
                    p_a[:, pa * 512: pa * 512 + tlen], 1.0, gbuf[:, 0:tlen],
                    ALU.mult, ALU.mult), inc="v")
                ab_free[pa] = c.counts["v"]
                last_h4_v = c.counts["v"]
            h4_v = c.counts["v"]
            h4_v = c.counts["v"]
            c.wait("tensor", "v", h4_v)
            for dc in range(8):
                ob = dc % 2
                xs_sem = "e0" if ob == 0 else "e1"
                c.wait("sync", "v", xmf_free[0])
                c.op("sync", lambda e, dc=dc, toff=toff, tlen=tlen: e.dma_start(
                    out=xmf[:, 0:tlen],
                    in_=xmr[:, dc, toff: toff + tlen]), inc="d1", n=16)
                if o_free.get(ob):
                    c.wait("tensor", "v", o_free[ob])
                for kc in range(22):
                    c.op("tensor", lambda e, dc=dc, kc=kc, ob=ob, tlen=tlen: e.matmul(
                        p_o[:, ob * 512: ob * 512 + tlen],
                        w2[:, kc * 1024 + dc * 128: kc * 1024 + (dc + 1) * 128],
                        h4[:, kc * 512: kc * 512 + tlen],
                        start=(kc == 0), stop=False))
                c.op("tensor", lambda e, dc=dc, ob=ob, tlen=tlen: e.matmul(
                    p_o[:, ob * 512: ob * 512 + tlen],
                    b2row[0:1, dc * 128:(dc + 1) * 128], self.onesrow[0:1, 0:tlen],
                    start=False, stop=True), inc="t")
                mm_t = c.counts["t"]
                c.wait("vector", "t", mm_t)
                c.wait("vector", "d1")
                c.wait("vector", "e0" if ob == 1 else "e1", xob_dma[1 - ob])
                c.op("vector", lambda e, dc=dc, ob=ob, tlen=tlen: e.scalar_tensor_tensor(
                    xob[:, 0:tlen],
                    p_o[:, ob * 512: ob * 512 + tlen], 1.0,
                    xmf[:, 0:tlen], ALU.mult, ALU.add),
                    inc="v")
                xmf_free[0] = c.counts["v"]
                o_free[ob] = c.counts["v"]
                if toff <= 1535 < toff + tlen:
                    fc = 1535 - toff
                    c.op("vector", lambda e, fc=fc: e.tensor_scalar(
                        xob[:, fc: fc + 1], xob[:, fc: fc + 1],
                        self.flag[:, 0:1], None, ALU.mult), inc="v")
                    xmf_free[0] = c.counts["v"]
                    o_free[ob] = c.counts["v"]
                c.wait("sync", "v")
                c.op("sync", lambda e, dc=dc, toff=toff, tlen=tlen, xs_sem=xs_sem: e.dma_start(
                    out=xor_[:, dc, toff: toff + tlen],
                    in_=xob[:, 0:tlen]), inc=xs_sem, n=16)
                xob_dma[ob] = c.counts[xs_sem]
        c.wait("sync", "e0", c.counts["e0"])
        c.wait("sync", "e1", c.counts["e1"])
        c.barrier()
        e2.close()

    def final_stage(self):
        nc, c = self.nc, self.c
        e2 = ExitStack()
        sb2 = lambda n, sh, dt: e2.enter_context(nc.sbuf_tensor(n, sh, dt))
        pb2 = lambda n, sh, dt: e2.enter_context(nc.psum_tensor(n, sh, dt))
        xo_t = [sb2(f"xoF{i}", [128, 8 * 512], F32) for i in range(2)]
        xt_t = [sb2(f"xtF{i}", [128, 8 * 512], F32) for i in range(2)]
        xb_ = sb2("xbF", [128, 8 * 512], BF16)
        x2_ = sb2("x2F", [128, 8 * 512], BF16)
        fo = sb2("foF", [128, 8 * 512], F32)
        p_s1 = pb2("p_s1F", [128, 512], F32)
        p_s2 = pb2("p_s2F", [128, 512], F32)
        st0, st1, st2 = self.st0, self.st1, self.st2
        xout_r = self.rr(self.sc_xout[1])
        xtld_r = self.rr(self.sc_xtld)
        out_r = self.outT.rearrange("(c p) n -> p c n", p=128)
        tile_free = [0, 0]
        stat_s = 0
        stat_v = 0
        fo_dma = [0, 0]
        for t in range(4):
            toff = 1536 + t * 512
            bf = t % 2
            dsem = "d0" if bf == 0 else "d1"
            c.wait("sync", "v", tile_free[bf])
            c.op("sync", lambda e, toff=toff, bf=bf: e.dma_start(
                out=xo_t[bf][:].rearrange("p (c n) -> p c n", n=512),
                in_=xout_r[:, :, toff: toff + 512]), inc=dsem, n=16)
            c.op("sync", lambda e, toff=toff, bf=bf: e.dma_start(
                out=xt_t[bf][:].rearrange("p (c n) -> p c n", n=512),
                in_=xtld_r[:, :, toff: toff + 512]), inc=dsem, n=16)
            c.wait("vector", dsem)
            c.op("vector", lambda e, bf=bf: e.tensor_copy(xb_[:], xo_t[bf][:]), inc="v")
            c.op("vector", lambda e: e.tensor_tensor(x2_[:], xb_[:], xb_[:],
                                                     ALU.mult), inc="v")
            x2_v = c.counts["v"]
            c.wait("tensor", "v", x2_v)
            c.wait("tensor", "s", stat_s)
            c.wait("tensor", "v", stat_v)
            for dc in range(8):
                c.op("tensor", lambda e, dc=dc: e.matmul(
                    p_s1[:], self.onesb[:], xb_[:, dc * 512:(dc + 1) * 512],
                    start=(dc == 0), stop=(dc == 7)))
            for dc in range(8):
                c.op("tensor", lambda e, dc=dc: e.matmul(
                    p_s2[:], self.onesb[:], x2_[:, dc * 512:(dc + 1) * 512],
                    start=(dc == 0), stop=(dc == 7)),
                    inc=("t" if dc == 7 else None))
            ln_chain(c, st0, st1, st2, self.s_bc, self.sh_bc, p_s1, p_s2)
            stat_s = c.counts["s"]
            stat_v = c.counts["v"]
            c.wait("vector", "o0" if bf == 1 else "o1", fo_dma[1 - bf])
            c.op("vector", lambda e, bf=bf: e.scalar_tensor_tensor(
                fo[:].rearrange("p (c n) -> p c n", n=512),
                xo_t[bf][:].rearrange("p (c n) -> p c n", n=512), 1.0,
                rep8(self.s_bc[:]), ALU.mult, ALU.mult), inc="v")
            c.op("vector", lambda e: e.tensor_tensor(
                fo[:].rearrange("p (c n) -> p c n", n=512),
                fo[:].rearrange("p (c n) -> p c n", n=512), rep8(self.sh_bc[:]),
                ALU.add), inc="v")
            c.op("vector", lambda e, bf=bf: e.tensor_tensor(
                fo[:], fo[:], xt_t[bf][:], ALU.add), inc="v")
            tile_free[bf] = c.counts["v"]
            c.wait("sync", "v")
            osem = "o0" if bf == 0 else "o1"
            c.op("sync", lambda e, t=t: e.dma_start(
                out=out_r[:, :, t * 512:(t + 1) * 512],
                in_=fo[:].rearrange("p (c n) -> p c n", n=512)), inc=osem, n=16)
            fo_dma[bf] = c.counts[osem]
        e2.close()


def build_l2():
    return L2().build()


BF = ml_dtypes.bfloat16


def make_tables(S):
    t = (S - 1536 + np.arange(TD)).astype(np.float64)
    half_idx = np.arange(0, 64, 2, dtype=np.float64)
    inv_freq = 1.0 / 10000.0 ** (half_idx / 64.0)
    freqs = t[:, None] * inv_freq[None, :]
    freqs64 = np.concatenate([freqs, freqs], axis=1)
    base = (half_idx + 0.4 * 64) / (1.4 * 64)
    power = (t - 4096.0) / 256.0
    scale = base[None, :] ** power[:, None]
    scale64 = np.concatenate([scale, scale], axis=1)
    cos, sin = np.cos(freqs64), np.sin(freqs64)
    cs = (cos * scale64).T
    ss = (sin * scale64).T
    ci = (cos / scale64).T
    si = (sin / scale64).T
    out = np.zeros((128, 4 * TD), dtype=BF)
    for i, tab in enumerate([cs, ss, ci, si]):
        tt = np.tile(tab, (2, 1))
        out[:, i * TD:(i + 1) * TD] = tt.astype(BF)
    return out


def make_RT():
    R = np.zeros((128, 128), dtype=np.float32)
    for hb in (0, 64):
        for i in range(32):
            R[hb + i, hb + 32 + i] = -1.0
            R[hb + 32 + i, hb + i] = 1.0
    return np.ascontiguousarray(R.T).astype(BF)


def make_masks(S):
    big = np.float32(-1e30)
    row = np.arange(128)[:, None]
    qq = np.arange(128)[None, :]
    M0 = np.where(qq <= row, 0.0, big).astype(BF)
    M4 = np.where(qq >= row, 0.0, big).astype(BF)
    mreg = np.concatenate([M0, M4], axis=1)
    tiles = []
    for l in range(2):
        for qb in range(NBND[l]):
            qs_abs = S + (QS[l] - 1536) + 128 * qb
            for ko in range(5):
                j_abs = qs_abs - 512 + ko * 128 + row
                q_abs = qs_abs + qq
                valid = (j_abs >= 0) & (j_abs <= q_abs) & (q_abs <= j_abs + 512)
                valid = valid | (q_abs < 0)
                tiles.append(np.where(valid, 0.0, big).astype(BF))
    return mreg, np.stack(tiles)


def slice_pad(arr2d, S, dtype):
    out = np.zeros((1024, TD), dtype=dtype)
    lo, hi = S - 1536, S + 2048
    src_lo, src_hi = max(lo, 0), min(hi, 8192)
    out[:, src_lo - lo: src_hi - lo] = arr2d[:, src_lo:src_hi].astype(dtype)
    return out


def l2_in_maps(inp, attn_full):
    x = np.asarray(inp["L0_attn_x"], dtype=np.float32)
    w_out = np.asarray(inp["L0_attn_w_out"], dtype=np.float32)
    b_out = np.asarray(inp["L0_attn_b_out"], dtype=np.float32)
    wqkv = np.asarray(inp["L0_ffn_wqkv"], dtype=np.float32)
    q_scale = np.asarray(inp["L0_ffn_q_scale"], dtype=np.float32)
    k_scale = np.asarray(inp["L0_ffn_k_scale"], dtype=np.float32)
    wo = np.asarray(inp["L0_ffn_wo"], dtype=np.float32)
    ln_g = np.asarray(inp["L0_ffn_ln_g"], dtype=np.float32)
    ln_b = np.asarray(inp["L0_ffn_ln_b"], dtype=np.float32)
    w1 = np.asarray(inp["L0_ffn_w1"], dtype=np.float32)
    b1 = np.asarray(inp["L0_ffn_b1"], dtype=np.float32)
    w2 = np.asarray(inp["L0_ffn_w2"], dtype=np.float32)
    b2 = np.asarray(inp["L0_ffn_b2"], dtype=np.float32)

    woutT = np.ascontiguousarray(w_out.T)
    w1r = np.zeros((2, 1024, 5632), dtype=np.float32)
    b1r = np.zeros((2, 1, 5632), dtype=np.float32)
    for l in range(2):
        a, g = w1[l][:, 0:2730], w1[l][:, 2730:5460]
        ba, bg = b1[l][0:2730], b1[l][2730:5460]
        for pr in range(22):
            n = min(128, 2730 - pr * 128)
            w1r[l][:, pr * 256: pr * 256 + n] = a[:, pr * 128: pr * 128 + n]
            w1r[l][:, pr * 256 + 128: pr * 256 + 128 + n] = g[:, pr * 128: pr * 128 + n]
            b1r[l][0, pr * 256: pr * 256 + n] = ba[pr * 128: pr * 128 + n]
            b1r[l][0, pr * 256 + 128: pr * 256 + 128 + n] = bg[pr * 128: pr * 128 + n]
    w2p = np.zeros((2, 2816, 1024), dtype=np.float32)
    w2p[:, 0:2730] = w2
    b2row = b2[:, None, :]
    lng = np.zeros((128, 16), dtype=np.float32)
    lnb = np.zeros((128, 16), dtype=np.float32)
    for l in range(2):
        lng[:, l * 8:(l + 1) * 8] = ln_g[l].reshape(8, 128).T
        lnb[:, l * 8:(l + 1) * 8] = ln_b[l].reshape(8, 128).T
    qs8 = np.stack([np.tile(q_scale[l] * 8.0, 2) for l in range(2)], axis=1)
    ks1 = np.stack([np.tile(k_scale[l], 2) for l in range(2)], axis=1)
    RT = make_RT()
    identb = np.eye(128, dtype=BF)
    onesb = np.ones((128, 128), dtype=BF)
    oneshh = np.zeros((128, 2), dtype=BF)
    oneshh[0:64, 0] = 1; oneshh[64:128, 1] = 1
    selT = np.ascontiguousarray(oneshh.T)
    onesrow = np.ones((1, 512), dtype=BF)
    onesrow_f = np.ones((1, 64), dtype=np.float32)
    csts = np.broadcast_to(np.array(
        [1.0 / 1024.0, 1e-5, -0.5, -1.0, 1e-30, 0, 0, 0], dtype=np.float32),
        (128, 8)).copy()

    in_maps = []
    for core in range(8):
        b, r = core // 4, core % 4
        S = 2048 * r
        mreg, mbnd = make_masks(S)
        flag = np.full((128, 1), 0.0 if r == 0 else 1.0, dtype=np.float32)
        in_maps.append({
            "attnT": slice_pad(attn_full[b], S, BF),
            "xT": slice_pad(np.ascontiguousarray(x[b].T), S, np.float32),
            "woutT": woutT, "boutrow": b_out[None, :],
            "wqkv": wqkv, "wo": wo, "w1r": w1r, "b1r": b1r, "w2p": w2p,
            "b2row": b2row, "lng": lng, "lnb": lnb, "qs8": qs8, "ks1": ks1,
            "tabs": make_tables(S), "RT": RT, "identb": identb, "onesb": onesb,
            "oneshh": oneshh, "selT": selT, "onesrow": onesrow,
            "onesrow_f": onesrow_f, "mreg": mreg, "mbnd": mbnd,
            "flag": flag, "csts": csts,
        })
    return in_maps




def kernel(**inputs):
    ims1 = l1_in_maps(inputs)
    nc1 = build_l1()
    res1 = run_bass_kernel_spmd(nc1, ims1, core_ids=list(range(8)))
    attn_full = []
    for b in range(2):
        attn_full.append(np.concatenate(
            [np.asarray(res1.results[b * 4 + hg]["attnoutT"]) for hg in range(4)],
            axis=0))
    ims2 = l2_in_maps(inputs, attn_full)
    nc2 = build_l2()
    res2 = run_bass_kernel_spmd(nc2, ims2, core_ids=list(range(8)))
    out = np.empty((2, 8192, 1024), dtype=np.float32)
    for core in range(8):
        b, r = core // 4, core % 4
        S = 2048 * r
        out[b, S:S + 2048, :] = np.asarray(res2.results[core]["outT"]).T
    return out


# version 04
# speedup vs baseline: 1.1353x; 1.1353x over previous
import numpy as np
import ml_dtypes
from contextlib import ExitStack
import concourse.bass as bass
import concourse.mybir as mybir
from concourse.bass_utils import run_bass_kernel_spmd

F32 = mybir.dt.float32
BF16 = mybir.dt.bfloat16
AF = mybir.ActivationFunctionType
ALU = mybir.AluOpType
AX = mybir.AxisListType
ENGS = ("sync", "gpsimd", "tensor", "scalar", "vector")

N = 8192
D = 1024
NT = 16
TT = 512


class _SkipRest(Exception):
    pass


class Ctx:
    def __init__(self, nc):
        self.nc = nc
        self.streams = {e: [] for e in ENGS}
        self.sems = {}
        self.counts = {}

    def sem(self, es, name):
        h = es.enter_context(self.nc.semaphore(name))
        self.sems[name] = h
        self.counts[name] = 0

    SELFSEM = {"vector": "v", "scalar": "s", "gpsimd": "g"}

    def op(self, eng, fn, inc=None, n=1):
        if not getattr(self, "enabled", True):
            return self.counts.get(inc, 0) if inc else None
        ss = self.SELFSEM.get(eng)
        if ss is not None and self.counts.get(ss, 0) > 0:
            self.streams[eng].append(("wait", self.sems[ss], self.counts[ss]))
        if inc is not None:
            self.counts[inc] += n
            pair = (self.sems[inc], n)
        else:
            pair = None
        self.streams[eng].append(("op", fn, pair))
        return self.counts[inc] if inc else None

    def wait(self, eng, sem_name, val=None):
        if not getattr(self, "enabled", True):
            return
        v = self.counts[sem_name] if val is None else val
        if v > 0:
            self.streams[eng].append(("wait", self.sems[sem_name], v))

    def raw(self, eng, fn, inc=None, n=1):
        if inc is not None:
            self.counts[inc] += n
        self.streams[eng].append(("op", fn, None))

    def barrier(self):
        snap = dict(self.counts)
        for e in ENGS:
            for s, v in snap.items():
                self.wait(e, s, v)

    def finalize(self, block):
        for e in ENGS:
            stream = self.streams[e]

            def make(stream):
                def run(eng):
                    for item in stream:
                        if item[0] == "op":
                            _, fn, pair = item
                            inst = fn(eng)
                            if pair is not None and inst is not None:
                                inst.then_inc(pair[0], pair[1])
                        else:
                            _, semh, v = item
                            eng.wait_ge(semh, v)
                return run
            getattr(block, e)(make(stream))




def phase1(nc, c, sh, agin):
    PH = "D"
    CUT = 9
    NORANK1 = False
    xT = nc.dram_tensor("xTfull", [D, N], F32, kind="ExternalInput").ap()
    wqkvT = nc.dram_tensor("wqkvT", [D, 768], F32, kind="ExternalInput").ap()
    gamma_c = nc.dram_tensor("gamma_c", [128, 8], F32, kind="ExternalInput").ap()
    beta_cb = nc.dram_tensor("beta_cb", [128, 8], BF16, kind="ExternalInput").ap()
    masks_i = nc.dram_tensor("masks", [128, 5 * 512], BF16, kind="ExternalInput").ap()
    aout = agin
    negm_dram = nc.dram_tensor("negm1_scratch", [1, 64], BF16, kind="Internal").ap()

    es = ExitStack()
    sb = lambda name, shape, dt: es.enter_context(nc.sbuf_tensor(name, shape, dt))
    with es:
        identb = sh["identb"]
        onesb = sh["onesb"]
        onesrow = sh["onesrow"]
        onesrow_f = sh["onesrow_f"]
        oneshh = sh["oneshh"]

        xt = [sb(f"xt{i}", [128, 8 * TT], F32) for i in range(2)]
        xb = sb("xb", [128, 8 * TT], BF16)
        xs = sb("xs", [128, 8 * TT], BF16)
        wstage = sb("wstage", [128, 768], F32)
        wraw = sb("wraw", [128, 768], BF16)
        wT = sb("wT", [128, 8 * 768], BF16)
        gam = sb("gam", [128, 8], F32)
        betab = sb("betab", [128, 8], BF16)
        maskt = sb("maskt", [128, 5 * 512], BF16)
        colsum_row = sb("colsum_row", [1, 768], BF16)
        bias_row = sb("bias_row", [1, 768], BF16)
        shift_row = sb("shift_row", [1, 512], BF16)
        st0 = sb("st0", [128, TT], F32)
        st1 = sb("st1", [128, TT], F32)
        st2 = sb("st2", [128, TT], F32)
        st3 = sb("st3", [128, TT], F32)
        s_bc = sb("s_bc", [128, TT], F32)
        shift_bc = sb("shift_bc", [128, TT], F32)
        Qp = [sb(f"Qp{i}", [128, N], BF16) for i in range(2)]
        Kp = [sb(f"Kp{i}", [128, N], BF16) for i in range(2)]
        V_km = sb("V_km", [128, 64 * 260], BF16)
        vstage = sb("vstage", [128, 1024], BF16)
        PT = [sb(f"PT{i}", [128, 1536], BF16) for i in range(2)]
        biasneg = sb("biasneg", [128, 64], F32)
        recip_row = sb("recip_row", [1, 512], F32)
        accst = [sb(f"accst{i}", [65, 512], F32) for i in range(2)]
        recip_bc = sb("recip_bc", [64, 512], F32)
        aout_sb = [sb(f"aout{i}", [64, 512], BF16) for i in range(2)]
        negm2 = sb("negm2", [2, 32], F32)
        negm2b = sb("negm2b", [2, 32], BF16)
        negmrow = sb("negmrow", [1, 64], BF16)

        es_b = ExitStack()
        pb = lambda name, shape, dt: es_b.enter_context(nc.psum_tensor(name, shape, dt))
        p_sumx = pb("p_sumx", [128, TT], F32)
        p_sumx2 = pb("p_sumx2", [128, TT], F32)
        p_qkv = pb("p_qkv", [128, 3 * 512], F32)
        p_vtr = pb("p_vtr", [128, 2 * 1024], BF16)
        p_nb = p_sumx

        for dst, src in [(gam, gamma_c), (betab, beta_cb), (maskt, masks_i)]:
            c.op("sync", lambda e, d=dst, s=src: e.dma_start(out=d[:], in_=s), inc="xc", n=16)
        vko = V_km[:].rearrange("p (c h d) -> p c h d", h=4, d=65)
        c.op("vector", lambda e: e.memset(vko[:, :, :, 64:65], 1.0), inc="v")
        c.wait("vector", "xc")
        for dc in range(8):
            wv = c.counts["v"]
            c.wait("sync", "v", wv)
            if dc > 0:
                c.wait("sync", "t")
            c.op("sync", lambda e, dc=dc: e.dma_start(
                out=wstage[:], in_=wqkvT[dc * 128:(dc + 1) * 128, :]), inc="xw", n=16)
            c.wait("vector", "xw")
            c.op("vector", lambda e, dc=dc: e.tensor_scalar(
                wT[:, dc * 768:(dc + 1) * 768], wstage[:], gam[:, dc:dc + 1], None,
                ALU.mult), inc="v")
            c.op("vector", lambda e: e.tensor_copy(wraw[:], wstage[:]), inc="v")
            wr_v = c.counts["v"]
            c.wait("tensor", "v", wr_v)
            c.op("tensor", lambda e, dc=dc: e.matmul(
                p_sumx[0:1, 0:512], betab[:, dc:dc + 1], wraw[:, 0:512],
                start=(dc == 0), stop=(dc == 7)), inc="t")
            c.op("tensor", lambda e, dc=dc: e.matmul(
                p_sumx2[0:1, 0:256], betab[:, dc:dc + 1], wraw[:, 512:768],
                start=(dc == 0), stop=(dc == 7)), inc="t")

        c.wait("tensor", "v")
        for (o0, o1) in ((0, 512), (512, 768)):
            for dc in range(8):
                c.op("tensor", lambda e, dc=dc, o0=o0, o1=o1: e.matmul(
                    p_qkv[0:1, o0:o1], onesb[:, 0:1],
                    wT[:, dc * 768 + o0: dc * 768 + o1],
                    start=(dc == 0), stop=(dc == 7)),
                    inc=("t" if (dc == 7 and o0 == 512) else None))
        c.wait("vector", "t")
        c.op("vector", lambda e: e.tensor_copy(colsum_row[:], p_qkv[0:1, 0:768]), inc="v")
        c.op("vector", lambda e: e.tensor_copy(bias_row[0:1, 0:512], p_sumx[0:1, 0:512]), inc="v")
        c.op("vector", lambda e: e.tensor_copy(bias_row[0:1, 512:768], p_sumx2[0:1, 0:256]), inc="v")

        NTILES = NT if PH != "A" else 0
        xt_free_v = [0, 0]
        qkv_evac_v = {}
        vtr_free_v = {}
        for t in range(NTILES):
            buf = t % 2
            c.wait("sync", "v", xt_free_v[buf])
            dsem = "d0" if buf == 0 else "d1"
            c.op("sync", lambda e, t=t, buf=buf: e.dma_start(
                out=xt[buf][:].rearrange("p (c n) -> p c n", n=TT),
                in_=xT.rearrange("(c p) n -> p c n", p=128)[:, :, t * TT:(t + 1) * TT]),
                inc=dsem, n=16)
            c.wait("vector", dsem)
            c.op("vector", lambda e, buf=buf: e.tensor_copy(xb[:], xt[buf][:]), inc="v")
            c.op("vector", lambda e: e.tensor_tensor(xs[:], xb[:], xb[:], ALU.mult), inc="v")
            xb_ready = c.counts["v"]
            c.enabled = CUT >= 1 and PH != "A"
            c.wait("tensor", "v", xb_ready)
            if t > 0:
                c.wait("tensor", "s", stats_consumed_s)
                c.wait("tensor", "v", stats_consumed_v)
            for dc in range(8):
                c.op("tensor", lambda e, dc=dc: e.matmul(
                    p_sumx[:], onesb[:], xb[:, dc * TT:(dc + 1) * TT],
                    start=(dc == 0), stop=(dc == 7)))
            for dc in range(8):
                c.op("tensor", lambda e, dc=dc: e.matmul(
                    p_sumx2[:], onesb[:], xs[:, dc * TT:(dc + 1) * TT],
                    start=(dc == 0), stop=(dc == 7)),
                    inc=("t" if dc == 7 else None))
            stats_t = c.counts["t"]
            c.wait("scalar", "t", stats_t)
            c.op("scalar", lambda e: e.activation(st0[:], p_sumx[:], AF.Square,
                                                  scale=1.0 / 1024.0), inc="s")
            stats_consumed_s = c.counts["s"]
            c.wait("vector", "s")
            c.op("vector", lambda e: e.scalar_tensor_tensor(
                st1[:], p_sumx2[:], 1.0 / 1024.0, st0[:], ALU.mult, ALU.subtract), inc="v")
            c.wait("scalar", "v")
            c.op("scalar", lambda e: e.activation(st2[:], st1[:], AF.Ln, bias=1e-5), inc="s")
            c.op("scalar", lambda e: e.activation(st0[:], st2[:], AF.Exp, scale=-0.5), inc="s")
            c.op("scalar", lambda e: e.activation(st3[:], st2[:], AF.Exp, scale=-1.0), inc="s")
            c.wait("vector", "s")
            c.op("vector", lambda e: e.tensor_scalar(
                st1[:], st3[:], -1e-5, 1.0 + 1e-5, ALU.mult, ALU.add), inc="v")
            c.wait("scalar", "v")
            c.op("scalar", lambda e: e.activation(st2[:], st1[:], AF.Ln), inc="s")
            c.op("scalar", lambda e: e.activation(st3[:], st2[:], AF.Exp, scale=-0.5), inc="s")
            c.wait("vector", "s")
            c.op("vector", lambda e: e.tensor_tensor(s_bc[:], st0[:], st3[:], ALU.mult), inc="v")
            c.op("vector", lambda e: e.scalar_tensor_tensor(
                shift_bc[:], p_sumx[:], -1.0 / 1024.0, s_bc[:], ALU.mult, ALU.mult), inc="v")
            stats_consumed_v = c.counts["v"]
            c.op("vector", lambda e: e.tensor_copy(shift_row[:], shift_bc[0:1, :]), inc="v")
            c.enabled = CUT >= 2 and PH != "A"
            sbc3 = s_bc[:].rearrange("p (a n) -> p a n", a=1).broadcast_to([128, 8, TT])
            c.op("vector", lambda e, buf=buf: e.tensor_tensor(
                xs[:].rearrange("p (c n) -> p c n", n=TT),
                xt[buf][:].rearrange("p (c n) -> p c n", n=TT), sbc3, ALU.mult), inc="v")
            xt_free_v[buf] = c.counts["v"]
            xs_ready = c.counts["v"]

            c.enabled = CUT >= 3 and PH != "A"
            c.wait("tensor", "v", xs_ready)
            for ec in range(6):
                bank = ec % 3
                if qkv_evac_v.get(bank):
                    c.wait("tensor", "v", qkv_evac_v[bank])
                for dc in range(8):
                    c.op("tensor", lambda e, dc=dc, ec=ec, bank=bank: e.matmul(
                        p_qkv[:, bank * 512:(bank + 1) * 512],
                        wT[:, dc * 768 + ec * 128: dc * 768 + (ec + 1) * 128],
                        xs[:, dc * TT:(dc + 1) * TT],
                        start=(dc == 0), stop=(NORANK1 and dc == 7)),
                        inc=("t" if (NORANK1 and dc == 7) else None))
                if not NORANK1:
                    c.op("tensor", lambda e, ec=ec, bank=bank: e.matmul(
                        p_qkv[:, bank * 512:(bank + 1) * 512],
                        colsum_row[0:1, ec * 128:(ec + 1) * 128], shift_row[:],
                        start=False, stop=False))
                    c.op("tensor", lambda e, ec=ec, bank=bank: e.matmul(
                        p_qkv[:, bank * 512:(bank + 1) * 512],
                        bias_row[0:1, ec * 128:(ec + 1) * 128], onesrow[:],
                        start=False, stop=True), inc="t")
                mm_t = c.counts["t"]
                c.wait("vector", "t", mm_t)
                pslice = (lambda bank=bank: p_qkv[:, bank * 512:(bank + 1) * 512])()
                if ec < 2:
                    c.op("vector", lambda e, ec=ec, t=t, ps=pslice: e.tensor_scalar(
                        Qp[ec][:, t * TT:(t + 1) * TT], ps, 0.125, None, ALU.mult), inc="v")
                elif ec < 4:
                    c.op("vector", lambda e, ec=ec, t=t, ps=pslice: e.tensor_copy(
                        Kp[ec - 2][:, t * TT:(t + 1) * TT], ps), inc="v")
                else:
                    c.op("vector", lambda e, ec=ec, ps=pslice: e.tensor_copy(
                        vstage[:, (ec - 4) * 512:(ec - 3) * 512], ps), inc="v")
                qkv_evac_v[bank] = c.counts["v"]
            vstage_ready = c.counts["v"]

            c.enabled = CUT >= 4 and PH != "A"
            c.wait("tensor", "v", vstage_ready)
            for ec in range(2):
                for sc in range(4):
                    tb = (ec * 4 + sc) % 2
                    if t > 0 or ec > 0 or sc > 1:
                        c.wait("tensor", "v", vtr_free_v.get(tb, 0))
                    c.op("tensor", lambda e, ec=ec, sc=sc, tb=tb: e.transpose(
                        p_vtr[:, tb * 1024: tb * 1024 + 128],
                        vstage[:, ec * 512 + sc * 128: ec * 512 + (sc + 1) * 128],
                        identb[:]), inc="t")
                    tr_t = c.counts["t"]
                    cc = 4 * t + sc
                    c.wait("vector", "t", tr_t)
                    c.op("vector", lambda e, cc=cc, ec=ec, tb=tb: e.tensor_copy(
                        V_km[:, cc * 260 + ec * 130: cc * 260 + (ec + 1) * 130]
                        .rearrange("p (h d) -> p h d", d=65)[:, :, 0:64],
                        p_vtr[:, tb * 1024: tb * 1024 + 128]
                        .rearrange("p (h d) -> p h d", d=64)), inc="v")
                    vtr_free_v[tb] = c.counts["v"]

        c.enabled = True
        for p in (range(2) if PH not in ("A", "B") else range(0)):
            for half in range(2):
                c.op("vector", lambda e, p=p, half=half: e.tensor_tensor(
                    xb[:], Qp[p][:, half * 4096:(half + 1) * 4096],
                    Kp[p][:, half * 4096:(half + 1) * 4096], ALU.mult), inc="v")
                qk_v = c.counts["v"]
                c.wait("tensor", "v", qk_v)
                for t8 in range(8):
                    c.op("tensor", lambda e, t8=t8: e.matmul(
                        p_nb[0:2, 0:512], oneshh[:, 0:2], xb[:, t8 * 512:(t8 + 1) * 512],
                        start=True, stop=True), inc="t")
                    mm_t = c.counts["t"]
                    c.wait("vector", "t", mm_t)
                    qt = half * 8 + t8
                    c.op("vector", lambda e, p=p, qt=qt: e.reduce_max(
                        negm2[0:2, p * 16 + qt: p * 16 + qt + 1],
                        p_nb[0:2, 0:512], axis=AX.X), inc="v")
                    c.wait("tensor", "v")
        c.enabled = PH not in ("A", "B")
        c.op("vector", lambda e: e.tensor_scalar(
            negm2b[:], negm2[:], -1.0, None, ALU.mult), inc="v")
        c.wait("gpsimd", "v")
        c.op("gpsimd", lambda e: e.dma_start(
            out=negm_dram.rearrange("o (r pq) -> (o r) pq", r=2), in_=negm2b[:]),
            inc="g", n=16)
        c.wait("gpsimd", "g")
        c.op("gpsimd", lambda e: e.dma_start(
            out=negmrow[:].rearrange("o (p r q) -> o p r q", p=2, r=2),
            in_=negm_dram.rearrange("o (r p q) -> o p r q", r=2, p=2)),
            inc="g", n=16)
        c.wait("tensor", "g")
        for h in range(4):
            for qt in range(NT):
                c.op("tensor", lambda e, h=h, qt=qt: e.matmul(
                    p_nb[:, 64 + h * 16 + qt: 65 + h * 16 + qt],
                    onesrow[0:1, 0:128], negmrow[0:1, h * 16 + qt: h * 16 + qt + 1],
                    start=True, stop=True),
                    inc=("t" if (h == 3 and qt == NT - 1) else None))
        c.wait("vector", "t")
        c.op("vector", lambda e: e.tensor_copy(biasneg[:], p_nb[:, 64:128]), inc="v")

        c.enabled = PH not in ("A", "B", "C")
        c.barrier()
        es_b.close()
        p_sim = [es.enter_context(nc.psum_tensor(f"p_sim{i}", [128, 1536], F32))
                 for i in range(2)]
        p_acc = es.enter_context(nc.psum_tensor("p_acc", [65, 512], F32))
        p_bcast = es.enter_context(nc.psum_tensor("p_bcast", [64, 512], F32))

        sim_free_s = [0, 0]
        pt_free_t = [0, 0]
        acc_free_v = 0
        bcast_free_v = 0
        aout_free_o = [0, 0]
        nqt_done = 0
        NH = 1 if PH == "D1" else 4
        NQT = 2 if PH == "D1" else NT
        for h in range(NH):
            p = h // 2
            rb = 64 * (h % 2)
            for qt in range(NQT):
                nk = 4 * qt + 4
                ng = (nk + 2) // 3
                exp_s = []
                av_count = 0
                total_av = 3 * ng
                for g in range(ng):
                    jb = g % 2
                    c.wait("tensor", "s", sim_free_s[jb])
                    simops = []
                    for j in range(3):
                        kc = 3 * g + j
                        kce = min(kc, 63)
                        patt = None
                        if kc > 4 * qt + 3:
                            patt = 4
                        elif kc >= 4 * qt:
                            patt = kc - 4 * qt
                        simops.append(lambda e, jb=jb, j=j, kce=kce, p=p, rb=rb, qt=qt, patt=patt: e.matmul(
                            p_sim[jb][:, j * 512:(j + 1) * 512],
                            Kp[p][rb:rb + 64, kce * 128:(kce + 1) * 128],
                            Qp[p][rb:rb + 64, qt * 512:(qt + 1) * 512],
                            start=True, stop=(patt is None)))
                        if patt is not None:
                            simops.append(lambda e, jb=jb, j=j, patt=patt: e.matmul(
                                p_sim[jb][:, j * 512:(j + 1) * 512],
                                identb[:], maskt[:, patt * 512:(patt + 1) * 512],
                                start=False, stop=True))
                    for fi, f in enumerate(simops):
                        c.op("tensor", f, inc=("t" if fi == len(simops) - 1 else None))
                    sim_t = c.counts["t"]
                    c.wait("scalar", "t", sim_t)
                    c.op("scalar", lambda e, jb=jb, h=h, qt=qt: e.activation(
                        PT[jb][:], p_sim[jb][:], AF.Exp,
                        bias=biasneg[:, h * 16 + qt: h * 16 + qt + 1], scale=1.0), inc="s")
                    exp_s.append(c.counts["s"])
                    sim_free_s[jb] = c.counts["s"]
                    if g >= 1:
                        gp = g - 1
                        jp = gp % 2
                        c.wait("tensor", "s", exp_s[gp])
                        if g == 1:
                            c.wait("tensor", "v", acc_free_v)
                        for j in range(3):
                            kc = 3 * gp + j
                            kce = min(kc, 63)
                            c.op("tensor", lambda e, jp=jp, j=j, kce=kce, h=h,
                                 st=(av_count == 0), sp=False: e.matmul(
                                p_acc[:], V_km[:, kce * 260 + h * 65: kce * 260 + h * 65 + 65],
                                PT[jp][:, j * 512:(j + 1) * 512],
                                start=st, stop=sp), inc=("t" if j == 2 else None))
                            av_count += 1
                        pt_free_t[jp] = c.counts["t"]
                gp = ng - 1
                jp = gp % 2
                c.wait("tensor", "s", exp_s[gp])
                if ng == 1:
                    c.wait("tensor", "v", acc_free_v)
                for j in range(3):
                    kc = 3 * gp + j
                    kce = min(kc, 63)
                    c.op("tensor", lambda e, jp=jp, j=j, kce=kce, h=h,
                         st=(av_count == 0), sp=(j == 2): e.matmul(
                        p_acc[:], V_km[:, kce * 260 + h * 65: kce * 260 + h * 65 + 65],
                        PT[jp][:, j * 512:(j + 1) * 512],
                        start=st, stop=sp), inc=("t" if j == 2 else None))
                    av_count += 1
                pt_free_t[jp] = c.counts["t"]
                av_done_t = c.counts["t"]

                ob = nqt_done % 2
                c.wait("vector", "t", av_done_t)
                c.op("vector", lambda e: e.reciprocal(recip_row[:], p_acc[64:65, :]), inc="v")
                recip_v = c.counts["v"]
                c.wait("tensor", "v", max(recip_v, bcast_free_v))
                c.op("tensor", lambda e: e.matmul(
                    p_bcast[:], onesrow_f[:], recip_row[:], start=True, stop=True), inc="t")
                bc_t = c.counts["t"]
                c.wait("vector", "t", bc_t)
                c.op("vector", lambda e: e.tensor_copy(recip_bc[:], p_bcast[:]), inc="v")
                bcast_free_v = c.counts["v"]
                c.wait("vector", "o0" if ob == 0 else "o1", aout_free_o[ob])
                c.op("vector", lambda e, ob=ob: e.scalar_tensor_tensor(
                    aout_sb[ob][:], p_acc[0:64, :], 1.0, recip_bc[:],
                    ALU.mult, ALU.mult), inc="v")
                acc_free_v = c.counts["v"]
                c.wait("sync", "v", acc_free_v)
                osem = "o0" if ob == 0 else "o1"
                c.op("sync", lambda e, h=h, qt=qt, ob=ob: e.dma_start(
                    out=aout[64 * h: 64 * (h + 1), qt * 512:(qt + 1) * 512],
                    in_=aout_sb[ob][:]), inc=osem, n=16)
                aout_free_o[ob] = c.counts[osem]
                nqt_done += 1






def make_masks_l1():
    m = np.zeros((128, 5 * 512), dtype=np.float32)
    for pnum in range(4):
        row = np.arange(128)[:, None] + 128 * pnum
        qq = np.arange(512)[None, :]
        m[:, pnum * 512:(pnum + 1) * 512] = np.where(row <= qq, 0.0, -1e30)
    m[:, 4 * 512:] = -1e30
    return m.astype(ml_dtypes.bfloat16)


def l1_in_maps(inp):
    x = np.asarray(inp["L0_attn_x"], dtype=np.float32)
    w_qkv = np.asarray(inp["L0_attn_w_qkv"], dtype=np.float32)
    gamma = np.asarray(inp["L0_attn_ln_gamma"], dtype=np.float32)
    beta = np.asarray(inp["L0_attn_ln_beta"], dtype=np.float32)
    gamma_c = np.ascontiguousarray(gamma.reshape(8, 128).T)
    beta_cb = np.ascontiguousarray(beta.reshape(8, 128).T).astype(ml_dtypes.bfloat16)
    masks = make_masks_l1()
    in_maps = []
    for core in range(8):
        b, hg = core // 4, core % 4
        xTb = np.ascontiguousarray(x[b].T)
        rows = []
        for part in range(3):
            rows.extend(range(part * 1024 + hg * 256, part * 1024 + (hg + 1) * 256))
        wqkvT = np.ascontiguousarray(w_qkv[rows, :].T)
        in_maps.append({
            "xTfull": xTb, "wqkvT": wqkvT, "gamma_c": gamma_c,
            "beta_cb": beta_cb, "masks": masks,
        })
    return in_maps




TD = 3584
QS = [640, 1408]
KS = [128, 896]
CS = [768, 1536]
NQB = [(TD - QS[0]) // 128, (TD - QS[1]) // 128]
NBND = [11, 5]
MAXQB = 56
H = 8


def ln_chain(c, st0, st1, st2, s_out, sh_out, p_sumx, p_sumx2):
    c.wait("scalar", "t")
    c.op("scalar", lambda e: e.activation(st0[:], p_sumx[:], AF.Square,
                                          scale=1.0 / 1024.0), inc="s")
    c.wait("vector", "s")
    c.op("vector", lambda e: e.scalar_tensor_tensor(
        st1[:], p_sumx2[:], 1.0 / 1024.0, st0[:], ALU.mult, ALU.subtract), inc="v")
    c.wait("scalar", "v")
    c.op("scalar", lambda e: e.activation(st2[:], st1[:], AF.Ln, bias=1e-5), inc="s")
    c.op("scalar", lambda e: e.activation(s_out[:], st2[:], AF.Exp, scale=-0.5), inc="s")
    c.wait("vector", "s")
    c.op("vector", lambda e: e.scalar_tensor_tensor(
        sh_out[:], p_sumx[:], -1.0 / 1024.0, s_out[:], ALU.mult, ALU.mult), inc="v")


def rep8(ap):
    return ap.rearrange("p (a n) -> p a n", a=1).broadcast_to([128, 8, 512])


class L2:
    def __init__(self, nc):
        self.nc = nc
        di = lambda n, sh, dt: nc.dram_tensor(n, sh, dt, kind="ExternalInput").ap()
        self.xT = di("xTwin", [1024, TD], F32)
        self.woutT = di("woutT", [1024, 1024], F32)
        self.boutrow = di("boutrow", [1, 1024], F32)
        self.wqkv_i = di("wqkv", [2, 1024, 1536], F32)
        self.wo_i = di("wo", [2, 512, 1024], F32)
        self.w1_i = di("w1r", [2, 1024, 5632], F32)
        self.b1_i = di("b1r", [2, 1, 5632], F32)
        self.w2_i = di("w2p", [2, 2816, 1024], F32)
        self.b2_i = di("b2row", [2, 1, 1024], F32)
        self.lng_i = di("lng", [128, 16], F32)
        self.lnb_i = di("lnb", [128, 16], F32)
        self.qs8_i = di("qs8", [128, 2], F32)
        self.ks_i = di("ks1", [128, 2], F32)
        self.tabs_i = di("tabs", [128, 4 * TD], BF16)
        self.RT_i = di("RT", [128, 128], BF16)
        self.mreg_i = di("mreg", [128, 2 * 128], BF16)
        self.mbnd_i = di("mbnd", [sum(NBND) * 5, 128, 128], BF16)
        self.flag_i = di("flag", [128, 1], F32)
        self.outT = nc.dram_tensor("outT", [1024, 2048], F32, kind="ExternalOutput").ap()
        ds = lambda n: nc.dram_tensor(n, [1024, TD], F32, kind="Internal").ap()
        self.sc_xtld = ds("sc_xtld")
        self.sc_xin = ds("sc_xin")
        self.sc_xmid = [ds(f"sc_xmid{l}") for l in range(2)]
        self.sc_xout = [ds(f"sc_xout{l}") for l in range(2)]
        self.negm_dram = nc.dram_tensor(
            "negm_scratch", [1, 2 * 4 * MAXQB], BF16, kind="Internal").ap()
        self.sc_o = nc.dram_tensor("sc_o", [512, TD], BF16, kind="Internal").ap()

    def rr(self, ap):
        return ap.rearrange("(c p) n -> p c n", p=128)

    def build(self, c, sh, attw):
        nc = self.nc
        self.attnT = attw
        es = ExitStack()
        self.es = es
        sb = lambda n, sh_, dt: es.enter_context(nc.sbuf_tensor(n, sh_, dt))
        with es:
            self.c = c
            self.identb = sh["identb"]
            self.onesb = sh["onesb"]
            self.oneshh = sh["oneshh"]
            self.selT = sh["selT"]
            self.onesrow = sh["onesrow"]
            self.onesrow_f = sh["onesrow_f"]
            self.RT = sb("RT_s", [128, 128], BF16)
            self.mreg = sb("mreg_s", [128, 2 * 128], BF16)
            self.lng = sb("lng_s", [128, 16], F32)
            self.lnb = sb("lnb_s", [128, 16], F32)
            self.qs8 = sb("qs8_s", [128, 2], F32)
            self.ks1 = sb("ks1_s", [128, 2], F32)
            self.flag = sb("flag_s", [128, 1], F32)
            self.st0 = sb("st0_p2", [128, 512], F32)
            self.st1 = sb("st1_p2", [128, 512], F32)
            self.st2 = sb("st2_p2", [128, 512], F32)
            self.s_bc = sb("s_bc_p2", [128, 512], F32)
            self.sh_bc = sb("sh_bc_p2", [128, 512], F32)

            for dst, src in [
                (self.RT, self.RT_i), (self.mreg, self.mreg_i),
                (self.lng, self.lng_i), (self.lnb, self.lnb_i),
                (self.qs8, self.qs8_i), (self.ks1, self.ks_i),
                (self.flag, self.flag_i),
            ]:
                c.op("sync", lambda e, d=dst, s=src: e.dma_start(out=d[:], in_=s),
                     inc="xc", n=16)
            for eng in ("vector", "tensor", "scalar", "gpsimd"):
                c.wait(eng, "xc")

            self.stageA()
            self.attn_part(0, self.sc_xin, self.sc_xmid[0])
            self.ffn_part(0, self.sc_xmid[0], self.sc_xout[0])
            self.attn_part(1, self.sc_xout[0], self.sc_xmid[1])
            self.ffn_part(1, self.sc_xmid[1], self.sc_xout[1])
            self.final_stage()

    def stageA(self):
        nc, c = self.nc, self.c
        esa = ExitStack()
        sba = lambda n, sh, dt: esa.enter_context(nc.sbuf_tensor(n, sh, dt))
        pba = lambda n, sh, dt: esa.enter_context(nc.psum_tensor(n, sh, dt))
        woT = sba("woT", [128, 8 * 1024], BF16)
        brow = sba("browA", [1, 1024], BF16)
        at_t = [sba(f"atA{i}", [128, 8 * 512], BF16) for i in range(2)]
        xf_t = [sba(f"xfA{i}", [128, 8 * 512], F32) for i in range(2)]
        yf = sba("yfA", [128, 8 * 512], F32)
        ybf = sba("ybfA", [128, 8 * 512], BF16)
        y2b = sba("y2bA", [128, 8 * 512], BF16)
        xtl = sba("xtlA", [128, 8 * 512], F32)
        xin_t = sba("xinA", [128, 8 * 512], F32)
        self.s2_bc = esa.enter_context(nc.sbuf_tensor("s2_bcA", [128, 512], F32))
        self.sh2_bc = esa.enter_context(nc.sbuf_tensor("sh2_bcA", [128, 512], F32))
        p_y = pba("p_yA", [128, 3 * 512], F32)
        p_s1 = pba("p_s1A", [128, 512], F32)
        p_s2 = pba("p_s2A", [128, 512], F32)

        c.op("gpsimd", lambda e: e.dma_start(out=brow[:], in_=self.boutrow),
             inc="g", n=16)
        c.op("gpsimd", lambda e: e.dma_start(
            out=woT[:].rearrange("p (c n) -> p c n", n=1024),
            in_=self.woutT.rearrange("(c p) n -> p c n", p=128)), inc="g", n=16)
        c.wait("tensor", "g")

        atr = self.rr(self.attnT)
        xtr = self.rr(self.xT)
        xtld_r = self.rr(self.sc_xtld)
        xin_r = self.rr(self.sc_xin)
        st0, st1, st2 = self.st0, self.st1, self.st2

        tile_free = [0, 0]
        tile_free_t = [0, 0]
        bank_free = {}
        statA_s = 0
        statA_v = 0
        xtl_dma = [0, 0]
        xin_dma = [0, 0]
        for t in range(7):
            bf = t % 2
            dsem = "d0" if bf == 0 else "d1"
            c.wait("sync", "v", tile_free[bf])
            c.wait("sync", "t", tile_free_t[bf])
            c.op("sync", lambda e, t=t, bf=bf: e.dma_start(
                out=at_t[bf][:].rearrange("p (c n) -> p c n", n=512),
                in_=atr[:, :, t * 512:(t + 1) * 512]), inc=dsem, n=16)
            c.op("sync", lambda e, t=t, bf=bf: e.dma_start(
                out=xf_t[bf][:].rearrange("p (c n) -> p c n", n=512),
                in_=xtr[:, :, t * 512:(t + 1) * 512]), inc=dsem, n=16)
            c.wait("tensor", dsem)
            for dc in range(8):
                bank = dc % 3
                if bank_free.get(bank):
                    c.wait("tensor", "v", bank_free[bank])
                for ec in range(8):
                    c.op("tensor", lambda e, dc=dc, ec=ec, bf=bf, bank=bank: e.matmul(
                        p_y[:, bank * 512:(bank + 1) * 512],
                        woT[:, ec * 1024 + dc * 128: ec * 1024 + (dc + 1) * 128],
                        at_t[bf][:, ec * 512:(ec + 1) * 512],
                        start=(ec == 0), stop=False))
                c.op("tensor", lambda e, dc=dc, bank=bank: e.matmul(
                    p_y[:, bank * 512:(bank + 1) * 512],
                    brow[0:1, dc * 128:(dc + 1) * 128], self.onesrow[:],
                    start=False, stop=True), inc="t")
                c.wait("vector", "t")
                c.op("vector", lambda e, dc=dc, bank=bank: e.tensor_copy(
                    yf[:, dc * 512:(dc + 1) * 512],
                    p_y[:, bank * 512:(bank + 1) * 512]), inc="v")
                bank_free[bank] = c.counts["v"]
            tile_free_t[bf] = c.counts["t"]
            c.op("vector", lambda e: e.tensor_copy(ybf[:], yf[:]), inc="v")
            c.op("vector", lambda e: e.tensor_tensor(y2b[:], ybf[:], ybf[:],
                                                     ALU.mult), inc="v")
            yb_v = c.counts["v"]
            c.wait("tensor", "v", yb_v)
            c.wait("tensor", "s", statA_s)
            c.wait("tensor", "v", statA_v)
            for dc in range(8):
                c.op("tensor", lambda e, dc=dc: e.matmul(
                    p_s1[:], self.onesb[:], ybf[:, dc * 512:(dc + 1) * 512],
                    start=(dc == 0), stop=(dc == 7)))
            for dc in range(8):
                c.op("tensor", lambda e, dc=dc: e.matmul(
                    p_s2[:], self.onesb[:], y2b[:, dc * 512:(dc + 1) * 512],
                    start=(dc == 0), stop=(dc == 7)),
                    inc=("t" if dc == 7 else None))
            ln_chain(c, st0, st1, st2, self.s_bc, self.sh_bc, p_s1, p_s2)
            statA_s = c.counts["s"]
            c.wait("vector", "e0" if bf == 1 else "e1", xtl_dma[1 - bf])
            c.op("vector", lambda e: e.scalar_tensor_tensor(
                xtl[:].rearrange("p (c n) -> p c n", n=512),
                yf[:].rearrange("p (c n) -> p c n", n=512), 1.0, rep8(self.s_bc[:]),
                ALU.mult, ALU.mult), inc="v")
            c.op("vector", lambda e: e.tensor_tensor(
                xtl[:].rearrange("p (c n) -> p c n", n=512),
                xtl[:].rearrange("p (c n) -> p c n", n=512), rep8(self.sh_bc[:]),
                ALU.add), inc="v")
            c.op("vector", lambda e, bf=bf: e.tensor_tensor(
                xtl[:], xtl[:], xf_t[bf][:], ALU.add), inc="v")
            tile_free[bf] = c.counts["v"]
            esem = "e0" if bf == 0 else "e1"
            c.wait("sync", "v")
            c.op("sync", lambda e, t=t: e.dma_start(
                out=xtld_r[:, :, t * 512:(t + 1) * 512],
                in_=xtl[:].rearrange("p (c n) -> p c n", n=512)), inc=esem, n=16)
            c.op("vector", lambda e: e.tensor_copy(ybf[:], xtl[:]), inc="v")
            c.op("vector", lambda e: e.tensor_tensor(y2b[:], ybf[:], ybf[:],
                                                     ALU.mult), inc="v")
            x2_v = c.counts["v"]
            c.wait("tensor", "v", x2_v)
            c.wait("tensor", "s", statA_s)
            for dc in range(8):
                c.op("tensor", lambda e, dc=dc: e.matmul(
                    p_s1[:], self.onesb[:], ybf[:, dc * 512:(dc + 1) * 512],
                    start=(dc == 0), stop=(dc == 7)))
            for dc in range(8):
                c.op("tensor", lambda e, dc=dc: e.matmul(
                    p_s2[:], self.onesb[:], y2b[:, dc * 512:(dc + 1) * 512],
                    start=(dc == 0), stop=(dc == 7)),
                    inc=("t" if dc == 7 else None))
            ln_chain(c, st0, st1, st2, self.s2_bc, self.sh2_bc, p_s1, p_s2)
            statA_s = c.counts["s"]
            statA_v = c.counts["v"]
            c.wait("vector", "e0" if bf == 1 else "e1", xin_dma[1 - bf])
            c.op("vector", lambda e: e.scalar_tensor_tensor(
                xin_t[:].rearrange("p (c n) -> p c n", n=512),
                xtl[:].rearrange("p (c n) -> p c n", n=512), 1.0, rep8(self.s2_bc[:]),
                ALU.mult, ALU.mult), inc="v")
            c.op("vector", lambda e: e.tensor_tensor(
                xin_t[:].rearrange("p (c n) -> p c n", n=512),
                xin_t[:].rearrange("p (c n) -> p c n", n=512), rep8(self.sh2_bc[:]),
                ALU.add), inc="v")
            if t == 2:
                c.op("vector", lambda e: e.tensor_scalar(
                    xin_t[:].rearrange("p (c n) -> p c n", n=512)[:, :, 511:512],
                    xin_t[:].rearrange("p (c n) -> p c n", n=512)[:, :, 511:512],
                    self.flag[:, 0:1], None, ALU.mult), inc="v")
            c.wait("sync", "v")
            c.op("sync", lambda e, t=t: e.dma_start(
                out=xin_r[:, :, t * 512:(t + 1) * 512],
                in_=xin_t[:].rearrange("p (c n) -> p c n", n=512)), inc=esem, n=16)
            xtl_dma[bf] = c.counts[esem]
            xin_dma[bf] = c.counts[esem]
        c.wait("sync", "e0", c.counts["e0"])
        c.wait("sync", "e1", c.counts["e1"])
        c.barrier()
        esa.close()

    def attn_part(self, l, z_dram, xmid_dram):
        nc, c = self.nc, self.c
        qs_off, ks_off = QS[l], KS[l]
        Tq, Tk = TD - qs_off, TD - ks_off
        nqb = NQB[l]
        nkt = (Tk + 511) // 512
        nbnd = NBND[l]
        mb_base = 5 * sum(NBND[:l])

        e2 = ExitStack()
        sb2 = lambda n, sh, dt: e2.enter_context(nc.sbuf_tensor(f"{n}_l{l}", sh, dt))
        pb2 = lambda n, sh, dt: e2.enter_context(nc.psum_tensor(f"{n}_l{l}", sh, dt))

        wq = sb2("wq", [128, 8 * 1536], BF16)
        zb = [sb2(f"zb{i}", [128, 8 * 513], BF16) for i in range(2)]
        tbt = [sb2(f"tbt{i}", [128, 4 * 512], BF16) for i in range(2)]
        Qf = sb2("Qf", [128, 4 * Tq], BF16)
        Kf = sb2("Kf", [128, 4 * Tk], BF16)
        Vkm = sb2("Vkm", [128, (Tk // 128) * 520], BF16)
        vstage = sb2("vstage", [128, 4 * 512], BF16)
        qraw = sb2("qraw", [128, 512], BF16)
        sqb_ = sb2("sqb", [128, 512], BF16)
        rnb = sb2("rnb", [2, 512], BF16)
        qn = sb2("qn", [128, 512], BF16)
        t1f = sb2("t1f", [128, 512], F32)
        uf = sb2("uf", [128, 512], F32)
        negm2 = sb2("negm2", [2, 4 * MAXQB], F32)
        negm2b = sb2("negm2b", [2, 4 * MAXQB], BF16)
        negmrow = sb2("negmrow", [1, 8 * MAXQB], BF16)
        biasneg = sb2("biasneg", [128, 8 * MAXQB], F32)
        dqb = sb2("dqb", [128, 512], BF16)
        mstr = [sb2(f"mstr{i}", [128, 5 * 128], BF16) for i in range(2)]
        PT = [sb2(f"PTl{i}", [128, 640], BF16) for i in range(2)]
        accst = sb2("accst", [65, 8 * 128], F32)
        lrec = sb2("lrec", [1, 8 * 128], F32)
        lntmp = sb2("lntmp", [1, 8 * 128], F32)
        rb_sb = sb2("rb_sb", [64, 8 * 128], F32)
        ost = [sb2(f"ost{i}", [64, 8 * 128], BF16) for i in range(2)]

        e2p = ExitStack()
        pb2p = lambda n, sh, dt: e2p.enter_context(nc.psum_tensor(f"{n}_l{l}", sh, dt))
        p_qkv = pb2p("p_qkv", [128, 3 * 512], F32)
        p_nrm = pb2p("p_nrm", [2, 512], F32)
        p_nb = pb2p("p_nb", [128, 512], F32)
        p_rot = pb2p("p_rot", [128, 512], F32)
        p_vt2 = pb2p("p_vt2", [128, 2 * 1024], BF16)

        c.op("gpsimd", lambda e, l=l: e.dma_start(
            out=wq[:].rearrange("p (c n) -> p c n", n=1536),
            in_=self.wqkv_i[l].rearrange("(c p) n -> p c n", p=128)), inc="g", n=16)
        c.wait("tensor", "g")

        zr = self.rr(z_dram)
        tabr = self.tabs_i.rearrange("p (f n) -> p f n", f=4)
        vko = Vkm[:].rearrange("p (k h d) -> p k h d", h=8, d=65)
        c.op("vector", lambda e: e.memset(vko[:, :, :, 64:65], 1.0), inc="v")
        c.op("vector", lambda e: e.memset(negm2[:], 0.0), inc="v")

        bank_free = {}
        zb_free = [0, 0]
        vt_free = [0, 0]
        for kt in range(nkt):
            t0 = ks_off + kt * 512
            tlen = min(512, TD - t0)
            bf = kt % 2
            gsem = "g0" if bf == 0 else "g1"
            c.wait("gpsimd", "v", zb_free[bf])
            c.op("gpsimd", lambda e, t0=t0, tlen=tlen, bf=bf: e.dma_start(
                out=zb[bf][:].rearrange("p (c n) -> p c n", n=513)[:, :, 0:tlen + 1],
                in_=zr[:, :, t0 - 1: t0 + tlen]), inc=gsem, n=16)
            c.op("gpsimd", lambda e, t0=t0, tlen=tlen, bf=bf: e.dma_start(
                out=tbt[bf][:].rearrange("p (f n) -> p f n", f=4)[:, :, 0:tlen],
                in_=tabr[:, :, t0: t0 + tlen]), inc=gsem, n=16)
            c.wait("tensor", gsem)
            for ec in range(12):
                bank = ec % 3
                if bank_free.get(bank):
                    c.wait("tensor", "v", bank_free[bank])
                for dc in range(8):
                    off = 1 if dc < 4 else 0
                    c.op("tensor", lambda e, ec=ec, dc=dc, bf=bf, bank=bank, off=off,
                         tlen=tlen: e.matmul(
                        p_qkv[:, bank * 512: bank * 512 + tlen],
                        wq[:, dc * 1536 + ec * 128: dc * 1536 + (ec + 1) * 128],
                        zb[bf][:, dc * 513 + off: dc * 513 + off + tlen],
                        start=(dc == 0), stop=(dc == 7)),
                        inc=("t" if dc == 7 else None))
                mm_t = c.counts["t"]
                c.wait("vector", "t", mm_t)
                ch = ec % 4
                pslice = p_qkv[:, bank * 512: bank * 512 + tlen]
                if ec >= 8:
                    c.op("vector", lambda e, ps=pslice, ch=ch, tlen=tlen: e.tensor_copy(
                        vstage[:, ch * 512: ch * 512 + tlen], ps), inc="v")
                    bank_free[bank] = c.counts["v"]
                    continue
                isq = ec < 4
                doff = (t0 - qs_off) if isq else (t0 - ks_off)
                c.op("vector", lambda e, ps=pslice, tlen=tlen: e.tensor_copy(
                    qraw[:, 0:tlen], ps), inc="v")
                bank_free[bank] = c.counts["v"]
                if isq and doff < 0:
                    continue
                c.op("vector", lambda e, tlen=tlen: e.tensor_tensor(
                    sqb_[:, 0:tlen], qraw[:, 0:tlen], qraw[:, 0:tlen], ALU.mult),
                    inc="v")
                sq_v = c.counts["v"]
                c.wait("tensor", "v", sq_v)
                c.op("tensor", lambda e, tlen=tlen: e.matmul(
                    p_nrm[:, 0:tlen], self.oneshh[:, 0:2], sqb_[:, 0:tlen],
                    start=True, stop=True), inc="t")
                c.wait("scalar", "t")
                c.op("scalar", lambda e, tlen=tlen: e.activation(
                    self.st0[0:2, 0:tlen], p_nrm[:, 0:tlen], AF.Ln, bias=1e-30),
                    inc="s")
                c.op("scalar", lambda e, tlen=tlen: e.activation(
                    rnb[:, 0:tlen], self.st0[0:2, 0:tlen], AF.Exp, scale=-0.5),
                    inc="s")
                c.wait("tensor", "s")
                c.op("tensor", lambda e, tlen=tlen: e.matmul(
                    p_nb[:, 0:tlen], self.selT[:, :], rnb[:, 0:tlen],
                    start=True, stop=True), inc="t")
                nb_t = c.counts["t"]
                c.wait("vector", "t", nb_t)
                sc_ap = self.qs8 if isq else self.ks1
                c.op("vector", lambda e, tlen=tlen, sc_ap=sc_ap, l=l: e.scalar_tensor_tensor(
                    qn[:, 0:tlen], qraw[:, 0:tlen], sc_ap[:, l:l + 1], p_nb[:, 0:tlen],
                    ALU.mult, ALU.mult), inc="v")
                qn_v = c.counts["v"]
                c.wait("tensor", "v", qn_v)
                c.op("tensor", lambda e, tlen=tlen: e.matmul(
                    p_rot[:, 0:tlen], self.RT[:, :], qn[:, 0:tlen],
                    start=True, stop=True), inc="t")
                rot_t = c.counts["t"]
                ci = 0 if isq else 2
                si = 1 if isq else 3
                c.wait("vector", "g0" if bf == 0 else "g1")
                c.op("vector", lambda e, bf=bf, ci=ci, tlen=tlen: e.tensor_tensor(
                    t1f[:, 0:tlen], qn[:, 0:tlen],
                    tbt[bf][:, ci * 512: ci * 512 + tlen], ALU.mult), inc="v")
                c.wait("vector", "t", rot_t)
                c.op("vector", lambda e, bf=bf, si=si, tlen=tlen: e.scalar_tensor_tensor(
                    uf[:, 0:tlen], p_rot[:, 0:tlen], 1.0,
                    tbt[bf][:, si * 512: si * 512 + tlen], ALU.mult, ALU.mult),
                    inc="v")
                dst, dT = (Qf, Tq) if isq else (Kf, Tk)
                c.op("vector", lambda e, ch=ch, dst=dst, dT=dT, doff=doff,
                     tlen=tlen: e.tensor_tensor(
                    dst[:, ch * dT + doff: ch * dT + doff + tlen],
                    t1f[:, 0:tlen], uf[:, 0:tlen], ALU.add), inc="v")
            zb_free[bf] = c.counts["v"]
            vs_v = c.counts["v"]
            c.wait("tensor", "v", vs_v)
            nsc = (tlen + 127) // 128
            for ch in range(4):
                for scn in range(nsc):
                    tbuf = (ch * nsc + scn) % 2
                    c.wait("tensor", "v", vt_free[tbuf])
                    c.op("tensor", lambda e, ch=ch, scn=scn, tbuf=tbuf: e.transpose(
                        p_vt2[:, tbuf * 1024: tbuf * 1024 + 128],
                        vstage[:, ch * 512 + scn * 128: ch * 512 + (scn + 1) * 128],
                        self.identb[:]), inc="t")
                    tr_t = c.counts["t"]
                    cc = kt * 4 + scn
                    c.wait("vector", "t", tr_t)
                    c.op("vector", lambda e, cc=cc, ch=ch, tbuf=tbuf: e.tensor_copy(
                        Vkm[:, cc * 520 + ch * 130: cc * 520 + (ch + 1) * 130]
                        .rearrange("p (h d) -> p h d", d=65)[:, :, 0:64],
                        p_vt2[:, tbuf * 1024: tbuf * 1024 + 128]
                        .rearrange("p (h d) -> p h d", d=64)), inc="v")
                    vt_free[tbuf] = c.counts["v"]

        ndt = (Tq + 511) // 512
        for ch in range(4):
            for dt_ in range(ndt):
                doff = dt_ * 512
                dlen = min(512, Tq - doff)
                nb = dlen // 128
                c.op("vector", lambda e, ch=ch, doff=doff, dlen=dlen: e.tensor_tensor(
                    dqb[:, 0:dlen], Qf[:, ch * Tq + doff: ch * Tq + doff + dlen],
                    Kf[:, ch * Tk + 512 + doff: ch * Tk + 512 + doff + dlen],
                    ALU.mult), inc="v")
                dq_v = c.counts["v"]
                c.wait("tensor", "v", dq_v)
                c.op("tensor", lambda e, dlen=dlen: e.matmul(
                    p_nrm[:, 0:dlen], self.oneshh[:, 0:2], dqb[:, 0:dlen],
                    start=True, stop=True), inc="t")
                mm_t = c.counts["t"]
                c.wait("vector", "t", mm_t)
                c.op("vector", lambda e, ch=ch, dt_=dt_, nb=nb: e.reduce_max(
                    negm2[0:2, ch * MAXQB + dt_ * 4: ch * MAXQB + dt_ * 4 + nb],
                    p_nrm[:, 0:nb * 128].rearrange("p (b n) -> p b n", n=128),
                    axis=AX.X), inc="v")
                c.wait("tensor", "v")
        c.op("vector", lambda e: e.tensor_scalar(
            negm2b[:], negm2[:], -1.0, None, ALU.mult), inc="v")
        c.wait("gpsimd", "v")
        c.op("gpsimd", lambda e: e.dma_start(
            out=self.negm_dram.rearrange("o (r pq) -> (o r) pq", r=2),
            in_=negm2b[:]), inc="g", n=16)
        c.wait("gpsimd", "g")
        c.op("gpsimd", lambda e: e.dma_start(
            out=negmrow[:].rearrange("o (p r q) -> o p r q", p=4, r=2),
            in_=self.negm_dram.rearrange("o (r p q) -> o p r q", r=2, p=4)),
            inc="g", n=16)
        c.wait("tensor", "g")
        c.op("tensor", lambda e: e.matmul(
            p_nb[:, 0:448], self.onesrow[0:1, 0:128], negmrow[0:1, 0:448],
            start=True, stop=True), inc="t")
        bt = c.counts["t"]
        c.wait("vector", "t", bt)
        c.op("vector", lambda e: e.tensor_copy(biasneg[:, 0:448], p_nb[:, 0:448]),
             inc="v")
        c.wait("tensor", "v")

        c.barrier()
        e2p.close()
        e3 = ExitStack()
        p_sml = [e3.enter_context(nc.psum_tensor(f"p_sml{i}_l{l}", [128, 640], F32))
                 for i in range(2)]
        p_acc = e3.enter_context(nc.psum_tensor(f"p_accl_l{l}", [65, 128], F32))
        p_rb = e3.enter_context(nc.psum_tensor(f"p_rbl_l{l}", [64, 1024], F32))

        sim_free = [0, 0]
        acc_free = 0
        rb_free = 0
        mb_free = [0, 0]
        ost_free = [0, 0]
        for qb in range(nqb):
            qoff = qb * 128
            mb = qb % 2
            if qb < nbnd:
                dsem = "d0" if mb == 0 else "d1"
                c.wait("sync", "t", mb_free[mb])
                c.op("sync", lambda e, l=l, qb=qb, mb=mb: e.dma_start(
                    out=mstr[mb][:].rearrange("p (k n) -> p k n", k=5),
                    in_=self.mbnd_i[mb_base + qb * 5: mb_base + (qb + 1) * 5]
                    .rearrange("k p n -> p k n")), inc=dsem, n=16)
                c.wait("tensor", dsem)
            for h in range(H):
                ch, rb = h // 2, 64 * (h % 2)
                jb = (qb * H + h) % 2
                c.wait("tensor", "s", sim_free[jb])
                simops = []
                for ko in range(5):
                    koff = qoff + ko * 128
                    masked = qb < nbnd or ko == 0 or ko == 4
                    simops.append(lambda e, jb=jb, ko=ko, ch=ch, rb=rb, koff=koff,
                                  qoff=qoff, masked=masked: e.matmul(
                        p_sml[jb][:, ko * 128:(ko + 1) * 128],
                        Kf[rb:rb + 64, ch * Tk + koff: ch * Tk + koff + 128],
                        Qf[rb:rb + 64, ch * Tq + qoff: ch * Tq + qoff + 128],
                        start=True, stop=(not masked)))
                    if qb < nbnd:
                        simops.append(lambda e, jb=jb, ko=ko, mb=mb: e.matmul(
                            p_sml[jb][:, ko * 128:(ko + 1) * 128],
                            self.identb[:], mstr[mb][:, ko * 128:(ko + 1) * 128],
                            start=False, stop=True))
                    elif ko == 0 or ko == 4:
                        mi = 0 if ko == 0 else 1
                        simops.append(lambda e, jb=jb, ko=ko, mi=mi: e.matmul(
                            p_sml[jb][:, ko * 128:(ko + 1) * 128],
                            self.identb[:], self.mreg[:, mi * 128:(mi + 1) * 128],
                            start=False, stop=True))
                for fi, f in enumerate(simops):
                    c.op("tensor", f, inc=("t" if fi == len(simops) - 1 else None))
                sim_t = c.counts["t"]
                if qb < nbnd:
                    mb_free[mb] = sim_t
                c.wait("scalar", "t", sim_t)
                c.op("scalar", lambda e, jb=jb, h=h, qb=qb: e.activation(
                    PT[jb][:], p_sml[jb][:], AF.Exp,
                    bias=biasneg[:, h * MAXQB + qb: h * MAXQB + qb + 1],
                    scale=1.0), inc="s")
                exp_s = c.counts["s"]
                sim_free[jb] = exp_s
                c.wait("tensor", "s", exp_s)
                c.wait("tensor", "v", acc_free)
                for ko in range(5):
                    kc = qb + ko
                    c.op("tensor", lambda e, jb=jb, ko=ko, h=h, kc=kc,
                         st=(ko == 0), sp=(ko == 4): e.matmul(
                        p_acc[:], Vkm[:, kc * 520 + h * 65: kc * 520 + h * 65 + 65],
                        PT[jb][:, ko * 128:(ko + 1) * 128],
                        start=st, stop=sp), inc=("t" if ko == 4 else None))
                av_t = c.counts["t"]
                c.wait("vector", "t", av_t)
                c.op("vector", lambda e, h=h: e.tensor_copy(
                    accst[:, h * 128:(h + 1) * 128], p_acc[:]), inc="v")
                acc_free = c.counts["v"]
            c.op("vector", lambda e: e.tensor_copy(
                lrec[:], accst[64:65, :]), inc="v")
            lr_v = c.counts["v"]
            c.wait("scalar", "v", lr_v)
            c.op("scalar", lambda e: e.activation(lntmp[:], lrec[:], AF.Ln), inc="s")
            c.op("scalar", lambda e: e.activation(lrec[:], lntmp[:], AF.Exp,
                                                  scale=-1.0), inc="s")
            rl_s = c.counts["s"]
            c.wait("tensor", "s", rl_s)
            c.wait("tensor", "v", rb_free)
            c.op("tensor", lambda e: e.matmul(
                p_rb[:, 0:512], self.onesrow_f[:], lrec[0:1, 0:512],
                start=True, stop=True))
            c.op("tensor", lambda e: e.matmul(
                p_rb[:, 512:1024], self.onesrow_f[:], lrec[0:1, 512:1024],
                start=True, stop=True), inc="t")
            bc_t = c.counts["t"]
            c.wait("vector", "t", bc_t)
            c.op("vector", lambda e: e.tensor_copy(rb_sb[:], p_rb[:]), inc="v")
            rb_free = c.counts["v"]
            ob2 = qb % 2
            c.wait("vector", "e0" if ob2 == 0 else "e1", ost_free[ob2])
            c.op("vector", lambda e, ob2=ob2: e.scalar_tensor_tensor(
                ost[ob2][0:64, :].rearrange("p (h n) -> p h n", h=8),
                accst[0:64, :].rearrange("p (h n) -> p h n", h=8),
                1.0,
                rb_sb[:].rearrange("p (h n) -> p h n", h=8),
                ALU.mult, ALU.mult), inc="v")
            acc_free = c.counts["v"]
            c.wait("sync", "v")
            esm = "e0" if ob2 == 0 else "e1"
            c.op("sync", lambda e, qoff=qoff, ob2=ob2: e.dma_start(
                out=self.sc_o.rearrange("(h d) n -> d h n", d=64)[
                    :, :, qs_off + qoff: qs_off + qoff + 128],
                in_=ost[ob2][0:64, :].rearrange("p (h n) -> p h n", h=8)),
                inc=esm, n=16)
            ost_free[ob2] = c.counts[esm]

        c.wait("sync", "e0", c.counts["e0"])
        c.wait("sync", "e1", c.counts["e1"])
        c.barrier()
        e3.close()

        e2.close()

        e4 = ExitStack()
        sb4 = lambda n, sh, dt: e4.enter_context(nc.sbuf_tensor(f"{n}_w{l}", sh, dt))
        zq = sb4("zq", [128, 8 * 512], F32)
        otile = sb4("otile", [128, 4 * 512], BF16)
        xmo = sb4("xmo", [128, 8 * 512], F32)
        woB = sb4("woB", [128, 4 * 1024], BF16)
        p_wo = e4.enter_context(nc.psum_tensor(f"p_wo_l{l}", [128, 3 * 512], F32))
        c.op("gpsimd", lambda e, l=l: e.dma_start(
            out=woB[:].rearrange("p (c n) -> p c n", n=1024),
            in_=self.wo_i[l].rearrange("(c p) n -> p c n", p=128)), inc="g", n=16)
        c.wait("tensor", "g")
        wo_bank_free = {}
        zq_free = 0
        ot_free = 0
        xmo_dma = [0, 0]
        nwt = (Tq + 511) // 512
        flag_tile = (1535 - qs_off) // 512
        flag_col = (1535 - qs_off) % 512
        o_pk = self.sc_o.rearrange("(c p) n -> p c n", p=128)
        for wt in range(nwt):
            woff = wt * 512
            wlen = min(512, Tq - woff)
            c.wait("sync", "v", zq_free)
            c.op("sync", lambda e, woff=woff, wlen=wlen: e.dma_start(
                out=zq[:].rearrange("p (c n) -> p c n", n=512)[:, :, 0:wlen],
                in_=zr[:, :, qs_off + woff: qs_off + woff + wlen]), inc="d0", n=16)
            c.wait("sync", "t", ot_free)
            c.op("sync", lambda e, woff=woff, wlen=wlen: e.dma_start(
                out=otile[:].rearrange("p (c n) -> p c n", n=512)[:, :, 0:wlen],
                in_=o_pk[:, :, qs_off + woff: qs_off + woff + wlen]), inc="d1", n=16)
            c.wait("vector", "d0")
            c.wait("tensor", "d0")
            c.wait("tensor", "d1")
            c.wait("vector", "e0" if wt % 2 == 1 else "e1", xmo_dma[1 - wt % 2])
            for dc in range(8):
                bank = dc % 3
                if wo_bank_free.get(bank):
                    c.wait("tensor", "v", wo_bank_free[bank])
                for ch in range(4):
                    c.op("tensor", lambda e, dc=dc, ch=ch, wlen=wlen,
                         bank=bank: e.matmul(
                        p_wo[:, bank * 512: bank * 512 + wlen],
                        woB[:, ch * 1024 + dc * 128: ch * 1024 + (dc + 1) * 128],
                        otile[:, ch * 512: ch * 512 + wlen],
                        start=(ch == 0), stop=(ch == 3)),
                        inc=("t" if ch == 3 else None))
                mm_t = c.counts["t"]
                c.wait("vector", "t", mm_t)
                c.op("vector", lambda e, dc=dc, bank=bank, wlen=wlen: e.scalar_tensor_tensor(
                    xmo[:, dc * 512: dc * 512 + wlen],
                    p_wo[:, bank * 512: bank * 512 + wlen], 1.0,
                    zq[:, dc * 512: dc * 512 + wlen], ALU.mult, ALU.add), inc="v")
                wo_bank_free[bank] = c.counts["v"]
            zq_free = c.counts["v"]
            ot_free = c.counts["t"]
            if wt == flag_tile:
                c.op("vector", lambda e, flag_col=flag_col: e.tensor_scalar(
                    xmo[:].rearrange("p (c n) -> p c n", n=512)[:, :, flag_col:flag_col + 1],
                    xmo[:].rearrange("p (c n) -> p c n", n=512)[:, :, flag_col:flag_col + 1],
                    self.flag[:, 0:1], None, ALU.mult), inc="v")
            c.wait("sync", "v")
            esem = "e0" if wt % 2 == 0 else "e1"
            c.op("sync", lambda e, woff=woff, wlen=wlen: e.dma_start(
                out=self.rr(xmid_dram)[:, :, qs_off + woff: qs_off + woff + wlen],
                in_=xmo[:].rearrange("p (c n) -> p c n", n=512)[:, :, 0:wlen]),
                inc=esem, n=16)
            xmo_dma[wt % 2] = c.counts[esem]
        c.wait("sync", "e0", c.counts["e0"])
        c.wait("sync", "e1", c.counts["e1"])
        c.barrier()
        e4.close()

    def ffn_part(self, l, xmid_dram, xout_dram):
        nc, c = self.nc, self.c
        cs_off = CS[l]
        Tc = TD - cs_off
        nt = (Tc + 511) // 512

        e2 = ExitStack()
        sb2 = lambda n, sh, dt: e2.enter_context(nc.sbuf_tensor(f"{n}_f{l}", sh, dt))
        pb2 = lambda n, sh, dt: e2.enter_context(nc.psum_tensor(f"{n}_f{l}", sh, dt))
        w1 = sb2("w1", [128, 8 * 5632], BF16)
        w2 = sb2("w2", [128, 22 * 1024], BF16)
        b1row = sb2("b1row", [1, 5632], BF16)
        b2row = sb2("b2row", [1, 1024], BF16)
        xmb = sb2("xmb", [128, 8 * 513], BF16)
        xmf = sb2("xmf", [128, 512], F32)
        h2b = sb2("h2b", [128, 512], BF16)
        xnb = sb2("xnb", [128, 8 * 512], BF16)
        gbuf = sb2("gbuf", [128, 512], BF16)
        h4 = sb2("h4", [128, 22 * 512], BF16)
        xob = sb2("xob", [128, 512], F32)
        shg = sb2("shg", [128, 512], F32)
        p_a = pb2("p_a", [128, 2 * 512], F32)
        p_g = pb2("p_g", [128, 2 * 512], F32)
        p_s1 = pb2("p_s1f", [128, 512], F32)
        p_s2 = pb2("p_s2f", [128, 512], F32)
        p_o = pb2("p_of", [128, 2 * 512], F32)

        c.op("gpsimd", lambda e, l=l: e.dma_start(out=b1row[:], in_=self.b1_i[l]),
             inc="g", n=16)
        c.op("gpsimd", lambda e, l=l: e.dma_start(out=b2row[:], in_=self.b2_i[l]),
             inc="g", n=16)
        c.op("gpsimd", lambda e, l=l: e.dma_start(
            out=w1[:].rearrange("p (c n) -> p c n", n=5632),
            in_=self.w1_i[l].rearrange("(c p) n -> p c n", p=128)), inc="g", n=16)
        c.op("gpsimd", lambda e, l=l: e.dma_start(
            out=w2[:].rearrange("p (c n) -> p c n", n=1024),
            in_=self.w2_i[l].rearrange("(c p) n -> p c n", p=128)), inc="g", n=16)
        c.wait("tensor", "g")

        xmr = self.rr(xmid_dram)
        xor_ = self.rr(xout_dram)
        st0, st1, st2 = self.st0, self.st1, self.st2
        g_ = self.lng
        b_ = self.lnb

        tile_free = 0
        tile_free_t = 0
        stat_s = 0
        stat_v = 0
        ab_free = [0, 0]
        gb_free = [0, 0]
        o_free = {}
        xmf_free = [0, 0]
        xob_dma = [0, 0]
        last_h4_v = 0
        for t in range(nt):
            toff = cs_off + t * 512
            tlen = min(512, TD - toff)
            c.wait("gpsimd", "v", tile_free)
            c.wait("gpsimd", "t", tile_free_t)
            c.op("gpsimd", lambda e, toff=toff, tlen=tlen: e.dma_start(
                out=xmb[:].rearrange("p (c n) -> p c n", n=513)[:, :, 0:tlen + 1],
                in_=xmr[:, :, toff - 1: toff + tlen]), inc="g0", n=16)
            c.wait("vector", "g0")
            c.wait("tensor", "g0")
            c.wait("tensor", "s", stat_s)
            c.wait("tensor", "v", stat_v)
            for dc in range(8):
                off = 1 if dc < 4 else 0
                c.op("tensor", lambda e, dc=dc, off=off, tlen=tlen: e.matmul(
                    p_s1[:, 0:tlen], self.onesb[:],
                    xmb[:, dc * 513 + off: dc * 513 + off + tlen],
                    start=(dc == 0), stop=(dc == 7)), inc=("t" if dc == 7 else None))
            for dc in range(8):
                off = 1 if dc < 4 else 0
                if dc > 0:
                    c.wait("vector", "t")
                c.op("vector", lambda e, dc=dc, off=off, tlen=tlen: e.tensor_tensor(
                    h2b[:, 0:tlen],
                    xmb[:, dc * 513 + off: dc * 513 + off + tlen],
                    xmb[:, dc * 513 + off: dc * 513 + off + tlen], ALU.mult), inc="v")
                h2_v = c.counts["v"]
                c.wait("tensor", "v", h2_v)
                c.op("tensor", lambda e, dc=dc, tlen=tlen: e.matmul(
                    p_s2[:, 0:tlen], self.onesb[:], h2b[:, 0:tlen],
                    start=(dc == 0), stop=(dc == 7)), inc="t")
            ln_chain(c, st0, st1, st2, self.s_bc, self.sh_bc, p_s1, p_s2)
            stat_s = c.counts["s"]
            for dc in range(8):
                off = 1 if dc < 4 else 0
                c.op("vector", lambda e, dc=dc, l=l: e.tensor_scalar(
                    shg[:], self.sh_bc[:],
                    g_[:, l * 8 + dc: l * 8 + dc + 1],
                    b_[:, l * 8 + dc: l * 8 + dc + 1],
                    ALU.mult, ALU.add), inc="v")
                c.op("vector", lambda e, dc=dc, off=off, l=l: e.scalar_tensor_tensor(
                    xnb[:, dc * 512:(dc + 1) * 512],
                    xmb[:, dc * 513 + off: dc * 513 + off + 512],
                    g_[:, l * 8 + dc: l * 8 + dc + 1], self.s_bc[:],
                    ALU.mult, ALU.mult), inc="v")
                c.op("vector", lambda e, dc=dc: e.tensor_tensor(
                    xnb[:, dc * 512:(dc + 1) * 512], xnb[:, dc * 512:(dc + 1) * 512],
                    shg[:], ALU.add), inc="v")
            xn_v = c.counts["v"]
            stat_v = xn_v
            tile_free = c.counts["v"]
            tile_free_t = c.counts["t"]
            c.wait("tensor", "v", xn_v)
            for pr in range(22):
                pa = pr % 2
                if ab_free[pa]:
                    c.wait("tensor", "s", gb_free[pa])
                    c.wait("tensor", "v", ab_free[pa])
                for dc in range(8):
                    c.op("tensor", lambda e, pr=pr, dc=dc, pa=pa, tlen=tlen: e.matmul(
                        p_a[:, pa * 512: pa * 512 + tlen],
                        w1[:, dc * 5632 + pr * 256: dc * 5632 + pr * 256 + 128],
                        xnb[:, dc * 512: dc * 512 + tlen],
                        start=(dc == 0), stop=False))
                c.op("tensor", lambda e, pr=pr, pa=pa, tlen=tlen: e.matmul(
                    p_a[:, pa * 512: pa * 512 + tlen],
                    b1row[0:1, pr * 256: pr * 256 + 128], self.onesrow[0:1, 0:tlen],
                    start=False, stop=True))
                for dc in range(8):
                    c.op("tensor", lambda e, pr=pr, dc=dc, pa=pa, tlen=tlen: e.matmul(
                        p_g[:, pa * 512: pa * 512 + tlen],
                        w1[:, dc * 5632 + pr * 256 + 128: dc * 5632 + pr * 256 + 256],
                        xnb[:, dc * 512: dc * 512 + tlen],
                        start=(dc == 0), stop=False))
                c.op("tensor", lambda e, pr=pr, pa=pa, tlen=tlen: e.matmul(
                    p_g[:, pa * 512: pa * 512 + tlen],
                    b1row[0:1, pr * 256 + 128: pr * 256 + 256],
                    self.onesrow[0:1, 0:tlen],
                    start=False, stop=True), inc="t")
                mm_t = c.counts["t"]
                c.wait("scalar", "t", mm_t)
                c.wait("scalar", "v", last_h4_v)
                c.op("scalar", lambda e, pa=pa, tlen=tlen: e.activation(
                    gbuf[:, 0:tlen], p_g[:, pa * 512: pa * 512 + tlen], AF.Gelu),
                    inc="s")
                gb_free[pa] = c.counts["s"]
                gl_s = c.counts["s"]
                c.wait("vector", "s", gl_s)
                c.op("vector", lambda e, pr=pr, pa=pa, tlen=tlen: e.scalar_tensor_tensor(
                    h4[:, pr * 512: pr * 512 + tlen],
                    p_a[:, pa * 512: pa * 512 + tlen], 1.0, gbuf[:, 0:tlen],
                    ALU.mult, ALU.mult), inc="v")
                ab_free[pa] = c.counts["v"]
                last_h4_v = c.counts["v"]
            h4_v = c.counts["v"]
            h4_v = c.counts["v"]
            c.wait("tensor", "v", h4_v)
            for dc in range(8):
                ob = dc % 2
                xs_sem = "e0" if ob == 0 else "e1"
                c.wait("sync", "v", xmf_free[0])
                c.op("sync", lambda e, dc=dc, toff=toff, tlen=tlen: e.dma_start(
                    out=xmf[:, 0:tlen],
                    in_=xmr[:, dc, toff: toff + tlen]), inc="d1", n=16)
                if o_free.get(ob):
                    c.wait("tensor", "v", o_free[ob])
                for kc in range(22):
                    c.op("tensor", lambda e, dc=dc, kc=kc, ob=ob, tlen=tlen: e.matmul(
                        p_o[:, ob * 512: ob * 512 + tlen],
                        w2[:, kc * 1024 + dc * 128: kc * 1024 + (dc + 1) * 128],
                        h4[:, kc * 512: kc * 512 + tlen],
                        start=(kc == 0), stop=False))
                c.op("tensor", lambda e, dc=dc, ob=ob, tlen=tlen: e.matmul(
                    p_o[:, ob * 512: ob * 512 + tlen],
                    b2row[0:1, dc * 128:(dc + 1) * 128], self.onesrow[0:1, 0:tlen],
                    start=False, stop=True), inc="t")
                mm_t = c.counts["t"]
                c.wait("vector", "t", mm_t)
                c.wait("vector", "d1")
                c.wait("vector", "e0" if ob == 1 else "e1", xob_dma[1 - ob])
                c.op("vector", lambda e, dc=dc, ob=ob, tlen=tlen: e.scalar_tensor_tensor(
                    xob[:, 0:tlen],
                    p_o[:, ob * 512: ob * 512 + tlen], 1.0,
                    xmf[:, 0:tlen], ALU.mult, ALU.add),
                    inc="v")
                xmf_free[0] = c.counts["v"]
                o_free[ob] = c.counts["v"]
                if toff <= 1535 < toff + tlen:
                    fc = 1535 - toff
                    c.op("vector", lambda e, fc=fc: e.tensor_scalar(
                        xob[:, fc: fc + 1], xob[:, fc: fc + 1],
                        self.flag[:, 0:1], None, ALU.mult), inc="v")
                    xmf_free[0] = c.counts["v"]
                    o_free[ob] = c.counts["v"]
                c.wait("sync", "v")
                c.op("sync", lambda e, dc=dc, toff=toff, tlen=tlen, xs_sem=xs_sem: e.dma_start(
                    out=xor_[:, dc, toff: toff + tlen],
                    in_=xob[:, 0:tlen]), inc=xs_sem, n=16)
                xob_dma[ob] = c.counts[xs_sem]
        c.wait("sync", "e0", c.counts["e0"])
        c.wait("sync", "e1", c.counts["e1"])
        c.barrier()
        e2.close()

    def final_stage(self):
        nc, c = self.nc, self.c
        e2 = ExitStack()
        sb2 = lambda n, sh, dt: e2.enter_context(nc.sbuf_tensor(n, sh, dt))
        pb2 = lambda n, sh, dt: e2.enter_context(nc.psum_tensor(n, sh, dt))
        xo_t = [sb2(f"xoF{i}", [128, 8 * 512], F32) for i in range(2)]
        xt_t = [sb2(f"xtF{i}", [128, 8 * 512], F32) for i in range(2)]
        xb_ = sb2("xbF", [128, 8 * 512], BF16)
        x2_ = sb2("x2F", [128, 8 * 512], BF16)
        fo = sb2("foF", [128, 8 * 512], F32)
        p_s1 = pb2("p_s1F", [128, 512], F32)
        p_s2 = pb2("p_s2F", [128, 512], F32)
        st0, st1, st2 = self.st0, self.st1, self.st2
        xout_r = self.rr(self.sc_xout[1])
        xtld_r = self.rr(self.sc_xtld)
        out_r = self.outT.rearrange("(c p) n -> p c n", p=128)
        tile_free = [0, 0]
        stat_s = 0
        stat_v = 0
        fo_dma = [0, 0]
        for t in range(4):
            toff = 1536 + t * 512
            bf = t % 2
            dsem = "d0" if bf == 0 else "d1"
            c.wait("sync", "v", tile_free[bf])
            c.op("sync", lambda e, toff=toff, bf=bf: e.dma_start(
                out=xo_t[bf][:].rearrange("p (c n) -> p c n", n=512),
                in_=xout_r[:, :, toff: toff + 512]), inc=dsem, n=16)
            c.op("sync", lambda e, toff=toff, bf=bf: e.dma_start(
                out=xt_t[bf][:].rearrange("p (c n) -> p c n", n=512),
                in_=xtld_r[:, :, toff: toff + 512]), inc=dsem, n=16)
            c.wait("vector", dsem)
            c.op("vector", lambda e, bf=bf: e.tensor_copy(xb_[:], xo_t[bf][:]), inc="v")
            c.op("vector", lambda e: e.tensor_tensor(x2_[:], xb_[:], xb_[:],
                                                     ALU.mult), inc="v")
            x2_v = c.counts["v"]
            c.wait("tensor", "v", x2_v)
            c.wait("tensor", "s", stat_s)
            c.wait("tensor", "v", stat_v)
            for dc in range(8):
                c.op("tensor", lambda e, dc=dc: e.matmul(
                    p_s1[:], self.onesb[:], xb_[:, dc * 512:(dc + 1) * 512],
                    start=(dc == 0), stop=(dc == 7)))
            for dc in range(8):
                c.op("tensor", lambda e, dc=dc: e.matmul(
                    p_s2[:], self.onesb[:], x2_[:, dc * 512:(dc + 1) * 512],
                    start=(dc == 0), stop=(dc == 7)),
                    inc=("t" if dc == 7 else None))
            ln_chain(c, st0, st1, st2, self.s_bc, self.sh_bc, p_s1, p_s2)
            stat_s = c.counts["s"]
            stat_v = c.counts["v"]
            c.wait("vector", "o0" if bf == 1 else "o1", fo_dma[1 - bf])
            c.op("vector", lambda e, bf=bf: e.scalar_tensor_tensor(
                fo[:].rearrange("p (c n) -> p c n", n=512),
                xo_t[bf][:].rearrange("p (c n) -> p c n", n=512), 1.0,
                rep8(self.s_bc[:]), ALU.mult, ALU.mult), inc="v")
            c.op("vector", lambda e: e.tensor_tensor(
                fo[:].rearrange("p (c n) -> p c n", n=512),
                fo[:].rearrange("p (c n) -> p c n", n=512), rep8(self.sh_bc[:]),
                ALU.add), inc="v")
            c.op("vector", lambda e, bf=bf: e.tensor_tensor(
                fo[:], fo[:], xt_t[bf][:], ALU.add), inc="v")
            tile_free[bf] = c.counts["v"]
            c.wait("sync", "v")
            osem = "o0" if bf == 0 else "o1"
            c.op("sync", lambda e, t=t: e.dma_start(
                out=out_r[:, :, t * 512:(t + 1) * 512],
                in_=fo[:].rearrange("p (c n) -> p c n", n=512)), inc=osem, n=16)
            fo_dma[bf] = c.counts[osem]
        e2.close()


def build_fused():
    nc = bass.Bass("TRN2", target_bir_lowering=False, debug=False)
    es = ExitStack()
    sb = lambda n, shp, dt: es.enter_context(nc.sbuf_tensor(n, shp, dt))
    di = lambda n, shp, dt: nc.dram_tensor(n, shp, dt, kind="ExternalInput").ap()
    with es:
        c = Ctx(nc)
        for s in ("xc", "xw", "d0", "d1", "e0", "e1", "g", "g0", "g1", "t",
                  "s", "v", "o0", "o1", "cc", "cp"):
            c.sem(es, s)

        identb_i = di("identb", [128, 128], BF16)
        onesb_i = di("onesb", [128, 128], BF16)
        oneshh_i = di("oneshh", [128, 2], BF16)
        selT_i = di("selT", [2, 128], BF16)
        onesrow_i = di("onesrow", [1, 512], BF16)
        onesrowf_i = di("onesrow_f", [1, 64], F32)
        csts_i = di("csts", [128, 8], F32)

        sh = {
            "identb": sb("identb_s", [128, 128], BF16),
            "onesb": sb("onesb_s", [128, 128], BF16),
            "oneshh": sb("oneshh_s", [128, 2], BF16),
            "selT": sb("selT_s", [2, 128], BF16),
            "onesrow": sb("onesrow_s", [1, 512], BF16),
            "onesrow_f": sb("onesrowf_s", [1, 64], F32),
        }
        cst = sb("cst_s", [128, 8], F32)
        zf = sb("zf1536", [128, 1536], BF16)
        nc.const_aps.register(F32, 1.0 / 1024.0, cst[:, 0:1])
        nc.const_aps.register(F32, 1e-5, cst[:, 1:2])
        nc.const_aps.register(F32, -0.5, cst[:, 2:3])
        nc.const_aps.register(F32, -1.0, cst[:, 3:4])
        nc.const_aps.register(F32, 1e-30, cst[:, 4:5])
        for dst, src in [(sh["identb"], identb_i), (sh["onesb"], onesb_i),
                         (sh["oneshh"], oneshh_i), (sh["selT"], selT_i),
                         (sh["onesrow"], onesrow_i),
                         (sh["onesrow_f"], onesrowf_i), (cst, csts_i)]:
            c.op("sync", lambda e, d=dst, s=src: e.dma_start(out=d[:], in_=s),
                 inc="xc", n=16)
        c.op("vector", lambda e: e.memset(zf[:], 0.0), inc="v")
        zf_v = c.counts["v"]
        for eng in ("vector", "tensor", "scalar", "gpsimd"):
            c.wait(eng, "xc")

        agin = nc.dram_tensor("agin", [256, N], BF16, kind="Internal").ap()
        agout = nc.dram_tensor("agout", [1024, N], BF16, kind="Internal").ap()
        attw = nc.dram_tensor("attw", [1024, TD], BF16, kind="Internal").ap()

        phase1(nc, c, sh, agin)

        attw_r = attw.rearrange("(c p) n -> p c n", p=128)
        c.wait("sync", "v", zf_v)
        for ch in range(8):
            c.op("sync", lambda e, ch=ch: e.dma_start(
                out=attw_r[:, ch, 0:1536], in_=zf[:]), inc="cp", n=16)
        c.wait("gpsimd", "o0", c.counts["o0"])
        c.wait("gpsimd", "o1", c.counts["o1"])
        c.op("gpsimd", lambda e: e.collective_compute(
            "AllGather", mybir.AluOpType.bypass,
            replica_groups=[[0, 1, 2, 3], [4, 5, 6, 7]],
            ins=[agin.opt()], outs=[agout.opt()]), inc="cc", n=1)
        c.wait("gpsimd", "cc")
        c.wait("gpsimd", "cp")
        cp_sem = c.sems["cp"]

        def emit_switch(e):
            for arm in e.switch_core_id(n=128):
                w = (arm.logical % 8) % 4
                S = 2048 * w
                if w == 0:
                    inst = e.dma_start(out=attw[:, 1536:3584],
                                       in_=agout[:, 0:2048])
                else:
                    inst = e.dma_start(out=attw[:, 0:3584],
                                       in_=agout[:, S - 1536:S + 2048])
                inst.then_inc(cp_sem, 16)
        c.raw("gpsimd", emit_switch, inc="cp", n=16)
        c.barrier()

        l2 = L2(nc)
        l2.build(c, sh, attw)

        c.wait("sync", "o0", c.counts["o0"])
        c.wait("sync", "o1", c.counts["o1"])
        with nc.Block() as block:
            c.finalize(block)
    return nc


BF = ml_dtypes.bfloat16


def make_tables(S):
    t = (S - 1536 + np.arange(TD)).astype(np.float64)
    half_idx = np.arange(0, 64, 2, dtype=np.float64)
    inv_freq = 1.0 / 10000.0 ** (half_idx / 64.0)
    freqs = t[:, None] * inv_freq[None, :]
    freqs64 = np.concatenate([freqs, freqs], axis=1)
    base = (half_idx + 0.4 * 64) / (1.4 * 64)
    power = (t - 4096.0) / 256.0
    scale = base[None, :] ** power[:, None]
    scale64 = np.concatenate([scale, scale], axis=1)
    cos, sin = np.cos(freqs64), np.sin(freqs64)
    cs = (cos * scale64).T
    ss = (sin * scale64).T
    ci = (cos / scale64).T
    si = (sin / scale64).T
    out = np.zeros((128, 4 * TD), dtype=BF)
    for i, tab in enumerate([cs, ss, ci, si]):
        tt = np.tile(tab, (2, 1))
        out[:, i * TD:(i + 1) * TD] = tt.astype(BF)
    return out


def make_RT():
    R = np.zeros((128, 128), dtype=np.float32)
    for hb in (0, 64):
        for i in range(32):
            R[hb + i, hb + 32 + i] = -1.0
            R[hb + 32 + i, hb + i] = 1.0
    return np.ascontiguousarray(R.T).astype(BF)


def make_masks(S):
    big = np.float32(-1e30)
    row = np.arange(128)[:, None]
    qq = np.arange(128)[None, :]
    M0 = np.where(qq <= row, 0.0, big).astype(BF)
    M4 = np.where(qq >= row, 0.0, big).astype(BF)
    mreg = np.concatenate([M0, M4], axis=1)
    tiles = []
    for l in range(2):
        for qb in range(NBND[l]):
            qs_abs = S + (QS[l] - 1536) + 128 * qb
            for ko in range(5):
                j_abs = qs_abs - 512 + ko * 128 + row
                q_abs = qs_abs + qq
                valid = (j_abs >= 0) & (j_abs <= q_abs) & (q_abs <= j_abs + 512)
                valid = valid | (q_abs < 0)
                tiles.append(np.where(valid, 0.0, big).astype(BF))
    return mreg, np.stack(tiles)


def slice_pad(arr2d, S, dtype):
    out = np.zeros((1024, TD), dtype=dtype)
    lo, hi = S - 1536, S + 2048
    src_lo, src_hi = max(lo, 0), min(hi, 8192)
    out[:, src_lo - lo: src_hi - lo] = arr2d[:, src_lo:src_hi].astype(dtype)
    return out


def l2_in_maps(inp):
    x = np.asarray(inp["L0_attn_x"], dtype=np.float32)
    w_out = np.asarray(inp["L0_attn_w_out"], dtype=np.float32)
    b_out = np.asarray(inp["L0_attn_b_out"], dtype=np.float32)
    wqkv = np.asarray(inp["L0_ffn_wqkv"], dtype=np.float32)
    q_scale = np.asarray(inp["L0_ffn_q_scale"], dtype=np.float32)
    k_scale = np.asarray(inp["L0_ffn_k_scale"], dtype=np.float32)
    wo = np.asarray(inp["L0_ffn_wo"], dtype=np.float32)
    ln_g = np.asarray(inp["L0_ffn_ln_g"], dtype=np.float32)
    ln_b = np.asarray(inp["L0_ffn_ln_b"], dtype=np.float32)
    w1 = np.asarray(inp["L0_ffn_w1"], dtype=np.float32)
    b1 = np.asarray(inp["L0_ffn_b1"], dtype=np.float32)
    w2 = np.asarray(inp["L0_ffn_w2"], dtype=np.float32)
    b2 = np.asarray(inp["L0_ffn_b2"], dtype=np.float32)

    woutT = np.ascontiguousarray(w_out.T)
    w1r = np.zeros((2, 1024, 5632), dtype=np.float32)
    b1r = np.zeros((2, 1, 5632), dtype=np.float32)
    for l in range(2):
        a, g = w1[l][:, 0:2730], w1[l][:, 2730:5460]
        ba, bg = b1[l][0:2730], b1[l][2730:5460]
        for pr in range(22):
            n = min(128, 2730 - pr * 128)
            w1r[l][:, pr * 256: pr * 256 + n] = a[:, pr * 128: pr * 128 + n]
            w1r[l][:, pr * 256 + 128: pr * 256 + 128 + n] = g[:, pr * 128: pr * 128 + n]
            b1r[l][0, pr * 256: pr * 256 + n] = ba[pr * 128: pr * 128 + n]
            b1r[l][0, pr * 256 + 128: pr * 256 + 128 + n] = bg[pr * 128: pr * 128 + n]
    w2p = np.zeros((2, 2816, 1024), dtype=np.float32)
    w2p[:, 0:2730] = w2
    b2row = b2[:, None, :]
    lng = np.zeros((128, 16), dtype=np.float32)
    lnb = np.zeros((128, 16), dtype=np.float32)
    for l in range(2):
        lng[:, l * 8:(l + 1) * 8] = ln_g[l].reshape(8, 128).T
        lnb[:, l * 8:(l + 1) * 8] = ln_b[l].reshape(8, 128).T
    qs8 = np.stack([np.tile(q_scale[l] * 8.0, 2) for l in range(2)], axis=1)
    ks1 = np.stack([np.tile(k_scale[l], 2) for l in range(2)], axis=1)
    RT = make_RT()
    identb = np.eye(128, dtype=BF)
    onesb = np.ones((128, 128), dtype=BF)
    oneshh = np.zeros((128, 2), dtype=BF)
    oneshh[0:64, 0] = 1; oneshh[64:128, 1] = 1
    selT = np.ascontiguousarray(oneshh.T)
    onesrow = np.ones((1, 512), dtype=BF)
    onesrow_f = np.ones((1, 64), dtype=np.float32)
    csts = np.broadcast_to(np.array(
        [1.0 / 1024.0, 1e-5, -0.5, -1.0, 1e-30, 0, 0, 0], dtype=np.float32),
        (128, 8)).copy()

    in_maps = []
    for core in range(8):
        b, r = core // 4, core % 4
        S = 2048 * r
        mreg, mbnd = make_masks(S)
        flag = np.full((128, 1), 0.0 if r == 0 else 1.0, dtype=np.float32)
        in_maps.append({
            "xTwin": slice_pad(np.ascontiguousarray(x[b].T), S, np.float32),
            "woutT": woutT, "boutrow": b_out[None, :],
            "wqkv": wqkv, "wo": wo, "w1r": w1r, "b1r": b1r, "w2p": w2p,
            "b2row": b2row, "lng": lng, "lnb": lnb, "qs8": qs8, "ks1": ks1,
            "tabs": make_tables(S), "RT": RT, "identb": identb, "onesb": onesb,
            "oneshh": oneshh, "selT": selT, "onesrow": onesrow,
            "onesrow_f": onesrow_f, "mreg": mreg, "mbnd": mbnd,
            "flag": flag, "csts": csts,
        })
    return in_maps




def kernel(**inputs):
    ims1 = l1_in_maps(inputs)
    ims2 = l2_in_maps(inputs)
    in_maps = []
    for core in range(8):
        m = dict(ims1[core])
        m.update(ims2[core])
        in_maps.append(m)
    nc = build_fused()
    res = run_bass_kernel_spmd(nc, in_maps, core_ids=list(range(8)))
    out = np.empty((2, 8192, 1024), dtype=np.float32)
    for core in range(8):
        b, r = core // 4, core % 4
        S = 2048 * r
        out[b, S:S + 2048, :] = np.asarray(res.results[core]["outT"]).T
    return out


# version 05
# speedup vs baseline: 1.1583x; 1.0203x over previous
import numpy as np
import ml_dtypes
from contextlib import ExitStack
import concourse.bass as bass
import concourse.mybir as mybir
from concourse.bass_utils import run_bass_kernel_spmd

F32 = mybir.dt.float32
BF16 = mybir.dt.bfloat16
AF = mybir.ActivationFunctionType
ALU = mybir.AluOpType
AX = mybir.AxisListType
ENGS = ("sync", "gpsimd", "tensor", "scalar", "vector")

N = 8192
D = 1024
NT = 16
TT = 512


class _SkipRest(Exception):
    pass


class Ctx:
    def __init__(self, nc):
        self.nc = nc
        self.streams = {e: [] for e in ENGS}
        self.sems = {}
        self.counts = {}

    def sem(self, es, name):
        h = es.enter_context(self.nc.semaphore(name))
        self.sems[name] = h
        self.counts[name] = 0

    SELFSEM = {"vector": "v", "scalar": "s", "gpsimd": "g"}

    def op(self, eng, fn, inc=None, n=1):
        if not getattr(self, "enabled", True):
            return self.counts.get(inc, 0) if inc else None
        ss = self.SELFSEM.get(eng)
        if ss is not None and self.counts.get(ss, 0) > 0:
            self.streams[eng].append(("wait", self.sems[ss], self.counts[ss]))
        if inc is not None:
            self.counts[inc] += n
            pair = (self.sems[inc], n)
        else:
            pair = None
        self.streams[eng].append(("op", fn, pair))
        return self.counts[inc] if inc else None

    def wait(self, eng, sem_name, val=None):
        if not getattr(self, "enabled", True):
            return
        v = self.counts[sem_name] if val is None else val
        if v > 0:
            self.streams[eng].append(("wait", self.sems[sem_name], v))

    def raw(self, eng, fn, inc=None, n=1):
        if inc is not None:
            self.counts[inc] += n
        self.streams[eng].append(("op", fn, None))

    def barrier(self):
        snap = dict(self.counts)
        for e in ENGS:
            for s, v in snap.items():
                self.wait(e, s, v)

    def finalize(self, block):
        for e in ENGS:
            stream = self.streams[e]

            def make(stream):
                def run(eng):
                    for item in stream:
                        if item[0] == "op":
                            _, fn, pair = item
                            inst = fn(eng)
                            if pair is not None and inst is not None:
                                inst.then_inc(pair[0], pair[1])
                        else:
                            _, semh, v = item
                            eng.wait_ge(semh, v)
                return run
            getattr(block, e)(make(stream))




def phase1(nc, c, sh, agin):
    PH = "D"
    CUT = 9
    NORANK1 = False
    xT = nc.dram_tensor("xTfull", [D, N], F32, kind="ExternalInput").ap()
    wqkvT = nc.dram_tensor("wqkvT", [D, 768], F32, kind="ExternalInput").ap()
    gamma_c = nc.dram_tensor("gamma_c", [128, 8], F32, kind="ExternalInput").ap()
    beta_cb = nc.dram_tensor("beta_cb", [128, 8], BF16, kind="ExternalInput").ap()
    masks01_i = nc.dram_tensor("masks01", [128, 5 * 512], BF16,
                               kind="ExternalInput").ap()
    aout = agin
    negm_dram = nc.dram_tensor("negm1_scratch", [1, 64], BF16, kind="Internal").ap()

    es = ExitStack()
    sb = lambda name, shape, dt: es.enter_context(nc.sbuf_tensor(name, shape, dt))
    with es:
        identb = sh["identb"]
        onesb = sh["onesb"]
        onesrow = sh["onesrow"]
        onesrow_f = sh["onesrow_f"]
        oneshh = sh["oneshh"]

        xt = [sb(f"xt{i}", [128, 8 * TT], F32) for i in range(2)]
        xb = sb("xb", [128, 8 * TT], BF16)
        xs = sb("xs", [128, 8 * TT], BF16)
        wstage = sb("wstage", [128, 768], F32)
        wraw = sb("wraw", [128, 768], BF16)
        wT = sb("wT", [128, 8 * 768], BF16)
        gam = sb("gam", [128, 8], F32)
        betab = sb("betab", [128, 8], BF16)
        maskt01 = sb("maskt01", [128, 5 * 512], BF16)
        colsum_row = sb("colsum_row", [1, 768], BF16)
        bias_row = sb("bias_row", [1, 768], BF16)
        shift_row = sb("shift_row", [1, 512], BF16)
        st0 = sb("st0", [128, TT], F32)
        st1 = sb("st1", [128, TT], F32)
        st2 = sb("st2", [128, TT], F32)
        st3 = sb("st3", [128, TT], F32)
        s_bc = sb("s_bc", [128, TT], F32)
        shift_bc = sb("shift_bc", [128, TT], F32)
        Qp = [sb(f"Qp{i}", [128, N], BF16) for i in range(2)]
        Kp = [sb(f"Kp{i}", [128, N], BF16) for i in range(2)]
        V_km = sb("V_km", [128, 64 * 260], BF16)
        vstage = sb("vstage", [128, 1024], BF16)
        PT = [sb(f"PT{i}", [128, 1536], BF16) for i in range(2)]
        biasneg = sb("biasneg", [128, 64], F32)
        recip_row = sb("recip_row", [1, 512], F32)
        accst = [sb(f"accst{i}", [65, 512], F32) for i in range(2)]
        recip_bc = sb("recip_bc", [64, 512], F32)
        aout_sb = [sb(f"aout{i}", [64, 512], BF16) for i in range(2)]
        negm2 = sb("negm2", [2, 32], F32)
        negm2b = sb("negm2b", [2, 32], BF16)
        negmrow = sb("negmrow", [1, 64], BF16)

        es_b = ExitStack()
        pb = lambda name, shape, dt: es_b.enter_context(nc.psum_tensor(name, shape, dt))
        p_sumx = pb("p_sumx", [128, TT], F32)
        p_sumx2 = pb("p_sumx2", [128, TT], F32)
        p_qkv = pb("p_qkv", [128, 3 * 512], F32)
        p_vtr = pb("p_vtr", [128, 2 * 1024], BF16)
        p_nb = p_sumx

        for dst, src in [(gam, gamma_c), (betab, beta_cb),
                         (maskt01, masks01_i)]:
            c.op("sync", lambda e, d=dst, s=src: e.dma_start(out=d[:], in_=s), inc="xc", n=16)
        vko = V_km[:].rearrange("p (c h d) -> p c h d", h=4, d=65)
        c.op("vector", lambda e: e.memset(vko[:, :, :, 64:65], 1.0), inc="v")
        c.wait("vector", "xc")
        for dc in range(8):
            wv = c.counts["v"]
            c.wait("sync", "v", wv)
            if dc > 0:
                c.wait("sync", "t")
            c.op("sync", lambda e, dc=dc: e.dma_start(
                out=wstage[:], in_=wqkvT[dc * 128:(dc + 1) * 128, :]), inc="xw", n=16)
            c.wait("vector", "xw")
            c.op("vector", lambda e, dc=dc: e.tensor_scalar(
                wT[:, dc * 768:(dc + 1) * 768], wstage[:], gam[:, dc:dc + 1], None,
                ALU.mult), inc="v")
            c.op("vector", lambda e: e.tensor_copy(wraw[:], wstage[:]), inc="v")
            wr_v = c.counts["v"]
            c.wait("tensor", "v", wr_v)
            c.op("tensor", lambda e, dc=dc: e.matmul(
                p_sumx[0:1, 0:512], betab[:, dc:dc + 1], wraw[:, 0:512],
                start=(dc == 0), stop=(dc == 7)), inc="t")
            c.op("tensor", lambda e, dc=dc: e.matmul(
                p_sumx2[0:1, 0:256], betab[:, dc:dc + 1], wraw[:, 512:768],
                start=(dc == 0), stop=(dc == 7)), inc="t")

        c.wait("tensor", "v")
        for (o0, o1) in ((0, 512), (512, 768)):
            for dc in range(8):
                c.op("tensor", lambda e, dc=dc, o0=o0, o1=o1: e.matmul(
                    p_qkv[0:1, o0:o1], onesb[:, 0:1],
                    wT[:, dc * 768 + o0: dc * 768 + o1],
                    start=(dc == 0), stop=(dc == 7)),
                    inc=("t" if (dc == 7 and o0 == 512) else None))
        c.wait("vector", "t")
        c.op("vector", lambda e: e.tensor_copy(colsum_row[:], p_qkv[0:1, 0:768]), inc="v")
        c.op("vector", lambda e: e.tensor_copy(bias_row[0:1, 0:512], p_sumx[0:1, 0:512]), inc="v")
        c.op("vector", lambda e: e.tensor_copy(bias_row[0:1, 512:768], p_sumx2[0:1, 0:256]), inc="v")

        NTILES = NT if PH != "A" else 0
        xt_free_v = [0, 0]
        qkv_evac_v = {}
        vtr_free_v = {}
        for t in range(NTILES):
            buf = t % 2
            c.wait("sync", "v", xt_free_v[buf])
            dsem = "d0" if buf == 0 else "d1"
            c.op("sync", lambda e, t=t, buf=buf: e.dma_start(
                out=xt[buf][:].rearrange("p (c n) -> p c n", n=TT),
                in_=xT.rearrange("(c p) n -> p c n", p=128)[:, :, t * TT:(t + 1) * TT]),
                inc=dsem, n=16)
            c.wait("vector", dsem)
            c.op("vector", lambda e, buf=buf: e.tensor_copy(xb[:], xt[buf][:]), inc="v")
            c.op("vector", lambda e: e.tensor_tensor(xs[:], xb[:], xb[:], ALU.mult), inc="v")
            xb_ready = c.counts["v"]
            c.enabled = CUT >= 1 and PH != "A"
            c.wait("tensor", "v", xb_ready)
            if t > 0:
                c.wait("tensor", "s", stats_consumed_s)
                c.wait("tensor", "v", stats_consumed_v)
            for dc in range(8):
                c.op("tensor", lambda e, dc=dc: e.matmul(
                    p_sumx[:], onesb[:], xb[:, dc * TT:(dc + 1) * TT],
                    start=(dc == 0), stop=(dc == 7)))
            for dc in range(8):
                c.op("tensor", lambda e, dc=dc: e.matmul(
                    p_sumx2[:], onesb[:], xs[:, dc * TT:(dc + 1) * TT],
                    start=(dc == 0), stop=(dc == 7)),
                    inc=("t" if dc == 7 else None))
            stats_t = c.counts["t"]
            c.wait("scalar", "t", stats_t)
            c.op("scalar", lambda e: e.activation(st0[:], p_sumx[:], AF.Square,
                                                  scale=1.0 / 1024.0), inc="s")
            stats_consumed_s = c.counts["s"]
            c.wait("vector", "s")
            c.op("vector", lambda e: e.scalar_tensor_tensor(
                st1[:], p_sumx2[:], 1.0 / 1024.0, st0[:], ALU.mult, ALU.subtract), inc="v")
            c.wait("scalar", "v")
            c.op("scalar", lambda e: e.activation(st2[:], st1[:], AF.Ln, bias=1e-5), inc="s")
            c.op("scalar", lambda e: e.activation(st0[:], st2[:], AF.Exp, scale=-0.5), inc="s")
            c.op("scalar", lambda e: e.activation(st3[:], st2[:], AF.Exp, scale=-1.0), inc="s")
            c.wait("vector", "s")
            c.op("vector", lambda e: e.tensor_scalar(
                st1[:], st3[:], -1e-5, 1.0 + 1e-5, ALU.mult, ALU.add), inc="v")
            c.wait("scalar", "v")
            c.op("scalar", lambda e: e.activation(st2[:], st1[:], AF.Ln), inc="s")
            c.op("scalar", lambda e: e.activation(st3[:], st2[:], AF.Exp, scale=-0.5), inc="s")
            c.wait("vector", "s")
            c.op("vector", lambda e: e.tensor_tensor(s_bc[:], st0[:], st3[:], ALU.mult), inc="v")
            c.op("vector", lambda e: e.scalar_tensor_tensor(
                shift_bc[:], p_sumx[:], -1.0 / 1024.0, s_bc[:], ALU.mult, ALU.mult), inc="v")
            stats_consumed_v = c.counts["v"]
            c.op("vector", lambda e: e.tensor_copy(shift_row[:], shift_bc[0:1, :]), inc="v")
            c.enabled = CUT >= 2 and PH != "A"
            sbc3 = s_bc[:].rearrange("p (a n) -> p a n", a=1).broadcast_to([128, 8, TT])
            c.op("vector", lambda e, buf=buf: e.tensor_tensor(
                xs[:].rearrange("p (c n) -> p c n", n=TT),
                xt[buf][:].rearrange("p (c n) -> p c n", n=TT), sbc3, ALU.mult), inc="v")
            xt_free_v[buf] = c.counts["v"]
            xs_ready = c.counts["v"]

            c.enabled = CUT >= 3 and PH != "A"
            c.wait("tensor", "v", xs_ready)
            for ec in range(6):
                bank = ec % 3
                if qkv_evac_v.get(bank):
                    c.wait("tensor", "v", qkv_evac_v[bank])
                for dc in range(8):
                    c.op("tensor", lambda e, dc=dc, ec=ec, bank=bank: e.matmul(
                        p_qkv[:, bank * 512:(bank + 1) * 512],
                        wT[:, dc * 768 + ec * 128: dc * 768 + (ec + 1) * 128],
                        xs[:, dc * TT:(dc + 1) * TT],
                        start=(dc == 0), stop=(NORANK1 and dc == 7)),
                        inc=("t" if (NORANK1 and dc == 7) else None))
                if not NORANK1:
                    c.op("tensor", lambda e, ec=ec, bank=bank: e.matmul(
                        p_qkv[:, bank * 512:(bank + 1) * 512],
                        colsum_row[0:1, ec * 128:(ec + 1) * 128], shift_row[:],
                        start=False, stop=False))
                    c.op("tensor", lambda e, ec=ec, bank=bank: e.matmul(
                        p_qkv[:, bank * 512:(bank + 1) * 512],
                        bias_row[0:1, ec * 128:(ec + 1) * 128], onesrow[:],
                        start=False, stop=True), inc="t")
                mm_t = c.counts["t"]
                c.wait("vector", "t", mm_t)
                pslice = (lambda bank=bank: p_qkv[:, bank * 512:(bank + 1) * 512])()
                if ec < 2:
                    c.op("vector", lambda e, ec=ec, t=t, ps=pslice: e.tensor_scalar(
                        Qp[ec][:, t * TT:(t + 1) * TT], ps, 0.125, None, ALU.mult), inc="v")
                elif ec < 4:
                    c.op("vector", lambda e, ec=ec, t=t, ps=pslice: e.tensor_copy(
                        Kp[ec - 2][:, t * TT:(t + 1) * TT], ps), inc="v")
                else:
                    c.op("vector", lambda e, ec=ec, ps=pslice: e.tensor_copy(
                        vstage[:, (ec - 4) * 512:(ec - 3) * 512], ps), inc="v")
                qkv_evac_v[bank] = c.counts["v"]
            vstage_ready = c.counts["v"]

            c.enabled = CUT >= 4 and PH != "A"
            c.wait("tensor", "v", vstage_ready)
            for ec in range(2):
                for sc in range(4):
                    tb = (ec * 4 + sc) % 2
                    if t > 0 or ec > 0 or sc > 1:
                        c.wait("tensor", "v", vtr_free_v.get(tb, 0))
                    c.op("tensor", lambda e, ec=ec, sc=sc, tb=tb: e.transpose(
                        p_vtr[:, tb * 1024: tb * 1024 + 128],
                        vstage[:, ec * 512 + sc * 128: ec * 512 + (sc + 1) * 128],
                        identb[:]), inc="t")
                    tr_t = c.counts["t"]
                    cc = 4 * t + sc
                    c.wait("vector", "t", tr_t)
                    c.op("vector", lambda e, cc=cc, ec=ec, tb=tb: e.tensor_copy(
                        V_km[:, cc * 260 + ec * 130: cc * 260 + (ec + 1) * 130]
                        .rearrange("p (h d) -> p h d", d=65)[:, :, 0:64],
                        p_vtr[:, tb * 1024: tb * 1024 + 128]
                        .rearrange("p (h d) -> p h d", d=64)), inc="v")
                    vtr_free_v[tb] = c.counts["v"]

        c.enabled = True
        for p in (range(2) if PH not in ("A", "B") else range(0)):
            for half in range(2):
                c.op("vector", lambda e, p=p, half=half: e.tensor_tensor(
                    xb[:], Qp[p][:, half * 4096:(half + 1) * 4096],
                    Kp[p][:, half * 4096:(half + 1) * 4096], ALU.mult), inc="v")
                qk_v = c.counts["v"]
                c.wait("tensor", "v", qk_v)
                for t8 in range(8):
                    c.op("tensor", lambda e, t8=t8: e.matmul(
                        p_nb[0:2, 0:512], oneshh[:, 0:2], xb[:, t8 * 512:(t8 + 1) * 512],
                        start=True, stop=True), inc="t")
                    mm_t = c.counts["t"]
                    c.wait("vector", "t", mm_t)
                    qt = half * 8 + t8
                    c.op("vector", lambda e, p=p, qt=qt: e.reduce_max(
                        negm2[0:2, p * 16 + qt: p * 16 + qt + 1],
                        p_nb[0:2, 0:512], axis=AX.X), inc="v")
                    c.wait("tensor", "v")
        c.enabled = PH not in ("A", "B")
        c.op("vector", lambda e: e.tensor_scalar(
            negm2b[:], negm2[:], -1.0, None, ALU.mult), inc="v")
        c.wait("gpsimd", "v")
        c.op("gpsimd", lambda e: e.dma_start(
            out=negm_dram.rearrange("o (r pq) -> (o r) pq", r=2), in_=negm2b[:]),
            inc="g", n=16)
        c.wait("gpsimd", "g")
        c.op("gpsimd", lambda e: e.dma_start(
            out=negmrow[:].rearrange("o (p r q) -> o p r q", p=2, r=2),
            in_=negm_dram.rearrange("o (r p q) -> o p r q", r=2, p=2)),
            inc="g", n=16)
        c.wait("tensor", "g")
        for h in range(4):
            for qt in range(NT):
                c.op("tensor", lambda e, h=h, qt=qt: e.matmul(
                    p_nb[:, 64 + h * 16 + qt: 65 + h * 16 + qt],
                    onesrow[0:1, 0:128], negmrow[0:1, h * 16 + qt: h * 16 + qt + 1],
                    start=True, stop=True),
                    inc=("t" if (h == 3 and qt == NT - 1) else None))
        c.wait("vector", "t")
        c.op("vector", lambda e: e.tensor_copy(biasneg[:], p_nb[:, 64:128]), inc="v")

        c.enabled = PH not in ("A", "B", "C")
        c.barrier()
        es_b.close()
        p_sim = [es.enter_context(nc.psum_tensor(f"p_sim{i}", [128, 1536], F32))
                 for i in range(2)]
        p_acc = es.enter_context(nc.psum_tensor("p_acc", [65, 512], F32))
        p_bcast = es.enter_context(nc.psum_tensor("p_bcast", [64, 512], F32))

        sim_free_s = [0, 0]
        pt_free_t = [0, 0]
        ptmask_free_v = [0, 0]
        acc_free_v = 0
        bcast_free_v = 0
        aout_free_o = [0, 0]
        nqt_done = 0
        NH = 1 if PH == "D1" else 4
        NQT = 2 if PH == "D1" else NT
        for h in range(NH):
            p = h // 2
            rb = 64 * (h % 2)
            for qt in range(NQT):
                nk = 4 * qt + 4
                ng = (nk + 2) // 3
                exp_s = []
                exp_v = []
                av_count = 0
                total_av = 3 * ng
                for g in range(ng):
                    jb = g % 2
                    c.wait("tensor", "s", sim_free_s[jb])
                    simops = []
                    masked = []
                    for j in range(3):
                        kc = 3 * g + j
                        kce = min(kc, 63)
                        patt = None
                        if kc > 4 * qt + 3:
                            patt = 4
                        elif kc >= 4 * qt:
                            patt = kc - 4 * qt
                        simops.append(lambda e, jb=jb, j=j, kce=kce, p=p, rb=rb, qt=qt: e.matmul(
                            p_sim[jb][:, j * 512:(j + 1) * 512],
                            Kp[p][rb:rb + 64, kce * 128:(kce + 1) * 128],
                            Qp[p][rb:rb + 64, qt * 512:(qt + 1) * 512],
                            start=True, stop=True))
                        if patt is not None:
                            masked.append((j, patt))
                    for fi, f in enumerate(simops):
                        c.op("tensor", f, inc=("t" if fi == len(simops) - 1 else None))
                    sim_t = c.counts["t"]
                    c.wait("scalar", "t", sim_t)
                    c.wait("scalar", "v", ptmask_free_v[jb])
                    c.op("scalar", lambda e, jb=jb, h=h, qt=qt: e.activation(
                        PT[jb][:], p_sim[jb][:], AF.Exp,
                        bias=biasneg[:, h * 16 + qt: h * 16 + qt + 1], scale=1.0), inc="s")
                    exp_s.append(c.counts["s"])
                    sim_free_s[jb] = c.counts["s"]
                    if masked:
                        c.wait("vector", "s", c.counts["s"])
                        for j, patt in masked:
                            c.op("vector", lambda e, jb=jb, j=j, patt=patt: e.tensor_tensor(
                                PT[jb][:, j * 512:(j + 1) * 512],
                                PT[jb][:, j * 512:(j + 1) * 512],
                                maskt01[:, patt * 512:(patt + 1) * 512],
                                ALU.mult), inc="v")
                        ptmask_free_v[jb] = c.counts["v"]
                        exp_v.append(c.counts["v"])
                    else:
                        exp_v.append(0)
                    if g >= 1:
                        gp = g - 1
                        jp = gp % 2
                        c.wait("tensor", "s", exp_s[gp])
                        if exp_v[gp]:
                            c.wait("tensor", "v", exp_v[gp])
                        if g == 1:
                            c.wait("tensor", "v", acc_free_v)
                        for j in range(3):
                            kc = 3 * gp + j
                            kce = min(kc, 63)
                            c.op("tensor", lambda e, jp=jp, j=j, kce=kce, h=h,
                                 st=(av_count == 0), sp=False: e.matmul(
                                p_acc[:], V_km[:, kce * 260 + h * 65: kce * 260 + h * 65 + 65],
                                PT[jp][:, j * 512:(j + 1) * 512],
                                start=st, stop=sp), inc=("t" if j == 2 else None))
                            av_count += 1
                        pt_free_t[jp] = c.counts["t"]
                gp = ng - 1
                jp = gp % 2
                c.wait("tensor", "s", exp_s[gp])
                if exp_v[gp]:
                    c.wait("tensor", "v", exp_v[gp])
                if ng == 1:
                    c.wait("tensor", "v", acc_free_v)
                for j in range(3):
                    kc = 3 * gp + j
                    kce = min(kc, 63)
                    c.op("tensor", lambda e, jp=jp, j=j, kce=kce, h=h,
                         st=(av_count == 0), sp=(j == 2): e.matmul(
                        p_acc[:], V_km[:, kce * 260 + h * 65: kce * 260 + h * 65 + 65],
                        PT[jp][:, j * 512:(j + 1) * 512],
                        start=st, stop=sp), inc=("t" if j == 2 else None))
                    av_count += 1
                pt_free_t[jp] = c.counts["t"]
                av_done_t = c.counts["t"]

                ob = nqt_done % 2
                c.wait("vector", "t", av_done_t)
                c.op("vector", lambda e: e.reciprocal(recip_row[:], p_acc[64:65, :]), inc="v")
                recip_v = c.counts["v"]
                c.wait("tensor", "v", max(recip_v, bcast_free_v))
                c.op("tensor", lambda e: e.matmul(
                    p_bcast[:], onesrow_f[:], recip_row[:], start=True, stop=True), inc="t")
                bc_t = c.counts["t"]
                c.wait("vector", "t", bc_t)
                c.op("vector", lambda e: e.tensor_copy(recip_bc[:], p_bcast[:]), inc="v")
                bcast_free_v = c.counts["v"]
                c.wait("vector", "o0" if ob == 0 else "o1", aout_free_o[ob])
                c.op("vector", lambda e, ob=ob: e.scalar_tensor_tensor(
                    aout_sb[ob][:], p_acc[0:64, :], 1.0, recip_bc[:],
                    ALU.mult, ALU.mult), inc="v")
                acc_free_v = c.counts["v"]
                c.wait("sync", "v", acc_free_v)
                osem = "o0" if ob == 0 else "o1"
                c.op("sync", lambda e, h=h, qt=qt, ob=ob: e.dma_start(
                    out=aout[64 * h: 64 * (h + 1), qt * 512:(qt + 1) * 512],
                    in_=aout_sb[ob][:]), inc=osem, n=16)
                aout_free_o[ob] = c.counts[osem]
                nqt_done += 1






def make_masks_l1():
    m = np.zeros((128, 5 * 512), dtype=np.float32)
    for pnum in range(4):
        row = np.arange(128)[:, None] + 128 * pnum
        qq = np.arange(512)[None, :]
        m[:, pnum * 512:(pnum + 1) * 512] = np.where(row <= qq, 0.0, -1e30)
    m[:, 4 * 512:] = -1e30
    return m.astype(ml_dtypes.bfloat16)


def l1_in_maps(inp):
    x = np.asarray(inp["L0_attn_x"], dtype=np.float32)
    w_qkv = np.asarray(inp["L0_attn_w_qkv"], dtype=np.float32)
    gamma = np.asarray(inp["L0_attn_ln_gamma"], dtype=np.float32)
    beta = np.asarray(inp["L0_attn_ln_beta"], dtype=np.float32)
    gamma_c = np.ascontiguousarray(gamma.reshape(8, 128).T)
    beta_cb = np.ascontiguousarray(beta.reshape(8, 128).T).astype(ml_dtypes.bfloat16)
    masks = make_masks_l1()
    masks01 = (masks == 0).astype(ml_dtypes.bfloat16)
    in_maps = []
    for core in range(8):
        b, hg = core // 4, core % 4
        xTb = np.ascontiguousarray(x[b].T)
        rows = []
        for part in range(3):
            rows.extend(range(part * 1024 + hg * 256, part * 1024 + (hg + 1) * 256))
        wqkvT = np.ascontiguousarray(w_qkv[rows, :].T)
        in_maps.append({
            "xTfull": xTb, "wqkvT": wqkvT, "gamma_c": gamma_c,
            "beta_cb": beta_cb, "masks01": masks01,
        })
    return in_maps




TD = 3584
QS = [640, 1408]
KS = [128, 896]
CS = [768, 1536]
NQB = [(TD - QS[0]) // 128, (TD - QS[1]) // 128]
NBND = [11, 5]
MAXQB = 56
H = 8


def ln_chain(c, st0, st1, st2, s_out, sh_out, p_sumx, p_sumx2):
    c.wait("scalar", "t")
    c.op("scalar", lambda e: e.activation(st0[:], p_sumx[:], AF.Square,
                                          scale=1.0 / 1024.0), inc="s")
    c.wait("vector", "s")
    c.op("vector", lambda e: e.scalar_tensor_tensor(
        st1[:], p_sumx2[:], 1.0 / 1024.0, st0[:], ALU.mult, ALU.subtract), inc="v")
    c.wait("scalar", "v")
    c.op("scalar", lambda e: e.activation(st2[:], st1[:], AF.Ln, bias=1e-5), inc="s")
    c.op("scalar", lambda e: e.activation(s_out[:], st2[:], AF.Exp, scale=-0.5), inc="s")
    c.wait("vector", "s")
    c.op("vector", lambda e: e.scalar_tensor_tensor(
        sh_out[:], p_sumx[:], -1.0 / 1024.0, s_out[:], ALU.mult, ALU.mult), inc="v")


def rep8(ap):
    return ap.rearrange("p (a n) -> p a n", a=1).broadcast_to([128, 8, 512])


class L2:
    def __init__(self, nc):
        self.nc = nc
        di = lambda n, sh, dt: nc.dram_tensor(n, sh, dt, kind="ExternalInput").ap()
        self.xT = di("xTwin", [1024, TD], F32)
        self.woutT = di("woutT", [1024, 1024], F32)
        self.boutrow = di("boutrow", [1, 1024], F32)
        self.wqkv_i = di("wqkv", [2, 1024, 1536], F32)
        self.wo_i = di("wo", [2, 512, 1024], F32)
        self.w1_i = di("w1r", [2, 1024, 5632], F32)
        self.b1_i = di("b1r", [2, 1, 5632], F32)
        self.w2_i = di("w2p", [2, 2816, 1024], F32)
        self.b2_i = di("b2row", [2, 1, 1024], F32)
        self.b1col_i = di("b1col", [128, 88], F32)
        self.b2col_i = di("b2col", [128, 16], F32)
        self.lng_i = di("lng", [128, 16], F32)
        self.lnb_i = di("lnb", [128, 16], F32)
        self.qs8_i = di("qs8", [128, 2], F32)
        self.ks_i = di("ks1", [128, 2], F32)
        self.tabs_i = di("tabs", [128, 4 * TD], BF16)
        self.RT_i = di("RT", [128, 128], BF16)
        self.mreg_i = di("mreg", [128, 2 * 128], BF16)
        self.mbnd_i = di("mbnd", [sum(NBND) * 5, 128, 128], BF16)
        self.flag_i = di("flag", [128, 1], F32)
        self.outT = nc.dram_tensor("outT", [1024, 2048], F32, kind="ExternalOutput").ap()
        ds = lambda n: nc.dram_tensor(n, [1024, TD], F32, kind="Internal").ap()
        self.sc_xtld = ds("sc_xtld")
        self.sc_xin = ds("sc_xin")
        self.sc_xmid = [ds(f"sc_xmid{l}") for l in range(2)]
        self.sc_xout = [ds(f"sc_xout{l}") for l in range(2)]
        self.negm_dram = nc.dram_tensor(
            "negm_scratch", [1, 2 * 4 * MAXQB], BF16, kind="Internal").ap()
        self.sc_o = nc.dram_tensor("sc_o", [512, TD], BF16, kind="Internal").ap()

    def rr(self, ap):
        return ap.rearrange("(c p) n -> p c n", p=128)

    def build(self, c, sh, attw):
        nc = self.nc
        self.attnT = attw
        es = ExitStack()
        self.es = es
        sb = lambda n, sh_, dt: es.enter_context(nc.sbuf_tensor(n, sh_, dt))
        with es:
            self.c = c
            self.identb = sh["identb"]
            self.onesb = sh["onesb"]
            self.oneshh = sh["oneshh"]
            self.selT = sh["selT"]
            self.onesrow = sh["onesrow"]
            self.onesrow_f = sh["onesrow_f"]
            self.RT = sb("RT_s", [128, 128], BF16)
            self.mreg = sb("mreg_s", [128, 2 * 128], BF16)
            self.lng = sb("lng_s", [128, 16], F32)
            self.lnb = sb("lnb_s", [128, 16], F32)
            self.qs8 = sb("qs8_s", [128, 2], F32)
            self.ks1 = sb("ks1_s", [128, 2], F32)
            self.flag = sb("flag_s", [128, 1], F32)
            self.b1col = sb("b1col_s", [128, 88], F32)
            self.b2col = sb("b2col_s", [128, 16], F32)
            self.st0 = sb("st0_p2", [128, 512], F32)
            self.st1 = sb("st1_p2", [128, 512], F32)
            self.st2 = sb("st2_p2", [128, 512], F32)
            self.s_bc = sb("s_bc_p2", [128, 512], F32)
            self.sh_bc = sb("sh_bc_p2", [128, 512], F32)

            for dst, src in [
                (self.RT, self.RT_i), (self.mreg, self.mreg_i),
                (self.lng, self.lng_i), (self.lnb, self.lnb_i),
                (self.qs8, self.qs8_i), (self.ks1, self.ks_i),
                (self.flag, self.flag_i), (self.b1col, self.b1col_i),
                (self.b2col, self.b2col_i),
            ]:
                c.op("sync", lambda e, d=dst, s=src: e.dma_start(out=d[:], in_=s),
                     inc="xc", n=16)
            for eng in ("vector", "tensor", "scalar", "gpsimd"):
                c.wait(eng, "xc")

            self.stageA()
            self.attn_part(0, self.sc_xin, self.sc_xmid[0])
            self.ffn_part(0, self.sc_xmid[0], self.sc_xout[0])
            self.attn_part(1, self.sc_xout[0], self.sc_xmid[1])
            self.ffn_part(1, self.sc_xmid[1], self.sc_xout[1])
            self.final_stage()

    def stageA(self):
        nc, c = self.nc, self.c
        esa = ExitStack()
        sba = lambda n, sh, dt: esa.enter_context(nc.sbuf_tensor(n, sh, dt))
        pba = lambda n, sh, dt: esa.enter_context(nc.psum_tensor(n, sh, dt))
        woT = sba("woT", [128, 8 * 1024], BF16)
        brow = sba("browA", [1, 1024], BF16)
        at_t = [sba(f"atA{i}", [128, 8 * 512], BF16) for i in range(2)]
        xf_t = [sba(f"xfA{i}", [128, 8 * 512], F32) for i in range(2)]
        yf = sba("yfA", [128, 8 * 512], F32)
        ybf = sba("ybfA", [128, 8 * 512], BF16)
        y2b = sba("y2bA", [128, 8 * 512], BF16)
        xtl = sba("xtlA", [128, 8 * 512], F32)
        xin_t = sba("xinA", [128, 8 * 512], F32)
        self.s2_bc = esa.enter_context(nc.sbuf_tensor("s2_bcA", [128, 512], F32))
        self.sh2_bc = esa.enter_context(nc.sbuf_tensor("sh2_bcA", [128, 512], F32))
        p_y = pba("p_yA", [128, 3 * 512], F32)
        p_s1 = pba("p_s1A", [128, 512], F32)
        p_s2 = pba("p_s2A", [128, 512], F32)

        c.op("gpsimd", lambda e: e.dma_start(out=brow[:], in_=self.boutrow),
             inc="g", n=16)
        c.op("gpsimd", lambda e: e.dma_start(
            out=woT[:].rearrange("p (c n) -> p c n", n=1024),
            in_=self.woutT.rearrange("(c p) n -> p c n", p=128)), inc="g", n=16)
        c.wait("tensor", "g")

        atr = self.rr(self.attnT)
        xtr = self.rr(self.xT)
        xtld_r = self.rr(self.sc_xtld)
        xin_r = self.rr(self.sc_xin)
        st0, st1, st2 = self.st0, self.st1, self.st2

        tile_free = [0, 0]
        tile_free_t = [0, 0]
        bank_free = {}
        statA_s = 0
        statA_v = 0
        xtl_dma = [0, 0]
        xin_dma = [0, 0]
        for t in range(7):
            bf = t % 2
            dsem = "d0" if bf == 0 else "d1"
            c.wait("sync", "v", tile_free[bf])
            c.wait("sync", "t", tile_free_t[bf])
            c.op("sync", lambda e, t=t, bf=bf: e.dma_start(
                out=at_t[bf][:].rearrange("p (c n) -> p c n", n=512),
                in_=atr[:, :, t * 512:(t + 1) * 512]), inc=dsem, n=16)
            c.op("sync", lambda e, t=t, bf=bf: e.dma_start(
                out=xf_t[bf][:].rearrange("p (c n) -> p c n", n=512),
                in_=xtr[:, :, t * 512:(t + 1) * 512]), inc=dsem, n=16)
            c.wait("tensor", dsem)
            for dc in range(8):
                bank = dc % 3
                if bank_free.get(bank):
                    c.wait("tensor", "v", bank_free[bank])
                for ec in range(8):
                    c.op("tensor", lambda e, dc=dc, ec=ec, bf=bf, bank=bank: e.matmul(
                        p_y[:, bank * 512:(bank + 1) * 512],
                        woT[:, ec * 1024 + dc * 128: ec * 1024 + (dc + 1) * 128],
                        at_t[bf][:, ec * 512:(ec + 1) * 512],
                        start=(ec == 0), stop=False))
                c.op("tensor", lambda e, dc=dc, bank=bank: e.matmul(
                    p_y[:, bank * 512:(bank + 1) * 512],
                    brow[0:1, dc * 128:(dc + 1) * 128], self.onesrow[:],
                    start=False, stop=True), inc="t")
                c.wait("vector", "t")
                c.op("vector", lambda e, dc=dc, bank=bank: e.tensor_copy(
                    yf[:, dc * 512:(dc + 1) * 512],
                    p_y[:, bank * 512:(bank + 1) * 512]), inc="v")
                bank_free[bank] = c.counts["v"]
            tile_free_t[bf] = c.counts["t"]
            c.op("vector", lambda e: e.tensor_copy(ybf[:], yf[:]), inc="v")
            c.op("vector", lambda e: e.tensor_tensor(y2b[:], ybf[:], ybf[:],
                                                     ALU.mult), inc="v")
            yb_v = c.counts["v"]
            c.wait("tensor", "v", yb_v)
            c.wait("tensor", "s", statA_s)
            c.wait("tensor", "v", statA_v)
            for dc in range(8):
                c.op("tensor", lambda e, dc=dc: e.matmul(
                    p_s1[:], self.onesb[:], ybf[:, dc * 512:(dc + 1) * 512],
                    start=(dc == 0), stop=(dc == 7)))
            for dc in range(8):
                c.op("tensor", lambda e, dc=dc: e.matmul(
                    p_s2[:], self.onesb[:], y2b[:, dc * 512:(dc + 1) * 512],
                    start=(dc == 0), stop=(dc == 7)),
                    inc=("t" if dc == 7 else None))
            ln_chain(c, st0, st1, st2, self.s_bc, self.sh_bc, p_s1, p_s2)
            statA_s = c.counts["s"]
            c.wait("vector", "e0" if bf == 1 else "e1", xtl_dma[1 - bf])
            c.op("vector", lambda e: e.scalar_tensor_tensor(
                xtl[:].rearrange("p (c n) -> p c n", n=512),
                yf[:].rearrange("p (c n) -> p c n", n=512), 1.0, rep8(self.s_bc[:]),
                ALU.mult, ALU.mult), inc="v")
            c.op("vector", lambda e: e.tensor_tensor(
                xtl[:].rearrange("p (c n) -> p c n", n=512),
                xtl[:].rearrange("p (c n) -> p c n", n=512), rep8(self.sh_bc[:]),
                ALU.add), inc="v")
            c.op("vector", lambda e, bf=bf: e.tensor_tensor(
                xtl[:], xtl[:], xf_t[bf][:], ALU.add), inc="v")
            tile_free[bf] = c.counts["v"]
            esem = "e0" if bf == 0 else "e1"
            c.wait("sync", "v")
            c.op("sync", lambda e, t=t: e.dma_start(
                out=xtld_r[:, :, t * 512:(t + 1) * 512],
                in_=xtl[:].rearrange("p (c n) -> p c n", n=512)), inc=esem, n=16)
            c.op("vector", lambda e: e.tensor_copy(ybf[:], xtl[:]), inc="v")
            c.op("vector", lambda e: e.tensor_tensor(y2b[:], ybf[:], ybf[:],
                                                     ALU.mult), inc="v")
            x2_v = c.counts["v"]
            c.wait("tensor", "v", x2_v)
            c.wait("tensor", "s", statA_s)
            for dc in range(8):
                c.op("tensor", lambda e, dc=dc: e.matmul(
                    p_s1[:], self.onesb[:], ybf[:, dc * 512:(dc + 1) * 512],
                    start=(dc == 0), stop=(dc == 7)))
            for dc in range(8):
                c.op("tensor", lambda e, dc=dc: e.matmul(
                    p_s2[:], self.onesb[:], y2b[:, dc * 512:(dc + 1) * 512],
                    start=(dc == 0), stop=(dc == 7)),
                    inc=("t" if dc == 7 else None))
            ln_chain(c, st0, st1, st2, self.s2_bc, self.sh2_bc, p_s1, p_s2)
            statA_s = c.counts["s"]
            statA_v = c.counts["v"]
            c.wait("vector", "e0" if bf == 1 else "e1", xin_dma[1 - bf])
            c.op("vector", lambda e: e.scalar_tensor_tensor(
                xin_t[:].rearrange("p (c n) -> p c n", n=512),
                xtl[:].rearrange("p (c n) -> p c n", n=512), 1.0, rep8(self.s2_bc[:]),
                ALU.mult, ALU.mult), inc="v")
            c.op("vector", lambda e: e.tensor_tensor(
                xin_t[:].rearrange("p (c n) -> p c n", n=512),
                xin_t[:].rearrange("p (c n) -> p c n", n=512), rep8(self.sh2_bc[:]),
                ALU.add), inc="v")
            if t == 2:
                c.op("vector", lambda e: e.tensor_scalar(
                    xin_t[:].rearrange("p (c n) -> p c n", n=512)[:, :, 511:512],
                    xin_t[:].rearrange("p (c n) -> p c n", n=512)[:, :, 511:512],
                    self.flag[:, 0:1], None, ALU.mult), inc="v")
            c.wait("sync", "v")
            c.op("sync", lambda e, t=t: e.dma_start(
                out=xin_r[:, :, t * 512:(t + 1) * 512],
                in_=xin_t[:].rearrange("p (c n) -> p c n", n=512)), inc=esem, n=16)
            xtl_dma[bf] = c.counts[esem]
            xin_dma[bf] = c.counts[esem]
        c.wait("sync", "e0", c.counts["e0"])
        c.wait("sync", "e1", c.counts["e1"])
        c.barrier()
        esa.close()

    def attn_part(self, l, z_dram, xmid_dram):
        nc, c = self.nc, self.c
        qs_off, ks_off = QS[l], KS[l]
        Tq, Tk = TD - qs_off, TD - ks_off
        nqb = NQB[l]
        nkt = (Tk + 511) // 512
        nbnd = NBND[l]
        mb_base = 5 * sum(NBND[:l])

        e2 = ExitStack()
        sb2 = lambda n, sh, dt: e2.enter_context(nc.sbuf_tensor(f"{n}_l{l}", sh, dt))
        pb2 = lambda n, sh, dt: e2.enter_context(nc.psum_tensor(f"{n}_l{l}", sh, dt))

        wq = sb2("wq", [128, 8 * 1536], BF16)
        zb = [sb2(f"zb{i}", [128, 8 * 513], BF16) for i in range(2)]
        tbt = [sb2(f"tbt{i}", [128, 4 * 512], BF16) for i in range(2)]
        Qf = sb2("Qf", [128, 4 * Tq], BF16)
        Kf = sb2("Kf", [128, 4 * Tk], BF16)
        Vkm = sb2("Vkm", [128, (Tk // 128) * 520], BF16)
        vstage = sb2("vstage", [128, 4 * 512], BF16)
        qraw = sb2("qraw", [128, 512], BF16)
        sqb_ = sb2("sqb", [128, 512], BF16)
        rnb = sb2("rnb", [2, 512], BF16)
        qn = sb2("qn", [128, 512], BF16)
        t1f = sb2("t1f", [128, 512], F32)
        uf = sb2("uf", [128, 512], F32)
        negm2 = sb2("negm2", [2, 4 * MAXQB], F32)
        negm2b = sb2("negm2b", [2, 4 * MAXQB], BF16)
        negmrow = sb2("negmrow", [1, 8 * MAXQB], BF16)
        biasneg = sb2("biasneg", [128, 8 * MAXQB], F32)
        dqb = sb2("dqb", [128, 512], BF16)
        mstr = [sb2(f"mstr{i}", [128, 5 * 128], BF16) for i in range(2)]
        PT = [sb2(f"PTl{i}", [128, 640], BF16) for i in range(2)]
        accst = sb2("accst", [65, 8 * 128], F32)
        lrec = sb2("lrec", [1, 8 * 128], F32)
        lntmp = sb2("lntmp", [1, 8 * 128], F32)
        rb_sb = sb2("rb_sb", [64, 8 * 128], F32)
        ost = [sb2(f"ost{i}", [64, 8 * 128], BF16) for i in range(2)]

        e2p = ExitStack()
        pb2p = lambda n, sh, dt: e2p.enter_context(nc.psum_tensor(f"{n}_l{l}", sh, dt))
        p_qkv = pb2p("p_qkv", [128, 3 * 512], F32)
        p_nrm = pb2p("p_nrm", [2, 512], F32)
        p_nb = pb2p("p_nb", [128, 512], F32)
        p_rot = pb2p("p_rot", [128, 512], F32)
        p_vt2 = pb2p("p_vt2", [128, 2 * 1024], BF16)

        c.op("gpsimd", lambda e, l=l: e.dma_start(
            out=wq[:].rearrange("p (c n) -> p c n", n=1536),
            in_=self.wqkv_i[l].rearrange("(c p) n -> p c n", p=128)), inc="g", n=16)
        c.wait("tensor", "g")

        zr = self.rr(z_dram)
        tabr = self.tabs_i.rearrange("p (f n) -> p f n", f=4)
        vko = Vkm[:].rearrange("p (k h d) -> p k h d", h=8, d=65)
        c.op("vector", lambda e: e.memset(vko[:, :, :, 64:65], 1.0), inc="v")
        c.op("vector", lambda e: e.memset(negm2[:], 0.0), inc="v")

        bank_free = {}
        zb_free = [0, 0]
        vt_free = [0, 0]
        for kt in range(nkt):
            t0 = ks_off + kt * 512
            tlen = min(512, TD - t0)
            bf = kt % 2
            gsem = "g0" if bf == 0 else "g1"
            c.wait("gpsimd", "v", zb_free[bf])
            c.op("gpsimd", lambda e, t0=t0, tlen=tlen, bf=bf: e.dma_start(
                out=zb[bf][:].rearrange("p (c n) -> p c n", n=513)[:, :, 0:tlen + 1],
                in_=zr[:, :, t0 - 1: t0 + tlen]), inc=gsem, n=16)
            c.op("gpsimd", lambda e, t0=t0, tlen=tlen, bf=bf: e.dma_start(
                out=tbt[bf][:].rearrange("p (f n) -> p f n", f=4)[:, :, 0:tlen],
                in_=tabr[:, :, t0: t0 + tlen]), inc=gsem, n=16)
            c.wait("tensor", gsem)
            for ec in range(12):
                bank = ec % 3
                if bank_free.get(bank):
                    c.wait("tensor", "v", bank_free[bank])
                for dc in range(8):
                    off = 1 if dc < 4 else 0
                    c.op("tensor", lambda e, ec=ec, dc=dc, bf=bf, bank=bank, off=off,
                         tlen=tlen: e.matmul(
                        p_qkv[:, bank * 512: bank * 512 + tlen],
                        wq[:, dc * 1536 + ec * 128: dc * 1536 + (ec + 1) * 128],
                        zb[bf][:, dc * 513 + off: dc * 513 + off + tlen],
                        start=(dc == 0), stop=(dc == 7)),
                        inc=("t" if dc == 7 else None))
                mm_t = c.counts["t"]
                c.wait("vector", "t", mm_t)
                ch = ec % 4
                pslice = p_qkv[:, bank * 512: bank * 512 + tlen]
                if ec >= 8:
                    c.op("vector", lambda e, ps=pslice, ch=ch, tlen=tlen: e.tensor_copy(
                        vstage[:, ch * 512: ch * 512 + tlen], ps), inc="v")
                    bank_free[bank] = c.counts["v"]
                    continue
                isq = ec < 4
                doff = (t0 - qs_off) if isq else (t0 - ks_off)
                c.op("vector", lambda e, ps=pslice, tlen=tlen: e.tensor_copy(
                    qraw[:, 0:tlen], ps), inc="v")
                bank_free[bank] = c.counts["v"]
                if isq and doff < 0:
                    continue
                c.op("vector", lambda e, tlen=tlen: e.tensor_tensor(
                    sqb_[:, 0:tlen], qraw[:, 0:tlen], qraw[:, 0:tlen], ALU.mult),
                    inc="v")
                sq_v = c.counts["v"]
                c.wait("tensor", "v", sq_v)
                c.op("tensor", lambda e, tlen=tlen: e.matmul(
                    p_nrm[:, 0:tlen], self.oneshh[:, 0:2], sqb_[:, 0:tlen],
                    start=True, stop=True), inc="t")
                c.wait("scalar", "t")
                c.op("scalar", lambda e, tlen=tlen: e.activation(
                    self.st0[0:2, 0:tlen], p_nrm[:, 0:tlen], AF.Ln, bias=1e-30),
                    inc="s")
                c.op("scalar", lambda e, tlen=tlen: e.activation(
                    rnb[:, 0:tlen], self.st0[0:2, 0:tlen], AF.Exp, scale=-0.5),
                    inc="s")
                c.wait("tensor", "s")
                c.op("tensor", lambda e, tlen=tlen: e.matmul(
                    p_nb[:, 0:tlen], self.selT[:, :], rnb[:, 0:tlen],
                    start=True, stop=True), inc="t")
                nb_t = c.counts["t"]
                c.wait("vector", "t", nb_t)
                sc_ap = self.qs8 if isq else self.ks1
                c.op("vector", lambda e, tlen=tlen, sc_ap=sc_ap, l=l: e.scalar_tensor_tensor(
                    qn[:, 0:tlen], qraw[:, 0:tlen], sc_ap[:, l:l + 1], p_nb[:, 0:tlen],
                    ALU.mult, ALU.mult), inc="v")
                qn_v = c.counts["v"]
                c.wait("tensor", "v", qn_v)
                c.op("tensor", lambda e, tlen=tlen: e.matmul(
                    p_rot[:, 0:tlen], self.RT[:, :], qn[:, 0:tlen],
                    start=True, stop=True), inc="t")
                rot_t = c.counts["t"]
                ci = 0 if isq else 2
                si = 1 if isq else 3
                c.wait("vector", "g0" if bf == 0 else "g1")
                c.op("vector", lambda e, bf=bf, ci=ci, tlen=tlen: e.tensor_tensor(
                    t1f[:, 0:tlen], qn[:, 0:tlen],
                    tbt[bf][:, ci * 512: ci * 512 + tlen], ALU.mult), inc="v")
                c.wait("vector", "t", rot_t)
                c.op("vector", lambda e, bf=bf, si=si, tlen=tlen: e.scalar_tensor_tensor(
                    uf[:, 0:tlen], p_rot[:, 0:tlen], 1.0,
                    tbt[bf][:, si * 512: si * 512 + tlen], ALU.mult, ALU.mult),
                    inc="v")
                dst, dT = (Qf, Tq) if isq else (Kf, Tk)
                c.op("vector", lambda e, ch=ch, dst=dst, dT=dT, doff=doff,
                     tlen=tlen: e.tensor_tensor(
                    dst[:, ch * dT + doff: ch * dT + doff + tlen],
                    t1f[:, 0:tlen], uf[:, 0:tlen], ALU.add), inc="v")
            zb_free[bf] = c.counts["v"]
            vs_v = c.counts["v"]
            c.wait("tensor", "v", vs_v)
            nsc = (tlen + 127) // 128
            for ch in range(4):
                for scn in range(nsc):
                    tbuf = (ch * nsc + scn) % 2
                    c.wait("tensor", "v", vt_free[tbuf])
                    c.op("tensor", lambda e, ch=ch, scn=scn, tbuf=tbuf: e.transpose(
                        p_vt2[:, tbuf * 1024: tbuf * 1024 + 128],
                        vstage[:, ch * 512 + scn * 128: ch * 512 + (scn + 1) * 128],
                        self.identb[:]), inc="t")
                    tr_t = c.counts["t"]
                    cc = kt * 4 + scn
                    c.wait("vector", "t", tr_t)
                    c.op("vector", lambda e, cc=cc, ch=ch, tbuf=tbuf: e.tensor_copy(
                        Vkm[:, cc * 520 + ch * 130: cc * 520 + (ch + 1) * 130]
                        .rearrange("p (h d) -> p h d", d=65)[:, :, 0:64],
                        p_vt2[:, tbuf * 1024: tbuf * 1024 + 128]
                        .rearrange("p (h d) -> p h d", d=64)), inc="v")
                    vt_free[tbuf] = c.counts["v"]

        ndt = (Tq + 511) // 512
        for ch in range(4):
            for dt_ in range(ndt):
                doff = dt_ * 512
                dlen = min(512, Tq - doff)
                nb = dlen // 128
                c.op("vector", lambda e, ch=ch, doff=doff, dlen=dlen: e.tensor_tensor(
                    dqb[:, 0:dlen], Qf[:, ch * Tq + doff: ch * Tq + doff + dlen],
                    Kf[:, ch * Tk + 512 + doff: ch * Tk + 512 + doff + dlen],
                    ALU.mult), inc="v")
                dq_v = c.counts["v"]
                c.wait("tensor", "v", dq_v)
                c.op("tensor", lambda e, dlen=dlen: e.matmul(
                    p_nrm[:, 0:dlen], self.oneshh[:, 0:2], dqb[:, 0:dlen],
                    start=True, stop=True), inc="t")
                mm_t = c.counts["t"]
                c.wait("vector", "t", mm_t)
                c.op("vector", lambda e, ch=ch, dt_=dt_, nb=nb: e.reduce_max(
                    negm2[0:2, ch * MAXQB + dt_ * 4: ch * MAXQB + dt_ * 4 + nb],
                    p_nrm[:, 0:nb * 128].rearrange("p (b n) -> p b n", n=128),
                    axis=AX.X), inc="v")
                c.wait("tensor", "v")
        c.op("vector", lambda e: e.tensor_scalar(
            negm2b[:], negm2[:], -1.0, None, ALU.mult), inc="v")
        c.wait("gpsimd", "v")
        c.op("gpsimd", lambda e: e.dma_start(
            out=self.negm_dram.rearrange("o (r pq) -> (o r) pq", r=2),
            in_=negm2b[:]), inc="g", n=16)
        c.wait("gpsimd", "g")
        c.op("gpsimd", lambda e: e.dma_start(
            out=negmrow[:].rearrange("o (p r q) -> o p r q", p=4, r=2),
            in_=self.negm_dram.rearrange("o (r p q) -> o p r q", r=2, p=4)),
            inc="g", n=16)
        c.wait("tensor", "g")
        c.op("tensor", lambda e: e.matmul(
            p_nb[:, 0:448], self.onesrow[0:1, 0:128], negmrow[0:1, 0:448],
            start=True, stop=True), inc="t")
        bt = c.counts["t"]
        c.wait("vector", "t", bt)
        c.op("vector", lambda e: e.tensor_copy(biasneg[:, 0:448], p_nb[:, 0:448]),
             inc="v")
        c.wait("tensor", "v")

        c.barrier()
        e2p.close()
        e3 = ExitStack()
        p_sml = [e3.enter_context(nc.psum_tensor(f"p_sml{i}_l{l}", [128, 640], F32))
                 for i in range(2)]
        p_acc = e3.enter_context(nc.psum_tensor(f"p_accl_l{l}", [65, 128], F32))
        p_rb = e3.enter_context(nc.psum_tensor(f"p_rbl_l{l}", [64, 1024], F32))

        sim_free = [0, 0]
        acc_free = 0
        rb_free = 0
        mb_free = [0, 0]
        ost_free = [0, 0]
        for qb in range(nqb):
            qoff = qb * 128
            mb = qb % 2
            if qb < nbnd:
                dsem = "d0" if mb == 0 else "d1"
                c.wait("sync", "t", mb_free[mb])
                c.op("sync", lambda e, l=l, qb=qb, mb=mb: e.dma_start(
                    out=mstr[mb][:].rearrange("p (k n) -> p k n", k=5),
                    in_=self.mbnd_i[mb_base + qb * 5: mb_base + (qb + 1) * 5]
                    .rearrange("k p n -> p k n")), inc=dsem, n=16)
                c.wait("tensor", dsem)
            for h in range(H):
                ch, rb = h // 2, 64 * (h % 2)
                jb = (qb * H + h) % 2
                c.wait("tensor", "s", sim_free[jb])
                simops = []
                for ko in range(5):
                    koff = qoff + ko * 128
                    masked = qb < nbnd or ko == 0 or ko == 4
                    simops.append(lambda e, jb=jb, ko=ko, ch=ch, rb=rb, koff=koff,
                                  qoff=qoff, masked=masked: e.matmul(
                        p_sml[jb][:, ko * 128:(ko + 1) * 128],
                        Kf[rb:rb + 64, ch * Tk + koff: ch * Tk + koff + 128],
                        Qf[rb:rb + 64, ch * Tq + qoff: ch * Tq + qoff + 128],
                        start=True, stop=(not masked)))
                    if qb < nbnd:
                        simops.append(lambda e, jb=jb, ko=ko, mb=mb: e.matmul(
                            p_sml[jb][:, ko * 128:(ko + 1) * 128],
                            self.identb[:], mstr[mb][:, ko * 128:(ko + 1) * 128],
                            start=False, stop=True))
                    elif ko == 0 or ko == 4:
                        mi = 0 if ko == 0 else 1
                        simops.append(lambda e, jb=jb, ko=ko, mi=mi: e.matmul(
                            p_sml[jb][:, ko * 128:(ko + 1) * 128],
                            self.identb[:], self.mreg[:, mi * 128:(mi + 1) * 128],
                            start=False, stop=True))
                for fi, f in enumerate(simops):
                    c.op("tensor", f, inc=("t" if fi == len(simops) - 1 else None))
                sim_t = c.counts["t"]
                if qb < nbnd:
                    mb_free[mb] = sim_t
                c.wait("scalar", "t", sim_t)
                c.op("scalar", lambda e, jb=jb, h=h, qb=qb: e.activation(
                    PT[jb][:], p_sml[jb][:], AF.Exp,
                    bias=biasneg[:, h * MAXQB + qb: h * MAXQB + qb + 1],
                    scale=1.0), inc="s")
                exp_s = c.counts["s"]
                sim_free[jb] = exp_s
                c.wait("tensor", "s", exp_s)
                c.wait("tensor", "v", acc_free)
                for ko in range(5):
                    kc = qb + ko
                    c.op("tensor", lambda e, jb=jb, ko=ko, h=h, kc=kc,
                         st=(ko == 0), sp=(ko == 4): e.matmul(
                        p_acc[:], Vkm[:, kc * 520 + h * 65: kc * 520 + h * 65 + 65],
                        PT[jb][:, ko * 128:(ko + 1) * 128],
                        start=st, stop=sp), inc=("t" if ko == 4 else None))
                av_t = c.counts["t"]
                c.wait("vector", "t", av_t)
                c.op("vector", lambda e, h=h: e.tensor_copy(
                    accst[:, h * 128:(h + 1) * 128], p_acc[:]), inc="v")
                acc_free = c.counts["v"]
            c.op("vector", lambda e: e.tensor_copy(
                lrec[:], accst[64:65, :]), inc="v")
            lr_v = c.counts["v"]
            c.wait("scalar", "v", lr_v)
            c.op("scalar", lambda e: e.activation(lntmp[:], lrec[:], AF.Ln), inc="s")
            c.op("scalar", lambda e: e.activation(lrec[:], lntmp[:], AF.Exp,
                                                  scale=-1.0), inc="s")
            rl_s = c.counts["s"]
            c.wait("tensor", "s", rl_s)
            c.wait("tensor", "v", rb_free)
            c.op("tensor", lambda e: e.matmul(
                p_rb[:, 0:512], self.onesrow_f[:], lrec[0:1, 0:512],
                start=True, stop=True))
            c.op("tensor", lambda e: e.matmul(
                p_rb[:, 512:1024], self.onesrow_f[:], lrec[0:1, 512:1024],
                start=True, stop=True), inc="t")
            bc_t = c.counts["t"]
            c.wait("vector", "t", bc_t)
            c.op("vector", lambda e: e.tensor_copy(rb_sb[:], p_rb[:]), inc="v")
            rb_free = c.counts["v"]
            ob2 = qb % 2
            c.wait("vector", "e0" if ob2 == 0 else "e1", ost_free[ob2])
            c.op("vector", lambda e, ob2=ob2: e.scalar_tensor_tensor(
                ost[ob2][0:64, :].rearrange("p (h n) -> p h n", h=8),
                accst[0:64, :].rearrange("p (h n) -> p h n", h=8),
                1.0,
                rb_sb[:].rearrange("p (h n) -> p h n", h=8),
                ALU.mult, ALU.mult), inc="v")
            acc_free = c.counts["v"]
            c.wait("sync", "v")
            esm = "e0" if ob2 == 0 else "e1"
            c.op("sync", lambda e, qoff=qoff, ob2=ob2: e.dma_start(
                out=self.sc_o.rearrange("(h d) n -> d h n", d=64)[
                    :, :, qs_off + qoff: qs_off + qoff + 128],
                in_=ost[ob2][0:64, :].rearrange("p (h n) -> p h n", h=8)),
                inc=esm, n=16)
            ost_free[ob2] = c.counts[esm]

        c.wait("sync", "e0", c.counts["e0"])
        c.wait("sync", "e1", c.counts["e1"])
        c.barrier()
        e3.close()

        e2.close()

        e4 = ExitStack()
        sb4 = lambda n, sh, dt: e4.enter_context(nc.sbuf_tensor(f"{n}_w{l}", sh, dt))
        zq = sb4("zq", [128, 8 * 512], F32)
        otile = sb4("otile", [128, 4 * 512], BF16)
        xmo = sb4("xmo", [128, 8 * 512], F32)
        woB = sb4("woB", [128, 4 * 1024], BF16)
        p_wo = e4.enter_context(nc.psum_tensor(f"p_wo_l{l}", [128, 3 * 512], F32))
        c.op("gpsimd", lambda e, l=l: e.dma_start(
            out=woB[:].rearrange("p (c n) -> p c n", n=1024),
            in_=self.wo_i[l].rearrange("(c p) n -> p c n", p=128)), inc="g", n=16)
        c.wait("tensor", "g")
        wo_bank_free = {}
        zq_free = 0
        ot_free = 0
        xmo_dma = [0, 0]
        nwt = (Tq + 511) // 512
        flag_tile = (1535 - qs_off) // 512
        flag_col = (1535 - qs_off) % 512
        o_pk = self.sc_o.rearrange("(c p) n -> p c n", p=128)
        for wt in range(nwt):
            woff = wt * 512
            wlen = min(512, Tq - woff)
            c.wait("sync", "v", zq_free)
            c.op("sync", lambda e, woff=woff, wlen=wlen: e.dma_start(
                out=zq[:].rearrange("p (c n) -> p c n", n=512)[:, :, 0:wlen],
                in_=zr[:, :, qs_off + woff: qs_off + woff + wlen]), inc="d0", n=16)
            c.wait("sync", "t", ot_free)
            c.op("sync", lambda e, woff=woff, wlen=wlen: e.dma_start(
                out=otile[:].rearrange("p (c n) -> p c n", n=512)[:, :, 0:wlen],
                in_=o_pk[:, :, qs_off + woff: qs_off + woff + wlen]), inc="d1", n=16)
            c.wait("vector", "d0")
            c.wait("tensor", "d0")
            c.wait("tensor", "d1")
            c.wait("vector", "e0" if wt % 2 == 1 else "e1", xmo_dma[1 - wt % 2])
            for dc in range(8):
                bank = dc % 3
                if wo_bank_free.get(bank):
                    c.wait("tensor", "v", wo_bank_free[bank])
                for ch in range(4):
                    c.op("tensor", lambda e, dc=dc, ch=ch, wlen=wlen,
                         bank=bank: e.matmul(
                        p_wo[:, bank * 512: bank * 512 + wlen],
                        woB[:, ch * 1024 + dc * 128: ch * 1024 + (dc + 1) * 128],
                        otile[:, ch * 512: ch * 512 + wlen],
                        start=(ch == 0), stop=(ch == 3)),
                        inc=("t" if ch == 3 else None))
                mm_t = c.counts["t"]
                c.wait("vector", "t", mm_t)
                c.op("vector", lambda e, dc=dc, bank=bank, wlen=wlen: e.scalar_tensor_tensor(
                    xmo[:, dc * 512: dc * 512 + wlen],
                    p_wo[:, bank * 512: bank * 512 + wlen], 1.0,
                    zq[:, dc * 512: dc * 512 + wlen], ALU.mult, ALU.add), inc="v")
                wo_bank_free[bank] = c.counts["v"]
            zq_free = c.counts["v"]
            ot_free = c.counts["t"]
            if wt == flag_tile:
                c.op("vector", lambda e, flag_col=flag_col: e.tensor_scalar(
                    xmo[:].rearrange("p (c n) -> p c n", n=512)[:, :, flag_col:flag_col + 1],
                    xmo[:].rearrange("p (c n) -> p c n", n=512)[:, :, flag_col:flag_col + 1],
                    self.flag[:, 0:1], None, ALU.mult), inc="v")
            c.wait("sync", "v")
            esem = "e0" if wt % 2 == 0 else "e1"
            c.op("sync", lambda e, woff=woff, wlen=wlen: e.dma_start(
                out=self.rr(xmid_dram)[:, :, qs_off + woff: qs_off + woff + wlen],
                in_=xmo[:].rearrange("p (c n) -> p c n", n=512)[:, :, 0:wlen]),
                inc=esem, n=16)
            xmo_dma[wt % 2] = c.counts[esem]
        c.wait("sync", "e0", c.counts["e0"])
        c.wait("sync", "e1", c.counts["e1"])
        c.barrier()
        e4.close()

    def ffn_part(self, l, xmid_dram, xout_dram):
        nc, c = self.nc, self.c
        cs_off = CS[l]
        Tc = TD - cs_off
        nt = (Tc + 511) // 512

        e2 = ExitStack()
        sb2 = lambda n, sh, dt: e2.enter_context(nc.sbuf_tensor(f"{n}_f{l}", sh, dt))
        pb2 = lambda n, sh, dt: e2.enter_context(nc.psum_tensor(f"{n}_f{l}", sh, dt))
        w1 = sb2("w1", [128, 8 * 5632], BF16)
        w2 = sb2("w2", [128, 22 * 1024], BF16)
        xmb = sb2("xmb", [128, 8 * 513], BF16)
        xmf = sb2("xmf", [128, 512], F32)
        h2b = sb2("h2b", [128, 512], BF16)
        xnb = sb2("xnb", [128, 8 * 512], BF16)
        gbuf = sb2("gbuf", [128, 512], BF16)
        h4 = sb2("h4", [128, 22 * 512], BF16)
        xob = sb2("xob", [128, 512], F32)
        shg = sb2("shg", [128, 512], F32)
        p_a = pb2("p_a", [128, 2 * 512], F32)
        p_g = pb2("p_g", [128, 2 * 512], F32)
        p_s1 = pb2("p_s1f", [128, 512], F32)
        p_s2 = pb2("p_s2f", [128, 512], F32)
        p_o = pb2("p_of", [128, 2 * 512], F32)

        c.op("gpsimd", lambda e, l=l: e.dma_start(
            out=w1[:].rearrange("p (c n) -> p c n", n=5632),
            in_=self.w1_i[l].rearrange("(c p) n -> p c n", p=128)), inc="g", n=16)
        c.op("gpsimd", lambda e, l=l: e.dma_start(
            out=w2[:].rearrange("p (c n) -> p c n", n=1024),
            in_=self.w2_i[l].rearrange("(c p) n -> p c n", p=128)), inc="g", n=16)
        c.wait("tensor", "g")

        xmr = self.rr(xmid_dram)
        xor_ = self.rr(xout_dram)
        st0, st1, st2 = self.st0, self.st1, self.st2
        g_ = self.lng
        b_ = self.lnb

        tile_free = 0
        tile_free_t = 0
        stat_s = 0
        stat_v = 0
        ab_free = [0, 0]
        gb_free = [0, 0]
        o_free = {}
        xmf_free = [0, 0]
        xob_dma = [0, 0]
        last_h4_v = 0
        for t in range(nt):
            toff = cs_off + t * 512
            tlen = min(512, TD - toff)
            c.wait("gpsimd", "v", tile_free)
            c.wait("gpsimd", "t", tile_free_t)
            c.op("gpsimd", lambda e, toff=toff, tlen=tlen: e.dma_start(
                out=xmb[:].rearrange("p (c n) -> p c n", n=513)[:, :, 0:tlen + 1],
                in_=xmr[:, :, toff - 1: toff + tlen]), inc="g0", n=16)
            c.wait("vector", "g0")
            c.wait("tensor", "g0")
            c.wait("tensor", "s", stat_s)
            c.wait("tensor", "v", stat_v)
            for dc in range(8):
                off = 1 if dc < 4 else 0
                c.op("tensor", lambda e, dc=dc, off=off, tlen=tlen: e.matmul(
                    p_s1[:, 0:tlen], self.onesb[:],
                    xmb[:, dc * 513 + off: dc * 513 + off + tlen],
                    start=(dc == 0), stop=(dc == 7)), inc=("t" if dc == 7 else None))
            for dc in range(8):
                off = 1 if dc < 4 else 0
                if dc > 0:
                    c.wait("vector", "t")
                c.op("vector", lambda e, dc=dc, off=off, tlen=tlen: e.tensor_tensor(
                    h2b[:, 0:tlen],
                    xmb[:, dc * 513 + off: dc * 513 + off + tlen],
                    xmb[:, dc * 513 + off: dc * 513 + off + tlen], ALU.mult), inc="v")
                h2_v = c.counts["v"]
                c.wait("tensor", "v", h2_v)
                c.op("tensor", lambda e, dc=dc, tlen=tlen: e.matmul(
                    p_s2[:, 0:tlen], self.onesb[:], h2b[:, 0:tlen],
                    start=(dc == 0), stop=(dc == 7)), inc="t")
            ln_chain(c, st0, st1, st2, self.s_bc, self.sh_bc, p_s1, p_s2)
            stat_s = c.counts["s"]
            for dc in range(8):
                off = 1 if dc < 4 else 0
                c.op("vector", lambda e, dc=dc, l=l: e.tensor_scalar(
                    shg[:], self.sh_bc[:],
                    g_[:, l * 8 + dc: l * 8 + dc + 1],
                    b_[:, l * 8 + dc: l * 8 + dc + 1],
                    ALU.mult, ALU.add), inc="v")
                c.op("vector", lambda e, dc=dc, off=off, l=l: e.scalar_tensor_tensor(
                    xnb[:, dc * 512:(dc + 1) * 512],
                    xmb[:, dc * 513 + off: dc * 513 + off + 512],
                    g_[:, l * 8 + dc: l * 8 + dc + 1], self.s_bc[:],
                    ALU.mult, ALU.mult), inc="v")
                c.op("vector", lambda e, dc=dc: e.tensor_tensor(
                    xnb[:, dc * 512:(dc + 1) * 512], xnb[:, dc * 512:(dc + 1) * 512],
                    shg[:], ALU.add), inc="v")
            xn_v = c.counts["v"]
            stat_v = xn_v
            tile_free = c.counts["v"]
            tile_free_t = c.counts["t"]
            c.wait("tensor", "v", xn_v)
            for pr in range(22):
                pa = pr % 2
                if ab_free[pa]:
                    c.wait("tensor", "s", gb_free[pa])
                    c.wait("tensor", "v", ab_free[pa])
                for dc in range(8):
                    c.op("tensor", lambda e, pr=pr, dc=dc, pa=pa, tlen=tlen: e.matmul(
                        p_a[:, pa * 512: pa * 512 + tlen],
                        w1[:, dc * 5632 + pr * 256: dc * 5632 + pr * 256 + 128],
                        xnb[:, dc * 512: dc * 512 + tlen],
                        start=(dc == 0), stop=(dc == 7)))
                for dc in range(8):
                    c.op("tensor", lambda e, pr=pr, dc=dc, pa=pa, tlen=tlen: e.matmul(
                        p_g[:, pa * 512: pa * 512 + tlen],
                        w1[:, dc * 5632 + pr * 256 + 128: dc * 5632 + pr * 256 + 256],
                        xnb[:, dc * 512: dc * 512 + tlen],
                        start=(dc == 0), stop=(dc == 7)),
                        inc=("t" if dc == 7 else None))
                mm_t = c.counts["t"]
                c.wait("scalar", "t", mm_t)
                c.wait("scalar", "v", last_h4_v)
                c.op("scalar", lambda e, pr=pr, pa=pa, tlen=tlen, l=l: e.activation(
                    gbuf[:, 0:tlen], p_g[:, pa * 512: pa * 512 + tlen], AF.Gelu,
                    bias=self.b1col[:, l * 44 + pr * 2 + 1: l * 44 + pr * 2 + 2]),
                    inc="s")
                gb_free[pa] = c.counts["s"]
                gl_s = c.counts["s"]
                c.wait("vector", "s", gl_s)
                c.op("vector", lambda e, pr=pr, pa=pa, tlen=tlen, l=l: e.scalar_tensor_tensor(
                    h4[:, pr * 512: pr * 512 + tlen],
                    p_a[:, pa * 512: pa * 512 + tlen],
                    self.b1col[:, l * 44 + pr * 2: l * 44 + pr * 2 + 1],
                    gbuf[:, 0:tlen],
                    ALU.add, ALU.mult), inc="v")
                ab_free[pa] = c.counts["v"]
                last_h4_v = c.counts["v"]
            h4_v = c.counts["v"]
            h4_v = c.counts["v"]
            c.wait("tensor", "v", h4_v)
            for dc in range(8):
                ob = dc % 2
                xs_sem = "e0" if ob == 0 else "e1"
                c.wait("sync", "v", xmf_free[0])
                c.op("sync", lambda e, dc=dc, toff=toff, tlen=tlen: e.dma_start(
                    out=xmf[:, 0:tlen],
                    in_=xmr[:, dc, toff: toff + tlen]), inc="d1", n=16)
                if o_free.get(ob):
                    c.wait("tensor", "v", o_free[ob])
                for kc in range(22):
                    c.op("tensor", lambda e, dc=dc, kc=kc, ob=ob, tlen=tlen: e.matmul(
                        p_o[:, ob * 512: ob * 512 + tlen],
                        w2[:, kc * 1024 + dc * 128: kc * 1024 + (dc + 1) * 128],
                        h4[:, kc * 512: kc * 512 + tlen],
                        start=(kc == 0), stop=(kc == 21)),
                        inc=("t" if kc == 21 else None))
                mm_t = c.counts["t"]
                c.wait("vector", "t", mm_t)
                c.wait("vector", "d1")
                c.wait("vector", "e0" if ob == 1 else "e1", xob_dma[1 - ob])
                c.op("vector", lambda e, dc=dc, ob=ob, tlen=tlen, l=l: e.scalar_tensor_tensor(
                    xob[:, 0:tlen],
                    p_o[:, ob * 512: ob * 512 + tlen],
                    self.b2col[:, l * 8 + dc: l * 8 + dc + 1],
                    xmf[:, 0:tlen], ALU.add, ALU.add),
                    inc="v")
                xmf_free[0] = c.counts["v"]
                o_free[ob] = c.counts["v"]
                if toff <= 1535 < toff + tlen:
                    fc = 1535 - toff
                    c.op("vector", lambda e, fc=fc: e.tensor_scalar(
                        xob[:, fc: fc + 1], xob[:, fc: fc + 1],
                        self.flag[:, 0:1], None, ALU.mult), inc="v")
                    xmf_free[0] = c.counts["v"]
                    o_free[ob] = c.counts["v"]
                c.wait("sync", "v")
                c.op("sync", lambda e, dc=dc, toff=toff, tlen=tlen, xs_sem=xs_sem: e.dma_start(
                    out=xor_[:, dc, toff: toff + tlen],
                    in_=xob[:, 0:tlen]), inc=xs_sem, n=16)
                xob_dma[ob] = c.counts[xs_sem]
        c.wait("sync", "e0", c.counts["e0"])
        c.wait("sync", "e1", c.counts["e1"])
        c.barrier()
        e2.close()

    def final_stage(self):
        nc, c = self.nc, self.c
        e2 = ExitStack()
        sb2 = lambda n, sh, dt: e2.enter_context(nc.sbuf_tensor(n, sh, dt))
        pb2 = lambda n, sh, dt: e2.enter_context(nc.psum_tensor(n, sh, dt))
        xo_t = [sb2(f"xoF{i}", [128, 8 * 512], F32) for i in range(2)]
        xt_t = [sb2(f"xtF{i}", [128, 8 * 512], F32) for i in range(2)]
        xb_ = sb2("xbF", [128, 8 * 512], BF16)
        x2_ = sb2("x2F", [128, 8 * 512], BF16)
        fo = sb2("foF", [128, 8 * 512], F32)
        p_s1 = pb2("p_s1F", [128, 512], F32)
        p_s2 = pb2("p_s2F", [128, 512], F32)
        st0, st1, st2 = self.st0, self.st1, self.st2
        xout_r = self.rr(self.sc_xout[1])
        xtld_r = self.rr(self.sc_xtld)
        out_r = self.outT.rearrange("(c p) n -> p c n", p=128)
        tile_free = [0, 0]
        stat_s = 0
        stat_v = 0
        fo_dma = [0, 0]
        for t in range(4):
            toff = 1536 + t * 512
            bf = t % 2
            dsem = "d0" if bf == 0 else "d1"
            c.wait("sync", "v", tile_free[bf])
            c.op("sync", lambda e, toff=toff, bf=bf: e.dma_start(
                out=xo_t[bf][:].rearrange("p (c n) -> p c n", n=512),
                in_=xout_r[:, :, toff: toff + 512]), inc=dsem, n=16)
            c.op("sync", lambda e, toff=toff, bf=bf: e.dma_start(
                out=xt_t[bf][:].rearrange("p (c n) -> p c n", n=512),
                in_=xtld_r[:, :, toff: toff + 512]), inc=dsem, n=16)
            c.wait("vector", dsem)
            c.op("vector", lambda e, bf=bf: e.tensor_copy(xb_[:], xo_t[bf][:]), inc="v")
            c.op("vector", lambda e: e.tensor_tensor(x2_[:], xb_[:], xb_[:],
                                                     ALU.mult), inc="v")
            x2_v = c.counts["v"]
            c.wait("tensor", "v", x2_v)
            c.wait("tensor", "s", stat_s)
            c.wait("tensor", "v", stat_v)
            for dc in range(8):
                c.op("tensor", lambda e, dc=dc: e.matmul(
                    p_s1[:], self.onesb[:], xb_[:, dc * 512:(dc + 1) * 512],
                    start=(dc == 0), stop=(dc == 7)))
            for dc in range(8):
                c.op("tensor", lambda e, dc=dc: e.matmul(
                    p_s2[:], self.onesb[:], x2_[:, dc * 512:(dc + 1) * 512],
                    start=(dc == 0), stop=(dc == 7)),
                    inc=("t" if dc == 7 else None))
            ln_chain(c, st0, st1, st2, self.s_bc, self.sh_bc, p_s1, p_s2)
            stat_s = c.counts["s"]
            stat_v = c.counts["v"]
            c.wait("vector", "o0" if bf == 1 else "o1", fo_dma[1 - bf])
            c.op("vector", lambda e, bf=bf: e.scalar_tensor_tensor(
                fo[:].rearrange("p (c n) -> p c n", n=512),
                xo_t[bf][:].rearrange("p (c n) -> p c n", n=512), 1.0,
                rep8(self.s_bc[:]), ALU.mult, ALU.mult), inc="v")
            c.op("vector", lambda e: e.tensor_tensor(
                fo[:].rearrange("p (c n) -> p c n", n=512),
                fo[:].rearrange("p (c n) -> p c n", n=512), rep8(self.sh_bc[:]),
                ALU.add), inc="v")
            c.op("vector", lambda e, bf=bf: e.tensor_tensor(
                fo[:], fo[:], xt_t[bf][:], ALU.add), inc="v")
            tile_free[bf] = c.counts["v"]
            c.wait("sync", "v")
            osem = "o0" if bf == 0 else "o1"
            c.op("sync", lambda e, t=t: e.dma_start(
                out=out_r[:, :, t * 512:(t + 1) * 512],
                in_=fo[:].rearrange("p (c n) -> p c n", n=512)), inc=osem, n=16)
            fo_dma[bf] = c.counts[osem]
        e2.close()


def build_fused():
    nc = bass.Bass("TRN2", target_bir_lowering=False, debug=False)
    es = ExitStack()
    sb = lambda n, shp, dt: es.enter_context(nc.sbuf_tensor(n, shp, dt))
    di = lambda n, shp, dt: nc.dram_tensor(n, shp, dt, kind="ExternalInput").ap()
    with es:
        c = Ctx(nc)
        for s in ("xc", "xw", "d0", "d1", "e0", "e1", "g", "g0", "g1", "t",
                  "s", "v", "o0", "o1", "cc", "cp"):
            c.sem(es, s)

        identb_i = di("identb", [128, 128], BF16)
        onesb_i = di("onesb", [128, 128], BF16)
        oneshh_i = di("oneshh", [128, 2], BF16)
        selT_i = di("selT", [2, 128], BF16)
        onesrow_i = di("onesrow", [1, 512], BF16)
        onesrowf_i = di("onesrow_f", [1, 64], F32)
        csts_i = di("csts", [128, 8], F32)

        sh = {
            "identb": sb("identb_s", [128, 128], BF16),
            "onesb": sb("onesb_s", [128, 128], BF16),
            "oneshh": sb("oneshh_s", [128, 2], BF16),
            "selT": sb("selT_s", [2, 128], BF16),
            "onesrow": sb("onesrow_s", [1, 512], BF16),
            "onesrow_f": sb("onesrowf_s", [1, 64], F32),
        }
        cst = sb("cst_s", [128, 8], F32)
        zf = sb("zf1536", [128, 1536], BF16)
        nc.const_aps.register(F32, 1.0 / 1024.0, cst[:, 0:1])
        nc.const_aps.register(F32, 1e-5, cst[:, 1:2])
        nc.const_aps.register(F32, -0.5, cst[:, 2:3])
        nc.const_aps.register(F32, -1.0, cst[:, 3:4])
        nc.const_aps.register(F32, 1e-30, cst[:, 4:5])
        for dst, src in [(sh["identb"], identb_i), (sh["onesb"], onesb_i),
                         (sh["oneshh"], oneshh_i), (sh["selT"], selT_i),
                         (sh["onesrow"], onesrow_i),
                         (sh["onesrow_f"], onesrowf_i), (cst, csts_i)]:
            c.op("sync", lambda e, d=dst, s=src: e.dma_start(out=d[:], in_=s),
                 inc="xc", n=16)
        c.op("vector", lambda e: e.memset(zf[:], 0.0), inc="v")
        zf_v = c.counts["v"]
        for eng in ("vector", "tensor", "scalar", "gpsimd"):
            c.wait(eng, "xc")

        agin = nc.dram_tensor("agin", [256, N], BF16, kind="Internal").ap()
        agout = nc.dram_tensor("agout", [1024, N], BF16, kind="Internal").ap()
        attw = nc.dram_tensor("attw", [1024, TD], BF16, kind="Internal").ap()

        phase1(nc, c, sh, agin)

        attw_r = attw.rearrange("(c p) n -> p c n", p=128)
        c.wait("sync", "v", zf_v)
        for ch in range(8):
            c.op("sync", lambda e, ch=ch: e.dma_start(
                out=attw_r[:, ch, 0:1536], in_=zf[:]), inc="cp", n=16)
        c.wait("gpsimd", "o0", c.counts["o0"])
        c.wait("gpsimd", "o1", c.counts["o1"])
        c.op("gpsimd", lambda e: e.collective_compute(
            "AllGather", mybir.AluOpType.bypass,
            replica_groups=[[0, 1, 2, 3], [4, 5, 6, 7]],
            ins=[agin.opt()], outs=[agout.opt()]), inc="cc", n=1)
        c.wait("gpsimd", "cc")
        c.wait("gpsimd", "cp")
        cp_sem = c.sems["cp"]

        def emit_switch(e):
            for arm in e.switch_core_id(n=128):
                w = (arm.logical % 8) % 4
                S = 2048 * w
                if w == 0:
                    inst = e.dma_start(out=attw[:, 1536:3584],
                                       in_=agout[:, 0:2048])
                else:
                    inst = e.dma_start(out=attw[:, 0:3584],
                                       in_=agout[:, S - 1536:S + 2048])
                inst.then_inc(cp_sem, 16)
        c.raw("gpsimd", emit_switch, inc="cp", n=16)
        c.barrier()

        l2 = L2(nc)
        l2.build(c, sh, attw)

        c.wait("sync", "o0", c.counts["o0"])
        c.wait("sync", "o1", c.counts["o1"])
        with nc.Block() as block:
            c.finalize(block)
    return nc


BF = ml_dtypes.bfloat16


def make_tables(S):
    t = (S - 1536 + np.arange(TD)).astype(np.float64)
    half_idx = np.arange(0, 64, 2, dtype=np.float64)
    inv_freq = 1.0 / 10000.0 ** (half_idx / 64.0)
    freqs = t[:, None] * inv_freq[None, :]
    freqs64 = np.concatenate([freqs, freqs], axis=1)
    base = (half_idx + 0.4 * 64) / (1.4 * 64)
    power = (t - 4096.0) / 256.0
    scale = base[None, :] ** power[:, None]
    scale64 = np.concatenate([scale, scale], axis=1)
    cos, sin = np.cos(freqs64), np.sin(freqs64)
    cs = (cos * scale64).T
    ss = (sin * scale64).T
    ci = (cos / scale64).T
    si = (sin / scale64).T
    out = np.zeros((128, 4 * TD), dtype=BF)
    for i, tab in enumerate([cs, ss, ci, si]):
        tt = np.tile(tab, (2, 1))
        out[:, i * TD:(i + 1) * TD] = tt.astype(BF)
    return out


def make_RT():
    R = np.zeros((128, 128), dtype=np.float32)
    for hb in (0, 64):
        for i in range(32):
            R[hb + i, hb + 32 + i] = -1.0
            R[hb + 32 + i, hb + i] = 1.0
    return np.ascontiguousarray(R.T).astype(BF)


def make_masks(S):
    big = np.float32(-1e30)
    row = np.arange(128)[:, None]
    qq = np.arange(128)[None, :]
    M0 = np.where(qq <= row, 0.0, big).astype(BF)
    M4 = np.where(qq >= row, 0.0, big).astype(BF)
    mreg = np.concatenate([M0, M4], axis=1)
    tiles = []
    for l in range(2):
        for qb in range(NBND[l]):
            qs_abs = S + (QS[l] - 1536) + 128 * qb
            for ko in range(5):
                j_abs = qs_abs - 512 + ko * 128 + row
                q_abs = qs_abs + qq
                valid = (j_abs >= 0) & (j_abs <= q_abs) & (q_abs <= j_abs + 512)
                valid = valid | (q_abs < 0)
                tiles.append(np.where(valid, 0.0, big).astype(BF))
    return mreg, np.stack(tiles)


def slice_pad(arr2d, S, dtype):
    out = np.zeros((1024, TD), dtype=dtype)
    lo, hi = S - 1536, S + 2048
    src_lo, src_hi = max(lo, 0), min(hi, 8192)
    out[:, src_lo - lo: src_hi - lo] = arr2d[:, src_lo:src_hi].astype(dtype)
    return out


def l2_in_maps(inp):
    x = np.asarray(inp["L0_attn_x"], dtype=np.float32)
    w_out = np.asarray(inp["L0_attn_w_out"], dtype=np.float32)
    b_out = np.asarray(inp["L0_attn_b_out"], dtype=np.float32)
    wqkv = np.asarray(inp["L0_ffn_wqkv"], dtype=np.float32)
    q_scale = np.asarray(inp["L0_ffn_q_scale"], dtype=np.float32)
    k_scale = np.asarray(inp["L0_ffn_k_scale"], dtype=np.float32)
    wo = np.asarray(inp["L0_ffn_wo"], dtype=np.float32)
    ln_g = np.asarray(inp["L0_ffn_ln_g"], dtype=np.float32)
    ln_b = np.asarray(inp["L0_ffn_ln_b"], dtype=np.float32)
    w1 = np.asarray(inp["L0_ffn_w1"], dtype=np.float32)
    b1 = np.asarray(inp["L0_ffn_b1"], dtype=np.float32)
    w2 = np.asarray(inp["L0_ffn_w2"], dtype=np.float32)
    b2 = np.asarray(inp["L0_ffn_b2"], dtype=np.float32)

    woutT = np.ascontiguousarray(w_out.T)
    w1r = np.zeros((2, 1024, 5632), dtype=np.float32)
    b1r = np.zeros((2, 1, 5632), dtype=np.float32)
    for l in range(2):
        a, g = w1[l][:, 0:2730], w1[l][:, 2730:5460]
        ba, bg = b1[l][0:2730], b1[l][2730:5460]
        for pr in range(22):
            n = min(128, 2730 - pr * 128)
            w1r[l][:, pr * 256: pr * 256 + n] = a[:, pr * 128: pr * 128 + n]
            w1r[l][:, pr * 256 + 128: pr * 256 + 128 + n] = g[:, pr * 128: pr * 128 + n]
            b1r[l][0, pr * 256: pr * 256 + n] = ba[pr * 128: pr * 128 + n]
            b1r[l][0, pr * 256 + 128: pr * 256 + 128 + n] = bg[pr * 128: pr * 128 + n]
    w2p = np.zeros((2, 2816, 1024), dtype=np.float32)
    w2p[:, 0:2730] = w2
    b2row = b2[:, None, :]
    b1col = np.zeros((128, 88), dtype=np.float32)
    b2col = np.zeros((128, 16), dtype=np.float32)
    for l in range(2):
        for pr in range(22):
            b1col[:, l * 44 + pr * 2] = b1r[l][0, pr * 256: pr * 256 + 128]
            b1col[:, l * 44 + pr * 2 + 1] = b1r[l][0, pr * 256 + 128: pr * 256 + 256]
        for dc in range(8):
            b2col[:, l * 8 + dc] = b2[l][dc * 128:(dc + 1) * 128]
    lng = np.zeros((128, 16), dtype=np.float32)
    lnb = np.zeros((128, 16), dtype=np.float32)
    for l in range(2):
        lng[:, l * 8:(l + 1) * 8] = ln_g[l].reshape(8, 128).T
        lnb[:, l * 8:(l + 1) * 8] = ln_b[l].reshape(8, 128).T
    qs8 = np.stack([np.tile(q_scale[l] * 8.0, 2) for l in range(2)], axis=1)
    ks1 = np.stack([np.tile(k_scale[l], 2) for l in range(2)], axis=1)
    RT = make_RT()
    identb = np.eye(128, dtype=BF)
    onesb = np.ones((128, 128), dtype=BF)
    oneshh = np.zeros((128, 2), dtype=BF)
    oneshh[0:64, 0] = 1; oneshh[64:128, 1] = 1
    selT = np.ascontiguousarray(oneshh.T)
    onesrow = np.ones((1, 512), dtype=BF)
    onesrow_f = np.ones((1, 64), dtype=np.float32)
    csts = np.broadcast_to(np.array(
        [1.0 / 1024.0, 1e-5, -0.5, -1.0, 1e-30, 0, 0, 0], dtype=np.float32),
        (128, 8)).copy()

    in_maps = []
    for core in range(8):
        b, r = core // 4, core % 4
        S = 2048 * r
        mreg, mbnd = make_masks(S)
        flag = np.full((128, 1), 0.0 if r == 0 else 1.0, dtype=np.float32)
        in_maps.append({
            "xTwin": slice_pad(np.ascontiguousarray(x[b].T), S, np.float32),
            "woutT": woutT, "boutrow": b_out[None, :],
            "wqkv": wqkv, "wo": wo, "w1r": w1r, "b1r": b1r, "w2p": w2p,
            "b2row": b2row, "b1col": b1col, "b2col": b2col,
            "lng": lng, "lnb": lnb, "qs8": qs8, "ks1": ks1,
            "tabs": make_tables(S), "RT": RT, "identb": identb, "onesb": onesb,
            "oneshh": oneshh, "selT": selT, "onesrow": onesrow,
            "onesrow_f": onesrow_f, "mreg": mreg, "mbnd": mbnd,
            "flag": flag, "csts": csts,
        })
    return in_maps




def kernel(**inputs):
    ims1 = l1_in_maps(inputs)
    ims2 = l2_in_maps(inputs)
    in_maps = []
    for core in range(8):
        m = dict(ims1[core])
        m.update(ims2[core])
        in_maps.append(m)
    nc = build_fused()
    res = run_bass_kernel_spmd(nc, in_maps, core_ids=list(range(8)))
    out = np.empty((2, 8192, 1024), dtype=np.float32)
    for core in range(8):
        b, r = core // 4, core % 4
        S = 2048 * r
        out[b, S:S + 2048, :] = np.asarray(res.results[core]["outT"]).T
    return out


# version 06
# speedup vs baseline: 1.1749x; 1.0144x over previous
import numpy as np
import ml_dtypes
from contextlib import ExitStack
import concourse.bass as bass
import concourse.mybir as mybir
from concourse.bass_utils import run_bass_kernel_spmd

F32 = mybir.dt.float32
BF16 = mybir.dt.bfloat16
AF = mybir.ActivationFunctionType
ALU = mybir.AluOpType
AX = mybir.AxisListType
ENGS = ("sync", "gpsimd", "tensor", "scalar", "vector")

N = 8192
D = 1024
NT = 16
TT = 512


class _SkipRest(Exception):
    pass


class Ctx:
    def __init__(self, nc):
        self.nc = nc
        self.streams = {e: [] for e in ENGS}
        self.sems = {}
        self.counts = {}

    def sem(self, es, name):
        h = es.enter_context(self.nc.semaphore(name))
        self.sems[name] = h
        self.counts[name] = 0

    SELFSEM = {"vector": "v", "scalar": "s", "gpsimd": "g"}

    def op(self, eng, fn, inc=None, n=1):
        if not getattr(self, "enabled", True):
            return self.counts.get(inc, 0) if inc else None
        ss = self.SELFSEM.get(eng)
        if ss is not None and self.counts.get(ss, 0) > 0:
            self.streams[eng].append(("wait", self.sems[ss], self.counts[ss]))
        if inc is not None:
            self.counts[inc] += n
            pair = (self.sems[inc], n)
        else:
            pair = None
        self.streams[eng].append(("op", fn, pair))
        return self.counts[inc] if inc else None

    def wait(self, eng, sem_name, val=None):
        if not getattr(self, "enabled", True):
            return
        v = self.counts[sem_name] if val is None else val
        if v > 0:
            self.streams[eng].append(("wait", self.sems[sem_name], v))

    def raw(self, eng, fn, inc=None, n=1):
        if inc is not None:
            self.counts[inc] += n
        self.streams[eng].append(("op", fn, None))

    def barrier(self):
        snap = dict(self.counts)
        for e in ENGS:
            for s, v in snap.items():
                self.wait(e, s, v)

    def finalize(self, block):
        for e in ENGS:
            stream = self.streams[e]

            def make(stream):
                def run(eng):
                    for item in stream:
                        if item[0] == "op":
                            _, fn, pair = item
                            inst = fn(eng)
                            if pair is not None and inst is not None:
                                inst.then_inc(pair[0], pair[1])
                        else:
                            _, semh, v = item
                            eng.wait_ge(semh, v)
                return run
            getattr(block, e)(make(stream))




def phase1(nc, c, sh, agin, on_chunk=None):
    PH = "D"
    CUT = 9
    NORANK1 = False
    xT = nc.dram_tensor("xTfull", [D, N], F32, kind="ExternalInput").ap()
    wqkvT = nc.dram_tensor("wqkvT", [D, 768], F32, kind="ExternalInput").ap()
    gamma_c = nc.dram_tensor("gamma_c", [128, 8], F32, kind="ExternalInput").ap()
    beta_cb = nc.dram_tensor("beta_cb", [128, 8], BF16, kind="ExternalInput").ap()
    masks01_i = nc.dram_tensor("masks01", [128, 5 * 512], BF16,
                               kind="ExternalInput").ap()
    aout = agin
    negm_dram = nc.dram_tensor("negm1_scratch", [1, 64], BF16, kind="Internal").ap()

    es = ExitStack()
    sb = lambda name, shape, dt: es.enter_context(nc.sbuf_tensor(name, shape, dt))
    with es:
        identb = sh["identb"]
        onesb = sh["onesb"]
        onesrow = sh["onesrow"]
        onesrow_f = sh["onesrow_f"]
        oneshh = sh["oneshh"]

        xt = [sb(f"xt{i}", [128, 8 * TT], F32) for i in range(2)]
        xb = sb("xb", [128, 8 * TT], BF16)
        xs = sb("xs", [128, 8 * TT], BF16)
        wstage = sb("wstage", [128, 768], F32)
        wraw = sb("wraw", [128, 768], BF16)
        wT = sb("wT", [128, 8 * 768], BF16)
        gam = sb("gam", [128, 8], F32)
        betab = sb("betab", [128, 8], BF16)
        maskt01 = sb("maskt01", [128, 5 * 512], BF16)
        colsum_row = sb("colsum_row", [1, 768], BF16)
        bias_row = sb("bias_row", [1, 768], BF16)
        shift_row = sb("shift_row", [1, 512], BF16)
        st0 = sb("st0", [128, TT], F32)
        st1 = sb("st1", [128, TT], F32)
        st2 = sb("st2", [128, TT], F32)
        st3 = sb("st3", [128, TT], F32)
        s_bc = sb("s_bc", [128, TT], F32)
        shift_bc = sb("shift_bc", [128, TT], F32)
        Qp = [sb(f"Qp{i}", [128, N], BF16) for i in range(2)]
        Kp = [sb(f"Kp{i}", [128, N], BF16) for i in range(2)]
        V_km = sb("V_km", [128, 64 * 260], BF16)
        vstage = sb("vstage", [128, 1024], BF16)
        PT = [sb(f"PT{i}", [128, 1536], BF16) for i in range(2)]
        biasneg = sb("biasneg", [128, 64], F32)
        recip_row = sb("recip_row", [1, 512], F32)
        accst = [sb(f"accst{i}", [65, 512], F32) for i in range(2)]
        recip_bc = sb("recip_bc", [64, 512], F32)
        aout_sb = [sb(f"aout{i}", [64, 512], BF16) for i in range(2)]
        negm2 = sb("negm2", [2, 32], F32)
        negm2b = sb("negm2b", [2, 32], BF16)
        negmrow = sb("negmrow", [1, 64], BF16)

        es_b = ExitStack()
        pb = lambda name, shape, dt: es_b.enter_context(nc.psum_tensor(name, shape, dt))
        p_sumx = pb("p_sumx", [128, TT], F32)
        p_sumx2 = pb("p_sumx2", [128, TT], F32)
        p_qkv = pb("p_qkv", [128, 3 * 512], F32)
        p_vtr = pb("p_vtr", [128, 2 * 1024], BF16)
        p_nb = p_sumx

        for dst, src in [(gam, gamma_c), (betab, beta_cb),
                         (maskt01, masks01_i)]:
            c.op("sync", lambda e, d=dst, s=src: e.dma_start(out=d[:], in_=s), inc="xc", n=16)
        vko = V_km[:].rearrange("p (c h d) -> p c h d", h=4, d=65)
        c.op("vector", lambda e: e.memset(vko[:, :, :, 64:65], 1.0), inc="v")
        c.wait("vector", "xc")
        for dc in range(8):
            wv = c.counts["v"]
            c.wait("sync", "v", wv)
            if dc > 0:
                c.wait("sync", "t")
            c.op("sync", lambda e, dc=dc: e.dma_start(
                out=wstage[:], in_=wqkvT[dc * 128:(dc + 1) * 128, :]), inc="xw", n=16)
            c.wait("vector", "xw")
            c.op("vector", lambda e, dc=dc: e.tensor_scalar(
                wT[:, dc * 768:(dc + 1) * 768], wstage[:], gam[:, dc:dc + 1], None,
                ALU.mult), inc="v")
            c.op("vector", lambda e: e.tensor_copy(wraw[:], wstage[:]), inc="v")
            wr_v = c.counts["v"]
            c.wait("tensor", "v", wr_v)
            c.op("tensor", lambda e, dc=dc: e.matmul(
                p_sumx[0:1, 0:512], betab[:, dc:dc + 1], wraw[:, 0:512],
                start=(dc == 0), stop=(dc == 7)), inc="t")
            c.op("tensor", lambda e, dc=dc: e.matmul(
                p_sumx2[0:1, 0:256], betab[:, dc:dc + 1], wraw[:, 512:768],
                start=(dc == 0), stop=(dc == 7)), inc="t")

        c.wait("tensor", "v")
        for (o0, o1) in ((0, 512), (512, 768)):
            for dc in range(8):
                c.op("tensor", lambda e, dc=dc, o0=o0, o1=o1: e.matmul(
                    p_qkv[0:1, o0:o1], onesb[:, 0:1],
                    wT[:, dc * 768 + o0: dc * 768 + o1],
                    start=(dc == 0), stop=(dc == 7)),
                    inc=("t" if (dc == 7 and o0 == 512) else None))
        c.wait("vector", "t")
        c.op("vector", lambda e: e.tensor_copy(colsum_row[:], p_qkv[0:1, 0:768]), inc="v")
        c.op("vector", lambda e: e.tensor_copy(bias_row[0:1, 0:512], p_sumx[0:1, 0:512]), inc="v")
        c.op("vector", lambda e: e.tensor_copy(bias_row[0:1, 512:768], p_sumx2[0:1, 0:256]), inc="v")

        NTILES = NT if PH != "A" else 0
        xt_free_v = [0, 0]
        qkv_evac_v = {}
        vtr_free_v = {}
        for t in range(NTILES):
            buf = t % 2
            c.wait("sync", "v", xt_free_v[buf])
            dsem = "d0" if buf == 0 else "d1"
            c.op("sync", lambda e, t=t, buf=buf: e.dma_start(
                out=xt[buf][:].rearrange("p (c n) -> p c n", n=TT),
                in_=xT.rearrange("(c p) n -> p c n", p=128)[:, :, t * TT:(t + 1) * TT]),
                inc=dsem, n=16)
            c.wait("vector", dsem)
            c.op("vector", lambda e, buf=buf: e.tensor_copy(xb[:], xt[buf][:]), inc="v")
            c.op("vector", lambda e: e.tensor_tensor(xs[:], xb[:], xb[:], ALU.mult), inc="v")
            xb_ready = c.counts["v"]
            c.enabled = CUT >= 1 and PH != "A"
            c.wait("tensor", "v", xb_ready)
            if t > 0:
                c.wait("tensor", "s", stats_consumed_s)
                c.wait("tensor", "v", stats_consumed_v)
            for dc in range(8):
                c.op("tensor", lambda e, dc=dc: e.matmul(
                    p_sumx[:], onesb[:], xb[:, dc * TT:(dc + 1) * TT],
                    start=(dc == 0), stop=(dc == 7)))
            for dc in range(8):
                c.op("tensor", lambda e, dc=dc: e.matmul(
                    p_sumx2[:], onesb[:], xs[:, dc * TT:(dc + 1) * TT],
                    start=(dc == 0), stop=(dc == 7)),
                    inc=("t" if dc == 7 else None))
            stats_t = c.counts["t"]
            c.wait("scalar", "t", stats_t)
            c.op("scalar", lambda e: e.activation(st0[:], p_sumx[:], AF.Square,
                                                  scale=1.0 / 1024.0), inc="s")
            stats_consumed_s = c.counts["s"]
            c.wait("vector", "s")
            c.op("vector", lambda e: e.scalar_tensor_tensor(
                st1[:], p_sumx2[:], 1.0 / 1024.0, st0[:], ALU.mult, ALU.subtract), inc="v")
            c.wait("scalar", "v")
            c.op("scalar", lambda e: e.activation(st2[:], st1[:], AF.Ln, bias=1e-5), inc="s")
            c.op("scalar", lambda e: e.activation(st0[:], st2[:], AF.Exp, scale=-0.5), inc="s")
            c.op("scalar", lambda e: e.activation(st3[:], st2[:], AF.Exp, scale=-1.0), inc="s")
            c.wait("vector", "s")
            c.op("vector", lambda e: e.tensor_scalar(
                st1[:], st3[:], -1e-5, 1.0 + 1e-5, ALU.mult, ALU.add), inc="v")
            c.wait("scalar", "v")
            c.op("scalar", lambda e: e.activation(st2[:], st1[:], AF.Ln), inc="s")
            c.op("scalar", lambda e: e.activation(st3[:], st2[:], AF.Exp, scale=-0.5), inc="s")
            c.wait("vector", "s")
            c.op("vector", lambda e: e.tensor_tensor(s_bc[:], st0[:], st3[:], ALU.mult), inc="v")
            c.op("vector", lambda e: e.scalar_tensor_tensor(
                shift_bc[:], p_sumx[:], -1.0 / 1024.0, s_bc[:], ALU.mult, ALU.mult), inc="v")
            stats_consumed_v = c.counts["v"]
            c.op("vector", lambda e: e.tensor_copy(shift_row[:], shift_bc[0:1, :]), inc="v")
            c.enabled = CUT >= 2 and PH != "A"
            sbc3 = s_bc[:].rearrange("p (a n) -> p a n", a=1).broadcast_to([128, 8, TT])
            c.op("vector", lambda e, buf=buf: e.tensor_tensor(
                xs[:].rearrange("p (c n) -> p c n", n=TT),
                xt[buf][:].rearrange("p (c n) -> p c n", n=TT), sbc3, ALU.mult), inc="v")
            xt_free_v[buf] = c.counts["v"]
            xs_ready = c.counts["v"]

            c.enabled = CUT >= 3 and PH != "A"
            c.wait("tensor", "v", xs_ready)
            for ec in range(6):
                bank = ec % 3
                if qkv_evac_v.get(bank):
                    c.wait("tensor", "v", qkv_evac_v[bank])
                for dc in range(8):
                    c.op("tensor", lambda e, dc=dc, ec=ec, bank=bank: e.matmul(
                        p_qkv[:, bank * 512:(bank + 1) * 512],
                        wT[:, dc * 768 + ec * 128: dc * 768 + (ec + 1) * 128],
                        xs[:, dc * TT:(dc + 1) * TT],
                        start=(dc == 0), stop=(NORANK1 and dc == 7)),
                        inc=("t" if (NORANK1 and dc == 7) else None))
                if not NORANK1:
                    c.op("tensor", lambda e, ec=ec, bank=bank: e.matmul(
                        p_qkv[:, bank * 512:(bank + 1) * 512],
                        colsum_row[0:1, ec * 128:(ec + 1) * 128], shift_row[:],
                        start=False, stop=False))
                    c.op("tensor", lambda e, ec=ec, bank=bank: e.matmul(
                        p_qkv[:, bank * 512:(bank + 1) * 512],
                        bias_row[0:1, ec * 128:(ec + 1) * 128], onesrow[:],
                        start=False, stop=True), inc="t")
                mm_t = c.counts["t"]
                c.wait("vector", "t", mm_t)
                pslice = (lambda bank=bank: p_qkv[:, bank * 512:(bank + 1) * 512])()
                if ec < 2:
                    c.op("vector", lambda e, ec=ec, t=t, ps=pslice: e.tensor_scalar(
                        Qp[ec][:, t * TT:(t + 1) * TT], ps, 0.125, None, ALU.mult), inc="v")
                elif ec < 4:
                    c.op("vector", lambda e, ec=ec, t=t, ps=pslice: e.tensor_copy(
                        Kp[ec - 2][:, t * TT:(t + 1) * TT], ps), inc="v")
                else:
                    c.op("vector", lambda e, ec=ec, ps=pslice: e.tensor_copy(
                        vstage[:, (ec - 4) * 512:(ec - 3) * 512], ps), inc="v")
                qkv_evac_v[bank] = c.counts["v"]
            vstage_ready = c.counts["v"]

            c.enabled = CUT >= 4 and PH != "A"
            c.wait("tensor", "v", vstage_ready)
            for ec in range(2):
                for sc in range(4):
                    tb = (ec * 4 + sc) % 2
                    if t > 0 or ec > 0 or sc > 1:
                        c.wait("tensor", "v", vtr_free_v.get(tb, 0))
                    c.op("tensor", lambda e, ec=ec, sc=sc, tb=tb: e.transpose(
                        p_vtr[:, tb * 1024: tb * 1024 + 128],
                        vstage[:, ec * 512 + sc * 128: ec * 512 + (sc + 1) * 128],
                        identb[:]), inc="t")
                    tr_t = c.counts["t"]
                    cc = 4 * t + sc
                    c.wait("vector", "t", tr_t)
                    c.op("vector", lambda e, cc=cc, ec=ec, tb=tb: e.tensor_copy(
                        V_km[:, cc * 260 + ec * 130: cc * 260 + (ec + 1) * 130]
                        .rearrange("p (h d) -> p h d", d=65)[:, :, 0:64],
                        p_vtr[:, tb * 1024: tb * 1024 + 128]
                        .rearrange("p (h d) -> p h d", d=64)), inc="v")
                    vtr_free_v[tb] = c.counts["v"]

        c.enabled = True
        for p in (range(2) if PH not in ("A", "B") else range(0)):
            for half in range(2):
                c.op("vector", lambda e, p=p, half=half: e.tensor_tensor(
                    xb[:], Qp[p][:, half * 4096:(half + 1) * 4096],
                    Kp[p][:, half * 4096:(half + 1) * 4096], ALU.mult), inc="v")
                qk_v = c.counts["v"]
                c.wait("tensor", "v", qk_v)
                for t8 in range(8):
                    c.op("tensor", lambda e, t8=t8: e.matmul(
                        p_nb[0:2, 0:512], oneshh[:, 0:2], xb[:, t8 * 512:(t8 + 1) * 512],
                        start=True, stop=True), inc="t")
                    mm_t = c.counts["t"]
                    c.wait("vector", "t", mm_t)
                    qt = half * 8 + t8
                    c.op("vector", lambda e, p=p, qt=qt: e.reduce_max(
                        negm2[0:2, p * 16 + qt: p * 16 + qt + 1],
                        p_nb[0:2, 0:512], axis=AX.X), inc="v")
                    c.wait("tensor", "v")
        c.enabled = PH not in ("A", "B")
        c.op("vector", lambda e: e.tensor_scalar(
            negm2b[:], negm2[:], -1.0, None, ALU.mult), inc="v")
        c.wait("gpsimd", "v")
        c.op("gpsimd", lambda e: e.dma_start(
            out=negm_dram.rearrange("o (r pq) -> (o r) pq", r=2), in_=negm2b[:]),
            inc="g", n=16)
        c.wait("gpsimd", "g")
        c.op("gpsimd", lambda e: e.dma_start(
            out=negmrow[:].rearrange("o (p r q) -> o p r q", p=2, r=2),
            in_=negm_dram.rearrange("o (r p q) -> o p r q", r=2, p=2)),
            inc="g", n=16)
        c.wait("tensor", "g")
        for h in range(4):
            for qt in range(NT):
                c.op("tensor", lambda e, h=h, qt=qt: e.matmul(
                    p_nb[:, 64 + h * 16 + qt: 65 + h * 16 + qt],
                    onesrow[0:1, 0:128], negmrow[0:1, h * 16 + qt: h * 16 + qt + 1],
                    start=True, stop=True),
                    inc=("t" if (h == 3 and qt == NT - 1) else None))
        c.wait("vector", "t")
        c.op("vector", lambda e: e.tensor_copy(biasneg[:], p_nb[:, 64:128]), inc="v")

        c.enabled = PH not in ("A", "B", "C")
        c.barrier()
        es_b.close()
        p_sim = [es.enter_context(nc.psum_tensor(f"p_sim{i}", [128, 1536], F32))
                 for i in range(2)]
        p_acc = es.enter_context(nc.psum_tensor("p_acc", [65, 512], F32))
        p_bcast = es.enter_context(nc.psum_tensor("p_bcast", [64, 512], F32))

        sim_free_s = [0, 0]
        pt_free_t = [0, 0]
        ptmask_free_v = [0, 0]
        acc_free_v = 0
        bcast_free_v = 0
        aout_free_o = [0, 0]
        nqt_done = 0
        NH = 1 if PH == "D1" else 4
        NQT = 2 if PH == "D1" else NT
        for qt in range(NQT):
            for h in range(NH):
                p = h // 2
                rb = 64 * (h % 2)
                nk = 4 * qt + 4
                ng = (nk + 2) // 3
                exp_s = []
                exp_v = []
                av_count = 0
                total_av = 3 * ng
                for g in range(ng):
                    jb = g % 2
                    c.wait("tensor", "s", sim_free_s[jb])
                    simops = []
                    masked = []
                    for j in range(3):
                        kc = 3 * g + j
                        kce = min(kc, 63)
                        patt = None
                        if kc > 4 * qt + 3:
                            patt = 4
                        elif kc >= 4 * qt:
                            patt = kc - 4 * qt
                        simops.append(lambda e, jb=jb, j=j, kce=kce, p=p, rb=rb, qt=qt: e.matmul(
                            p_sim[jb][:, j * 512:(j + 1) * 512],
                            Kp[p][rb:rb + 64, kce * 128:(kce + 1) * 128],
                            Qp[p][rb:rb + 64, qt * 512:(qt + 1) * 512],
                            start=True, stop=True))
                        if patt is not None:
                            masked.append((j, patt))
                    for fi, f in enumerate(simops):
                        c.op("tensor", f, inc=("t" if fi == len(simops) - 1 else None))
                    sim_t = c.counts["t"]
                    c.wait("scalar", "t", sim_t)
                    c.wait("scalar", "v", ptmask_free_v[jb])
                    c.op("scalar", lambda e, jb=jb, h=h, qt=qt: e.activation(
                        PT[jb][:], p_sim[jb][:], AF.Exp,
                        bias=biasneg[:, h * 16 + qt: h * 16 + qt + 1], scale=1.0), inc="s")
                    exp_s.append(c.counts["s"])
                    sim_free_s[jb] = c.counts["s"]
                    if masked:
                        c.wait("vector", "s", c.counts["s"])
                        for j, patt in masked:
                            c.op("vector", lambda e, jb=jb, j=j, patt=patt: e.tensor_tensor(
                                PT[jb][:, j * 512:(j + 1) * 512],
                                PT[jb][:, j * 512:(j + 1) * 512],
                                maskt01[:, patt * 512:(patt + 1) * 512],
                                ALU.mult), inc="v")
                        ptmask_free_v[jb] = c.counts["v"]
                        exp_v.append(c.counts["v"])
                    else:
                        exp_v.append(0)
                    if g >= 1:
                        gp = g - 1
                        jp = gp % 2
                        c.wait("tensor", "s", exp_s[gp])
                        if exp_v[gp]:
                            c.wait("tensor", "v", exp_v[gp])
                        if g == 1:
                            c.wait("tensor", "v", acc_free_v)
                        for j in range(3):
                            kc = 3 * gp + j
                            kce = min(kc, 63)
                            c.op("tensor", lambda e, jp=jp, j=j, kce=kce, h=h,
                                 st=(av_count == 0), sp=False: e.matmul(
                                p_acc[:], V_km[:, kce * 260 + h * 65: kce * 260 + h * 65 + 65],
                                PT[jp][:, j * 512:(j + 1) * 512],
                                start=st, stop=sp), inc=("t" if j == 2 else None))
                            av_count += 1
                        pt_free_t[jp] = c.counts["t"]
                gp = ng - 1
                jp = gp % 2
                c.wait("tensor", "s", exp_s[gp])
                if exp_v[gp]:
                    c.wait("tensor", "v", exp_v[gp])
                if ng == 1:
                    c.wait("tensor", "v", acc_free_v)
                for j in range(3):
                    kc = 3 * gp + j
                    kce = min(kc, 63)
                    c.op("tensor", lambda e, jp=jp, j=j, kce=kce, h=h,
                         st=(av_count == 0), sp=(j == 2): e.matmul(
                        p_acc[:], V_km[:, kce * 260 + h * 65: kce * 260 + h * 65 + 65],
                        PT[jp][:, j * 512:(j + 1) * 512],
                        start=st, stop=sp), inc=("t" if j == 2 else None))
                    av_count += 1
                pt_free_t[jp] = c.counts["t"]
                av_done_t = c.counts["t"]

                ob = nqt_done % 2
                c.wait("vector", "t", av_done_t)
                c.op("vector", lambda e: e.reciprocal(recip_row[:], p_acc[64:65, :]), inc="v")
                recip_v = c.counts["v"]
                c.wait("tensor", "v", max(recip_v, bcast_free_v))
                c.op("tensor", lambda e: e.matmul(
                    p_bcast[:], onesrow_f[:], recip_row[:], start=True, stop=True), inc="t")
                bc_t = c.counts["t"]
                c.wait("vector", "t", bc_t)
                c.op("vector", lambda e: e.tensor_copy(recip_bc[:], p_bcast[:]), inc="v")
                bcast_free_v = c.counts["v"]
                c.wait("vector", "o0" if ob == 0 else "o1", aout_free_o[ob])
                c.op("vector", lambda e, ob=ob: e.scalar_tensor_tensor(
                    aout_sb[ob][:], p_acc[0:64, :], 1.0, recip_bc[:],
                    ALU.mult, ALU.mult), inc="v")
                acc_free_v = c.counts["v"]
                c.wait("sync", "v", acc_free_v)
                osem = "o0" if ob == 0 else "o1"
                c.op("sync", lambda e, h=h, qt=qt, ob=ob: e.dma_start(
                    out=aout[qt // 4][64 * h: 64 * (h + 1),
                              (qt % 4) * 512:((qt % 4) + 1) * 512],
                    in_=aout_sb[ob][:]), inc=osem, n=16)
                aout_free_o[ob] = c.counts[osem]
                nqt_done += 1
            if qt % 4 == 3 and on_chunk is not None:
                on_chunk(qt // 4)






def make_masks_l1():
    m = np.zeros((128, 5 * 512), dtype=np.float32)
    for pnum in range(4):
        row = np.arange(128)[:, None] + 128 * pnum
        qq = np.arange(512)[None, :]
        m[:, pnum * 512:(pnum + 1) * 512] = np.where(row <= qq, 0.0, -1e30)
    m[:, 4 * 512:] = -1e30
    return m.astype(ml_dtypes.bfloat16)


def l1_in_maps(inp):
    x = np.asarray(inp["L0_attn_x"], dtype=np.float32)
    w_qkv = np.asarray(inp["L0_attn_w_qkv"], dtype=np.float32)
    gamma = np.asarray(inp["L0_attn_ln_gamma"], dtype=np.float32)
    beta = np.asarray(inp["L0_attn_ln_beta"], dtype=np.float32)
    gamma_c = np.ascontiguousarray(gamma.reshape(8, 128).T)
    beta_cb = np.ascontiguousarray(beta.reshape(8, 128).T).astype(ml_dtypes.bfloat16)
    masks = make_masks_l1()
    masks01 = (masks == 0).astype(ml_dtypes.bfloat16)
    in_maps = []
    for core in range(8):
        b, hg = core // 4, core % 4
        xTb = np.ascontiguousarray(x[b].T)
        rows = []
        for part in range(3):
            rows.extend(range(part * 1024 + hg * 256, part * 1024 + (hg + 1) * 256))
        wqkvT = np.ascontiguousarray(w_qkv[rows, :].T)
        in_maps.append({
            "xTfull": xTb, "wqkvT": wqkvT, "gamma_c": gamma_c,
            "beta_cb": beta_cb, "masks01": masks01,
        })
    return in_maps




TD = 3584
QS = [640, 1408]
KS = [128, 896]
CS = [768, 1536]
NQB = [(TD - QS[0]) // 128, (TD - QS[1]) // 128]
NBND = [11, 5]
MAXQB = 56
H = 8


def ln_chain(c, st0, st1, st2, s_out, sh_out, p_sumx, p_sumx2):
    c.wait("scalar", "t")
    c.op("scalar", lambda e: e.activation(st0[:], p_sumx[:], AF.Square,
                                          scale=1.0 / 1024.0), inc="s")
    c.wait("vector", "s")
    c.op("vector", lambda e: e.scalar_tensor_tensor(
        st1[:], p_sumx2[:], 1.0 / 1024.0, st0[:], ALU.mult, ALU.subtract), inc="v")
    c.wait("scalar", "v")
    c.op("scalar", lambda e: e.activation(st2[:], st1[:], AF.Ln, bias=1e-5), inc="s")
    c.op("scalar", lambda e: e.activation(s_out[:], st2[:], AF.Exp, scale=-0.5), inc="s")
    c.wait("vector", "s")
    c.op("vector", lambda e: e.scalar_tensor_tensor(
        sh_out[:], p_sumx[:], -1.0 / 1024.0, s_out[:], ALU.mult, ALU.mult), inc="v")


def rep8(ap):
    return ap.rearrange("p (a n) -> p a n", a=1).broadcast_to([128, 8, 512])


class L2:
    def __init__(self, nc):
        self.nc = nc
        di = lambda n, sh, dt: nc.dram_tensor(n, sh, dt, kind="ExternalInput").ap()
        self.xT = di("xTwin", [1024, TD], F32)
        self.woutT = di("woutT", [1024, 1024], F32)
        self.boutrow = di("boutrow", [1, 1024], F32)
        self.wqkv_i = di("wqkv", [2, 1024, 1536], F32)
        self.wo_i = di("wo", [2, 512, 1024], F32)
        self.w1_i = di("w1r", [2, 1024, 5632], F32)
        self.b1_i = di("b1r", [2, 1, 5632], F32)
        self.w2_i = di("w2p", [2, 2816, 1024], F32)
        self.b2_i = di("b2row", [2, 1, 1024], F32)
        self.b1col_i = di("b1col", [128, 88], F32)
        self.b2col_i = di("b2col", [128, 16], F32)
        self.lng_i = di("lng", [128, 16], F32)
        self.lnb_i = di("lnb", [128, 16], F32)
        self.qs8_i = di("qs8", [128, 2], F32)
        self.ks_i = di("ks1", [128, 2], F32)
        self.tabs_i = di("tabs", [128, 4 * TD], BF16)
        self.RT_i = di("RT", [128, 128], BF16)
        self.mreg_i = di("mreg", [128, 2 * 128], BF16)
        self.mbnd_i = di("mbnd", [sum(NBND) * 5, 128, 128], BF16)
        self.flag_i = di("flag", [128, 1], F32)
        self.outT = nc.dram_tensor("outT", [1024, 2048], F32, kind="ExternalOutput").ap()
        ds = lambda n: nc.dram_tensor(n, [1024, TD], F32, kind="Internal").ap()
        self.sc_xtld = ds("sc_xtld")
        self.sc_xin = ds("sc_xin")
        self.sc_xmid = [ds(f"sc_xmid{l}") for l in range(2)]
        self.sc_xout = [ds(f"sc_xout{l}") for l in range(2)]
        self.negm_dram = nc.dram_tensor(
            "negm_scratch", [1, 2 * 4 * MAXQB], BF16, kind="Internal").ap()
        self.sc_o = nc.dram_tensor("sc_o", [512, TD], BF16, kind="Internal").ap()

    def rr(self, ap):
        return ap.rearrange("(c p) n -> p c n", p=128)

    def build(self, c, sh, attw):
        nc = self.nc
        self.attnT = attw
        es = ExitStack()
        self.es = es
        sb = lambda n, sh_, dt: es.enter_context(nc.sbuf_tensor(n, sh_, dt))
        with es:
            self.c = c
            self.identb = sh["identb"]
            self.onesb = sh["onesb"]
            self.oneshh = sh["oneshh"]
            self.selT = sh["selT"]
            self.onesrow = sh["onesrow"]
            self.onesrow_f = sh["onesrow_f"]
            self.RT = sb("RT_s", [128, 128], BF16)
            self.mreg = sb("mreg_s", [128, 2 * 128], BF16)
            self.lng = sb("lng_s", [128, 16], F32)
            self.lnb = sb("lnb_s", [128, 16], F32)
            self.qs8 = sb("qs8_s", [128, 2], F32)
            self.ks1 = sb("ks1_s", [128, 2], F32)
            self.flag = sb("flag_s", [128, 1], F32)
            self.b1col = sb("b1col_s", [128, 88], F32)
            self.b2col = sb("b2col_s", [128, 16], F32)
            self.st0 = sb("st0_p2", [128, 512], F32)
            self.st1 = sb("st1_p2", [128, 512], F32)
            self.st2 = sb("st2_p2", [128, 512], F32)
            self.s_bc = sb("s_bc_p2", [128, 512], F32)
            self.sh_bc = sb("sh_bc_p2", [128, 512], F32)

            for dst, src in [
                (self.RT, self.RT_i), (self.mreg, self.mreg_i),
                (self.lng, self.lng_i), (self.lnb, self.lnb_i),
                (self.qs8, self.qs8_i), (self.ks1, self.ks_i),
                (self.flag, self.flag_i), (self.b1col, self.b1col_i),
                (self.b2col, self.b2col_i),
            ]:
                c.op("sync", lambda e, d=dst, s=src: e.dma_start(out=d[:], in_=s),
                     inc="xc", n=16)
            for eng in ("vector", "tensor", "scalar", "gpsimd"):
                c.wait(eng, "xc")

            self.stageA()
            self.attn_part(0, self.sc_xin, self.sc_xmid[0])
            self.ffn_part(0, self.sc_xmid[0], self.sc_xout[0])
            self.attn_part(1, self.sc_xout[0], self.sc_xmid[1])
            self.ffn_part(1, self.sc_xmid[1], self.sc_xout[1])
            self.final_stage()

    def stageA(self):
        nc, c = self.nc, self.c
        esa = ExitStack()
        sba = lambda n, sh, dt: esa.enter_context(nc.sbuf_tensor(n, sh, dt))
        pba = lambda n, sh, dt: esa.enter_context(nc.psum_tensor(n, sh, dt))
        woT = sba("woT", [128, 8 * 1024], BF16)
        brow = sba("browA", [1, 1024], BF16)
        at_t = [sba(f"atA{i}", [128, 8 * 512], BF16) for i in range(2)]
        xf_t = [sba(f"xfA{i}", [128, 8 * 512], F32) for i in range(2)]
        yf = sba("yfA", [128, 8 * 512], F32)
        ybf = sba("ybfA", [128, 8 * 512], BF16)
        y2b = sba("y2bA", [128, 8 * 512], BF16)
        xtl = sba("xtlA", [128, 8 * 512], F32)
        xin_t = sba("xinA", [128, 8 * 512], F32)
        self.s2_bc = esa.enter_context(nc.sbuf_tensor("s2_bcA", [128, 512], F32))
        self.sh2_bc = esa.enter_context(nc.sbuf_tensor("sh2_bcA", [128, 512], F32))
        p_y = pba("p_yA", [128, 3 * 512], F32)
        p_s1 = pba("p_s1A", [128, 512], F32)
        p_s2 = pba("p_s2A", [128, 512], F32)

        c.op("gpsimd", lambda e: e.dma_start(out=brow[:], in_=self.boutrow),
             inc="g", n=16)
        c.op("gpsimd", lambda e: e.dma_start(
            out=woT[:].rearrange("p (c n) -> p c n", n=1024),
            in_=self.woutT.rearrange("(c p) n -> p c n", p=128)), inc="g", n=16)
        c.wait("tensor", "g")

        atr = self.rr(self.attnT)
        xtr = self.rr(self.xT)
        xtld_r = self.rr(self.sc_xtld)
        xin_r = self.rr(self.sc_xin)
        st0, st1, st2 = self.st0, self.st1, self.st2

        tile_free = [0, 0]
        tile_free_t = [0, 0]
        bank_free = {}
        statA_s = 0
        statA_v = 0
        xtl_dma = [0, 0]
        xin_dma = [0, 0]
        for t in range(7):
            bf = t % 2
            dsem = "d0" if bf == 0 else "d1"
            c.wait("sync", "v", tile_free[bf])
            c.wait("sync", "t", tile_free_t[bf])
            c.op("sync", lambda e, t=t, bf=bf: e.dma_start(
                out=at_t[bf][:].rearrange("p (c n) -> p c n", n=512),
                in_=atr[:, :, t * 512:(t + 1) * 512]), inc=dsem, n=16)
            c.op("sync", lambda e, t=t, bf=bf: e.dma_start(
                out=xf_t[bf][:].rearrange("p (c n) -> p c n", n=512),
                in_=xtr[:, :, t * 512:(t + 1) * 512]), inc=dsem, n=16)
            c.wait("tensor", dsem)
            for dc in range(8):
                bank = dc % 3
                if bank_free.get(bank):
                    c.wait("tensor", "v", bank_free[bank])
                for ec in range(8):
                    c.op("tensor", lambda e, dc=dc, ec=ec, bf=bf, bank=bank: e.matmul(
                        p_y[:, bank * 512:(bank + 1) * 512],
                        woT[:, ec * 1024 + dc * 128: ec * 1024 + (dc + 1) * 128],
                        at_t[bf][:, ec * 512:(ec + 1) * 512],
                        start=(ec == 0), stop=False))
                c.op("tensor", lambda e, dc=dc, bank=bank: e.matmul(
                    p_y[:, bank * 512:(bank + 1) * 512],
                    brow[0:1, dc * 128:(dc + 1) * 128], self.onesrow[:],
                    start=False, stop=True), inc="t")
                c.wait("vector", "t")
                c.op("vector", lambda e, dc=dc, bank=bank: e.tensor_copy(
                    yf[:, dc * 512:(dc + 1) * 512],
                    p_y[:, bank * 512:(bank + 1) * 512]), inc="v")
                bank_free[bank] = c.counts["v"]
            tile_free_t[bf] = c.counts["t"]
            c.op("vector", lambda e: e.tensor_copy(ybf[:], yf[:]), inc="v")
            c.op("vector", lambda e: e.tensor_tensor(y2b[:], ybf[:], ybf[:],
                                                     ALU.mult), inc="v")
            yb_v = c.counts["v"]
            c.wait("tensor", "v", yb_v)
            c.wait("tensor", "s", statA_s)
            c.wait("tensor", "v", statA_v)
            for dc in range(8):
                c.op("tensor", lambda e, dc=dc: e.matmul(
                    p_s1[:], self.onesb[:], ybf[:, dc * 512:(dc + 1) * 512],
                    start=(dc == 0), stop=(dc == 7)))
            for dc in range(8):
                c.op("tensor", lambda e, dc=dc: e.matmul(
                    p_s2[:], self.onesb[:], y2b[:, dc * 512:(dc + 1) * 512],
                    start=(dc == 0), stop=(dc == 7)),
                    inc=("t" if dc == 7 else None))
            ln_chain(c, st0, st1, st2, self.s_bc, self.sh_bc, p_s1, p_s2)
            statA_s = c.counts["s"]
            c.wait("vector", "e0" if bf == 1 else "e1", xtl_dma[1 - bf])
            c.op("vector", lambda e: e.scalar_tensor_tensor(
                xtl[:].rearrange("p (c n) -> p c n", n=512),
                yf[:].rearrange("p (c n) -> p c n", n=512), 1.0, rep8(self.s_bc[:]),
                ALU.mult, ALU.mult), inc="v")
            c.op("vector", lambda e: e.tensor_tensor(
                xtl[:].rearrange("p (c n) -> p c n", n=512),
                xtl[:].rearrange("p (c n) -> p c n", n=512), rep8(self.sh_bc[:]),
                ALU.add), inc="v")
            c.op("vector", lambda e, bf=bf: e.tensor_tensor(
                xtl[:], xtl[:], xf_t[bf][:], ALU.add), inc="v")
            tile_free[bf] = c.counts["v"]
            esem = "e0" if bf == 0 else "e1"
            c.wait("sync", "v")
            c.op("sync", lambda e, t=t: e.dma_start(
                out=xtld_r[:, :, t * 512:(t + 1) * 512],
                in_=xtl[:].rearrange("p (c n) -> p c n", n=512)), inc=esem, n=16)
            c.op("vector", lambda e: e.tensor_copy(ybf[:], xtl[:]), inc="v")
            c.op("vector", lambda e: e.tensor_tensor(y2b[:], ybf[:], ybf[:],
                                                     ALU.mult), inc="v")
            x2_v = c.counts["v"]
            c.wait("tensor", "v", x2_v)
            c.wait("tensor", "s", statA_s)
            for dc in range(8):
                c.op("tensor", lambda e, dc=dc: e.matmul(
                    p_s1[:], self.onesb[:], ybf[:, dc * 512:(dc + 1) * 512],
                    start=(dc == 0), stop=(dc == 7)))
            for dc in range(8):
                c.op("tensor", lambda e, dc=dc: e.matmul(
                    p_s2[:], self.onesb[:], y2b[:, dc * 512:(dc + 1) * 512],
                    start=(dc == 0), stop=(dc == 7)),
                    inc=("t" if dc == 7 else None))
            ln_chain(c, st0, st1, st2, self.s2_bc, self.sh2_bc, p_s1, p_s2)
            statA_s = c.counts["s"]
            statA_v = c.counts["v"]
            c.wait("vector", "e0" if bf == 1 else "e1", xin_dma[1 - bf])
            c.op("vector", lambda e: e.scalar_tensor_tensor(
                xin_t[:].rearrange("p (c n) -> p c n", n=512),
                xtl[:].rearrange("p (c n) -> p c n", n=512), 1.0, rep8(self.s2_bc[:]),
                ALU.mult, ALU.mult), inc="v")
            c.op("vector", lambda e: e.tensor_tensor(
                xin_t[:].rearrange("p (c n) -> p c n", n=512),
                xin_t[:].rearrange("p (c n) -> p c n", n=512), rep8(self.sh2_bc[:]),
                ALU.add), inc="v")
            if t == 2:
                c.op("vector", lambda e: e.tensor_scalar(
                    xin_t[:].rearrange("p (c n) -> p c n", n=512)[:, :, 511:512],
                    xin_t[:].rearrange("p (c n) -> p c n", n=512)[:, :, 511:512],
                    self.flag[:, 0:1], None, ALU.mult), inc="v")
            c.wait("sync", "v")
            c.op("sync", lambda e, t=t: e.dma_start(
                out=xin_r[:, :, t * 512:(t + 1) * 512],
                in_=xin_t[:].rearrange("p (c n) -> p c n", n=512)), inc=esem, n=16)
            xtl_dma[bf] = c.counts[esem]
            xin_dma[bf] = c.counts[esem]
        c.wait("sync", "e0", c.counts["e0"])
        c.wait("sync", "e1", c.counts["e1"])
        c.barrier()
        esa.close()

    def attn_part(self, l, z_dram, xmid_dram):
        nc, c = self.nc, self.c
        qs_off, ks_off = QS[l], KS[l]
        Tq, Tk = TD - qs_off, TD - ks_off
        nqb = NQB[l]
        nkt = (Tk + 511) // 512
        nbnd = NBND[l]
        mb_base = 5 * sum(NBND[:l])

        e2 = ExitStack()
        sb2 = lambda n, sh, dt: e2.enter_context(nc.sbuf_tensor(f"{n}_l{l}", sh, dt))
        pb2 = lambda n, sh, dt: e2.enter_context(nc.psum_tensor(f"{n}_l{l}", sh, dt))

        wq = sb2("wq", [128, 8 * 1536], BF16)
        zb = [sb2(f"zb{i}", [128, 8 * 513], BF16) for i in range(2)]
        tbt = [sb2(f"tbt{i}", [128, 4 * 512], BF16) for i in range(2)]
        Qf = sb2("Qf", [128, 4 * Tq], BF16)
        Kf = sb2("Kf", [128, 4 * Tk], BF16)
        Vkm = sb2("Vkm", [128, (Tk // 128) * 520], BF16)
        vstage = sb2("vstage", [128, 4 * 512], BF16)
        qraw = sb2("qraw", [128, 512], BF16)
        sqb_ = sb2("sqb", [128, 512], BF16)
        rnb = sb2("rnb", [2, 512], BF16)
        qn = sb2("qn", [128, 512], BF16)
        t1f = sb2("t1f", [128, 512], F32)
        uf = sb2("uf", [128, 512], F32)
        negm2 = sb2("negm2", [2, 4 * MAXQB], F32)
        negm2b = sb2("negm2b", [2, 4 * MAXQB], BF16)
        negmrow = sb2("negmrow", [1, 8 * MAXQB], BF16)
        biasneg = sb2("biasneg", [128, 8 * MAXQB], F32)
        dqb = sb2("dqb", [128, 512], BF16)
        mstr = [sb2(f"mstr{i}", [128, 5 * 128], BF16) for i in range(2)]
        PT = [sb2(f"PTl{i}", [128, 640], BF16) for i in range(2)]
        accst = sb2("accst", [65, 8 * 128], F32)
        lrec = sb2("lrec", [1, 8 * 128], F32)
        lntmp = sb2("lntmp", [1, 8 * 128], F32)
        rb_sb = sb2("rb_sb", [64, 8 * 128], F32)
        ost = [sb2(f"ost{i}", [64, 8 * 128], BF16) for i in range(2)]

        e2p = ExitStack()
        pb2p = lambda n, sh, dt: e2p.enter_context(nc.psum_tensor(f"{n}_l{l}", sh, dt))
        p_qkv = pb2p("p_qkv", [128, 3 * 512], F32)
        p_nrm = pb2p("p_nrm", [2, 512], F32)
        p_nb = pb2p("p_nb", [128, 512], F32)
        p_rot = pb2p("p_rot", [128, 512], F32)
        p_vt2 = pb2p("p_vt2", [128, 2 * 1024], BF16)

        c.op("gpsimd", lambda e, l=l: e.dma_start(
            out=wq[:].rearrange("p (c n) -> p c n", n=1536),
            in_=self.wqkv_i[l].rearrange("(c p) n -> p c n", p=128)), inc="g", n=16)
        c.wait("tensor", "g")

        zr = self.rr(z_dram)
        tabr = self.tabs_i.rearrange("p (f n) -> p f n", f=4)
        vko = Vkm[:].rearrange("p (k h d) -> p k h d", h=8, d=65)
        c.op("vector", lambda e: e.memset(vko[:, :, :, 64:65], 1.0), inc="v")
        c.op("vector", lambda e: e.memset(negm2[:], 0.0), inc="v")

        bank_free = {}
        zb_free = [0, 0]
        vt_free = [0, 0]
        for kt in range(nkt):
            t0 = ks_off + kt * 512
            tlen = min(512, TD - t0)
            bf = kt % 2
            gsem = "g0" if bf == 0 else "g1"
            c.wait("gpsimd", "v", zb_free[bf])
            c.op("gpsimd", lambda e, t0=t0, tlen=tlen, bf=bf: e.dma_start(
                out=zb[bf][:].rearrange("p (c n) -> p c n", n=513)[:, :, 0:tlen + 1],
                in_=zr[:, :, t0 - 1: t0 + tlen]), inc=gsem, n=16)
            c.op("gpsimd", lambda e, t0=t0, tlen=tlen, bf=bf: e.dma_start(
                out=tbt[bf][:].rearrange("p (f n) -> p f n", f=4)[:, :, 0:tlen],
                in_=tabr[:, :, t0: t0 + tlen]), inc=gsem, n=16)
            c.wait("tensor", gsem)
            for ec in range(12):
                bank = ec % 3
                if bank_free.get(bank):
                    c.wait("tensor", "v", bank_free[bank])
                for dc in range(8):
                    off = 1 if dc < 4 else 0
                    c.op("tensor", lambda e, ec=ec, dc=dc, bf=bf, bank=bank, off=off,
                         tlen=tlen: e.matmul(
                        p_qkv[:, bank * 512: bank * 512 + tlen],
                        wq[:, dc * 1536 + ec * 128: dc * 1536 + (ec + 1) * 128],
                        zb[bf][:, dc * 513 + off: dc * 513 + off + tlen],
                        start=(dc == 0), stop=(dc == 7)),
                        inc=("t" if dc == 7 else None))
                mm_t = c.counts["t"]
                c.wait("vector", "t", mm_t)
                ch = ec % 4
                pslice = p_qkv[:, bank * 512: bank * 512 + tlen]
                if ec >= 8:
                    c.op("vector", lambda e, ps=pslice, ch=ch, tlen=tlen: e.tensor_copy(
                        vstage[:, ch * 512: ch * 512 + tlen], ps), inc="v")
                    bank_free[bank] = c.counts["v"]
                    continue
                isq = ec < 4
                doff = (t0 - qs_off) if isq else (t0 - ks_off)
                c.op("vector", lambda e, ps=pslice, tlen=tlen: e.tensor_copy(
                    qraw[:, 0:tlen], ps), inc="v")
                bank_free[bank] = c.counts["v"]
                if isq and doff < 0:
                    continue
                c.op("vector", lambda e, tlen=tlen: e.tensor_tensor(
                    sqb_[:, 0:tlen], qraw[:, 0:tlen], qraw[:, 0:tlen], ALU.mult),
                    inc="v")
                sq_v = c.counts["v"]
                c.wait("tensor", "v", sq_v)
                c.op("tensor", lambda e, tlen=tlen: e.matmul(
                    p_nrm[:, 0:tlen], self.oneshh[:, 0:2], sqb_[:, 0:tlen],
                    start=True, stop=True), inc="t")
                c.wait("scalar", "t")
                c.op("scalar", lambda e, tlen=tlen: e.activation(
                    self.st0[0:2, 0:tlen], p_nrm[:, 0:tlen], AF.Ln, bias=1e-30),
                    inc="s")
                c.op("scalar", lambda e, tlen=tlen: e.activation(
                    rnb[:, 0:tlen], self.st0[0:2, 0:tlen], AF.Exp, scale=-0.5),
                    inc="s")
                c.wait("tensor", "s")
                c.op("tensor", lambda e, tlen=tlen: e.matmul(
                    p_nb[:, 0:tlen], self.selT[:, :], rnb[:, 0:tlen],
                    start=True, stop=True), inc="t")
                nb_t = c.counts["t"]
                c.wait("vector", "t", nb_t)
                sc_ap = self.qs8 if isq else self.ks1
                c.op("vector", lambda e, tlen=tlen, sc_ap=sc_ap, l=l: e.scalar_tensor_tensor(
                    qn[:, 0:tlen], qraw[:, 0:tlen], sc_ap[:, l:l + 1], p_nb[:, 0:tlen],
                    ALU.mult, ALU.mult), inc="v")
                qn_v = c.counts["v"]
                c.wait("tensor", "v", qn_v)
                c.op("tensor", lambda e, tlen=tlen: e.matmul(
                    p_rot[:, 0:tlen], self.RT[:, :], qn[:, 0:tlen],
                    start=True, stop=True), inc="t")
                rot_t = c.counts["t"]
                ci = 0 if isq else 2
                si = 1 if isq else 3
                c.wait("vector", "g0" if bf == 0 else "g1")
                c.op("vector", lambda e, bf=bf, ci=ci, tlen=tlen: e.tensor_tensor(
                    t1f[:, 0:tlen], qn[:, 0:tlen],
                    tbt[bf][:, ci * 512: ci * 512 + tlen], ALU.mult), inc="v")
                c.wait("vector", "t", rot_t)
                c.op("vector", lambda e, bf=bf, si=si, tlen=tlen: e.scalar_tensor_tensor(
                    uf[:, 0:tlen], p_rot[:, 0:tlen], 1.0,
                    tbt[bf][:, si * 512: si * 512 + tlen], ALU.mult, ALU.mult),
                    inc="v")
                dst, dT = (Qf, Tq) if isq else (Kf, Tk)
                c.op("vector", lambda e, ch=ch, dst=dst, dT=dT, doff=doff,
                     tlen=tlen: e.tensor_tensor(
                    dst[:, ch * dT + doff: ch * dT + doff + tlen],
                    t1f[:, 0:tlen], uf[:, 0:tlen], ALU.add), inc="v")
            zb_free[bf] = c.counts["v"]
            vs_v = c.counts["v"]
            c.wait("tensor", "v", vs_v)
            nsc = (tlen + 127) // 128
            for ch in range(4):
                for scn in range(nsc):
                    tbuf = (ch * nsc + scn) % 2
                    c.wait("tensor", "v", vt_free[tbuf])
                    c.op("tensor", lambda e, ch=ch, scn=scn, tbuf=tbuf: e.transpose(
                        p_vt2[:, tbuf * 1024: tbuf * 1024 + 128],
                        vstage[:, ch * 512 + scn * 128: ch * 512 + (scn + 1) * 128],
                        self.identb[:]), inc="t")
                    tr_t = c.counts["t"]
                    cc = kt * 4 + scn
                    c.wait("vector", "t", tr_t)
                    c.op("vector", lambda e, cc=cc, ch=ch, tbuf=tbuf: e.tensor_copy(
                        Vkm[:, cc * 520 + ch * 130: cc * 520 + (ch + 1) * 130]
                        .rearrange("p (h d) -> p h d", d=65)[:, :, 0:64],
                        p_vt2[:, tbuf * 1024: tbuf * 1024 + 128]
                        .rearrange("p (h d) -> p h d", d=64)), inc="v")
                    vt_free[tbuf] = c.counts["v"]

        ndt = (Tq + 511) // 512
        for ch in range(4):
            for dt_ in range(ndt):
                doff = dt_ * 512
                dlen = min(512, Tq - doff)
                nb = dlen // 128
                c.op("vector", lambda e, ch=ch, doff=doff, dlen=dlen: e.tensor_tensor(
                    dqb[:, 0:dlen], Qf[:, ch * Tq + doff: ch * Tq + doff + dlen],
                    Kf[:, ch * Tk + 512 + doff: ch * Tk + 512 + doff + dlen],
                    ALU.mult), inc="v")
                dq_v = c.counts["v"]
                c.wait("tensor", "v", dq_v)
                c.op("tensor", lambda e, dlen=dlen: e.matmul(
                    p_nrm[:, 0:dlen], self.oneshh[:, 0:2], dqb[:, 0:dlen],
                    start=True, stop=True), inc="t")
                mm_t = c.counts["t"]
                c.wait("vector", "t", mm_t)
                c.op("vector", lambda e, ch=ch, dt_=dt_, nb=nb: e.reduce_max(
                    negm2[0:2, ch * MAXQB + dt_ * 4: ch * MAXQB + dt_ * 4 + nb],
                    p_nrm[:, 0:nb * 128].rearrange("p (b n) -> p b n", n=128),
                    axis=AX.X), inc="v")
                c.wait("tensor", "v")
        c.op("vector", lambda e: e.tensor_scalar(
            negm2b[:], negm2[:], -1.0, None, ALU.mult), inc="v")
        c.wait("gpsimd", "v")
        c.op("gpsimd", lambda e: e.dma_start(
            out=self.negm_dram.rearrange("o (r pq) -> (o r) pq", r=2),
            in_=negm2b[:]), inc="g", n=16)
        c.wait("gpsimd", "g")
        c.op("gpsimd", lambda e: e.dma_start(
            out=negmrow[:].rearrange("o (p r q) -> o p r q", p=4, r=2),
            in_=self.negm_dram.rearrange("o (r p q) -> o p r q", r=2, p=4)),
            inc="g", n=16)
        c.wait("tensor", "g")
        c.op("tensor", lambda e: e.matmul(
            p_nb[:, 0:448], self.onesrow[0:1, 0:128], negmrow[0:1, 0:448],
            start=True, stop=True), inc="t")
        bt = c.counts["t"]
        c.wait("vector", "t", bt)
        c.op("vector", lambda e: e.tensor_copy(biasneg[:, 0:448], p_nb[:, 0:448]),
             inc="v")
        c.wait("tensor", "v")

        c.barrier()
        e2p.close()
        e3 = ExitStack()
        p_sml = [e3.enter_context(nc.psum_tensor(f"p_sml{i}_l{l}", [128, 640], F32))
                 for i in range(2)]
        p_acc = e3.enter_context(nc.psum_tensor(f"p_accl_l{l}", [65, 128], F32))
        p_rb = e3.enter_context(nc.psum_tensor(f"p_rbl_l{l}", [64, 1024], F32))

        sim_free = [0, 0]
        acc_free = 0
        rb_free = 0
        mb_free = [0, 0]
        ost_free = [0, 0]
        for qb in range(nqb):
            qoff = qb * 128
            mb = qb % 2
            if qb < nbnd:
                dsem = "d0" if mb == 0 else "d1"
                c.wait("sync", "t", mb_free[mb])
                c.op("sync", lambda e, l=l, qb=qb, mb=mb: e.dma_start(
                    out=mstr[mb][:].rearrange("p (k n) -> p k n", k=5),
                    in_=self.mbnd_i[mb_base + qb * 5: mb_base + (qb + 1) * 5]
                    .rearrange("k p n -> p k n")), inc=dsem, n=16)
                c.wait("tensor", dsem)
            for h in range(H):
                ch, rb = h // 2, 64 * (h % 2)
                jb = (qb * H + h) % 2
                c.wait("tensor", "s", sim_free[jb])
                simops = []
                for ko in range(5):
                    koff = qoff + ko * 128
                    masked = qb < nbnd or ko == 0 or ko == 4
                    simops.append(lambda e, jb=jb, ko=ko, ch=ch, rb=rb, koff=koff,
                                  qoff=qoff, masked=masked: e.matmul(
                        p_sml[jb][:, ko * 128:(ko + 1) * 128],
                        Kf[rb:rb + 64, ch * Tk + koff: ch * Tk + koff + 128],
                        Qf[rb:rb + 64, ch * Tq + qoff: ch * Tq + qoff + 128],
                        start=True, stop=(not masked)))
                    if qb < nbnd:
                        simops.append(lambda e, jb=jb, ko=ko, mb=mb: e.matmul(
                            p_sml[jb][:, ko * 128:(ko + 1) * 128],
                            self.identb[:], mstr[mb][:, ko * 128:(ko + 1) * 128],
                            start=False, stop=True))
                    elif ko == 0 or ko == 4:
                        mi = 0 if ko == 0 else 1
                        simops.append(lambda e, jb=jb, ko=ko, mi=mi: e.matmul(
                            p_sml[jb][:, ko * 128:(ko + 1) * 128],
                            self.identb[:], self.mreg[:, mi * 128:(mi + 1) * 128],
                            start=False, stop=True))
                for fi, f in enumerate(simops):
                    c.op("tensor", f, inc=("t" if fi == len(simops) - 1 else None))
                sim_t = c.counts["t"]
                if qb < nbnd:
                    mb_free[mb] = sim_t
                c.wait("scalar", "t", sim_t)
                c.op("scalar", lambda e, jb=jb, h=h, qb=qb: e.activation(
                    PT[jb][:], p_sml[jb][:], AF.Exp,
                    bias=biasneg[:, h * MAXQB + qb: h * MAXQB + qb + 1],
                    scale=1.0), inc="s")
                exp_s = c.counts["s"]
                sim_free[jb] = exp_s
                c.wait("tensor", "s", exp_s)
                c.wait("tensor", "v", acc_free)
                for ko in range(5):
                    kc = qb + ko
                    c.op("tensor", lambda e, jb=jb, ko=ko, h=h, kc=kc,
                         st=(ko == 0), sp=(ko == 4): e.matmul(
                        p_acc[:], Vkm[:, kc * 520 + h * 65: kc * 520 + h * 65 + 65],
                        PT[jb][:, ko * 128:(ko + 1) * 128],
                        start=st, stop=sp), inc=("t" if ko == 4 else None))
                av_t = c.counts["t"]
                c.wait("vector", "t", av_t)
                c.op("vector", lambda e, h=h: e.tensor_copy(
                    accst[:, h * 128:(h + 1) * 128], p_acc[:]), inc="v")
                acc_free = c.counts["v"]
            c.op("vector", lambda e: e.tensor_copy(
                lrec[:], accst[64:65, :]), inc="v")
            lr_v = c.counts["v"]
            c.wait("scalar", "v", lr_v)
            c.op("scalar", lambda e: e.activation(lntmp[:], lrec[:], AF.Ln), inc="s")
            c.op("scalar", lambda e: e.activation(lrec[:], lntmp[:], AF.Exp,
                                                  scale=-1.0), inc="s")
            rl_s = c.counts["s"]
            c.wait("tensor", "s", rl_s)
            c.wait("tensor", "v", rb_free)
            c.op("tensor", lambda e: e.matmul(
                p_rb[:, 0:512], self.onesrow_f[:], lrec[0:1, 0:512],
                start=True, stop=True))
            c.op("tensor", lambda e: e.matmul(
                p_rb[:, 512:1024], self.onesrow_f[:], lrec[0:1, 512:1024],
                start=True, stop=True), inc="t")
            bc_t = c.counts["t"]
            c.wait("vector", "t", bc_t)
            c.op("vector", lambda e: e.tensor_copy(rb_sb[:], p_rb[:]), inc="v")
            rb_free = c.counts["v"]
            ob2 = qb % 2
            c.wait("vector", "e0" if ob2 == 0 else "e1", ost_free[ob2])
            c.op("vector", lambda e, ob2=ob2: e.scalar_tensor_tensor(
                ost[ob2][0:64, :].rearrange("p (h n) -> p h n", h=8),
                accst[0:64, :].rearrange("p (h n) -> p h n", h=8),
                1.0,
                rb_sb[:].rearrange("p (h n) -> p h n", h=8),
                ALU.mult, ALU.mult), inc="v")
            acc_free = c.counts["v"]
            c.wait("sync", "v")
            esm = "e0" if ob2 == 0 else "e1"
            c.op("sync", lambda e, qoff=qoff, ob2=ob2: e.dma_start(
                out=self.sc_o.rearrange("(h d) n -> d h n", d=64)[
                    :, :, qs_off + qoff: qs_off + qoff + 128],
                in_=ost[ob2][0:64, :].rearrange("p (h n) -> p h n", h=8)),
                inc=esm, n=16)
            ost_free[ob2] = c.counts[esm]

        c.wait("sync", "e0", c.counts["e0"])
        c.wait("sync", "e1", c.counts["e1"])
        c.barrier()
        e3.close()

        e2.close()

        e4 = ExitStack()
        sb4 = lambda n, sh, dt: e4.enter_context(nc.sbuf_tensor(f"{n}_w{l}", sh, dt))
        zq = sb4("zq", [128, 8 * 512], F32)
        otile = sb4("otile", [128, 4 * 512], BF16)
        xmo = sb4("xmo", [128, 8 * 512], F32)
        woB = sb4("woB", [128, 4 * 1024], BF16)
        p_wo = e4.enter_context(nc.psum_tensor(f"p_wo_l{l}", [128, 3 * 512], F32))
        c.op("gpsimd", lambda e, l=l: e.dma_start(
            out=woB[:].rearrange("p (c n) -> p c n", n=1024),
            in_=self.wo_i[l].rearrange("(c p) n -> p c n", p=128)), inc="g", n=16)
        c.wait("tensor", "g")
        wo_bank_free = {}
        zq_free = 0
        ot_free = 0
        xmo_dma = [0, 0]
        nwt = (Tq + 511) // 512
        flag_tile = (1535 - qs_off) // 512
        flag_col = (1535 - qs_off) % 512
        o_pk = self.sc_o.rearrange("(c p) n -> p c n", p=128)
        for wt in range(nwt):
            woff = wt * 512
            wlen = min(512, Tq - woff)
            c.wait("sync", "v", zq_free)
            c.op("sync", lambda e, woff=woff, wlen=wlen: e.dma_start(
                out=zq[:].rearrange("p (c n) -> p c n", n=512)[:, :, 0:wlen],
                in_=zr[:, :, qs_off + woff: qs_off + woff + wlen]), inc="d0", n=16)
            c.wait("sync", "t", ot_free)
            c.op("sync", lambda e, woff=woff, wlen=wlen: e.dma_start(
                out=otile[:].rearrange("p (c n) -> p c n", n=512)[:, :, 0:wlen],
                in_=o_pk[:, :, qs_off + woff: qs_off + woff + wlen]), inc="d1", n=16)
            c.wait("vector", "d0")
            c.wait("tensor", "d0")
            c.wait("tensor", "d1")
            c.wait("vector", "e0" if wt % 2 == 1 else "e1", xmo_dma[1 - wt % 2])
            for dc in range(8):
                bank = dc % 3
                if wo_bank_free.get(bank):
                    c.wait("tensor", "v", wo_bank_free[bank])
                for ch in range(4):
                    c.op("tensor", lambda e, dc=dc, ch=ch, wlen=wlen,
                         bank=bank: e.matmul(
                        p_wo[:, bank * 512: bank * 512 + wlen],
                        woB[:, ch * 1024 + dc * 128: ch * 1024 + (dc + 1) * 128],
                        otile[:, ch * 512: ch * 512 + wlen],
                        start=(ch == 0), stop=(ch == 3)),
                        inc=("t" if ch == 3 else None))
                mm_t = c.counts["t"]
                c.wait("vector", "t", mm_t)
                c.op("vector", lambda e, dc=dc, bank=bank, wlen=wlen: e.scalar_tensor_tensor(
                    xmo[:, dc * 512: dc * 512 + wlen],
                    p_wo[:, bank * 512: bank * 512 + wlen], 1.0,
                    zq[:, dc * 512: dc * 512 + wlen], ALU.mult, ALU.add), inc="v")
                wo_bank_free[bank] = c.counts["v"]
            zq_free = c.counts["v"]
            ot_free = c.counts["t"]
            if wt == flag_tile:
                c.op("vector", lambda e, flag_col=flag_col: e.tensor_scalar(
                    xmo[:].rearrange("p (c n) -> p c n", n=512)[:, :, flag_col:flag_col + 1],
                    xmo[:].rearrange("p (c n) -> p c n", n=512)[:, :, flag_col:flag_col + 1],
                    self.flag[:, 0:1], None, ALU.mult), inc="v")
            c.wait("sync", "v")
            esem = "e0" if wt % 2 == 0 else "e1"
            c.op("sync", lambda e, woff=woff, wlen=wlen: e.dma_start(
                out=self.rr(xmid_dram)[:, :, qs_off + woff: qs_off + woff + wlen],
                in_=xmo[:].rearrange("p (c n) -> p c n", n=512)[:, :, 0:wlen]),
                inc=esem, n=16)
            xmo_dma[wt % 2] = c.counts[esem]
        c.wait("sync", "e0", c.counts["e0"])
        c.wait("sync", "e1", c.counts["e1"])
        c.barrier()
        e4.close()

    def ffn_part(self, l, xmid_dram, xout_dram):
        nc, c = self.nc, self.c
        cs_off = CS[l]
        Tc = TD - cs_off
        nt = (Tc + 511) // 512

        e2 = ExitStack()
        sb2 = lambda n, sh, dt: e2.enter_context(nc.sbuf_tensor(f"{n}_f{l}", sh, dt))
        pb2 = lambda n, sh, dt: e2.enter_context(nc.psum_tensor(f"{n}_f{l}", sh, dt))
        w1 = sb2("w1", [128, 8 * 5632], BF16)
        w2 = sb2("w2", [128, 22 * 1024], BF16)
        xmb = sb2("xmb", [128, 8 * 513], BF16)
        xmf = sb2("xmf", [128, 512], F32)
        h2b = sb2("h2b", [128, 512], BF16)
        xnb = sb2("xnb", [128, 8 * 512], BF16)
        gbuf = sb2("gbuf", [128, 512], BF16)
        h4 = sb2("h4", [128, 22 * 512], BF16)
        xob = sb2("xob", [128, 512], F32)
        shg = sb2("shg", [128, 512], F32)
        p_a = pb2("p_a", [128, 2 * 512], F32)
        p_g = pb2("p_g", [128, 2 * 512], F32)
        p_s1 = pb2("p_s1f", [128, 512], F32)
        p_s2 = pb2("p_s2f", [128, 512], F32)
        p_o = pb2("p_of", [128, 2 * 512], F32)

        c.op("gpsimd", lambda e, l=l: e.dma_start(
            out=w1[:].rearrange("p (c n) -> p c n", n=5632),
            in_=self.w1_i[l].rearrange("(c p) n -> p c n", p=128)), inc="g", n=16)
        c.op("gpsimd", lambda e, l=l: e.dma_start(
            out=w2[:].rearrange("p (c n) -> p c n", n=1024),
            in_=self.w2_i[l].rearrange("(c p) n -> p c n", p=128)), inc="g", n=16)
        c.wait("tensor", "g")

        xmr = self.rr(xmid_dram)
        xor_ = self.rr(xout_dram)
        st0, st1, st2 = self.st0, self.st1, self.st2
        g_ = self.lng
        b_ = self.lnb

        tile_free = 0
        tile_free_t = 0
        stat_s = 0
        stat_v = 0
        ab_free = [0, 0]
        gb_free = [0, 0]
        o_free = {}
        xmf_free = [0, 0]
        xob_dma = [0, 0]
        last_h4_v = 0
        for t in range(nt):
            toff = cs_off + t * 512
            tlen = min(512, TD - toff)
            c.wait("gpsimd", "v", tile_free)
            c.wait("gpsimd", "t", tile_free_t)
            c.op("gpsimd", lambda e, toff=toff, tlen=tlen: e.dma_start(
                out=xmb[:].rearrange("p (c n) -> p c n", n=513)[:, :, 0:tlen + 1],
                in_=xmr[:, :, toff - 1: toff + tlen]), inc="g0", n=16)
            c.wait("vector", "g0")
            c.wait("tensor", "g0")
            c.wait("tensor", "s", stat_s)
            c.wait("tensor", "v", stat_v)
            for dc in range(8):
                off = 1 if dc < 4 else 0
                c.op("tensor", lambda e, dc=dc, off=off, tlen=tlen: e.matmul(
                    p_s1[:, 0:tlen], self.onesb[:],
                    xmb[:, dc * 513 + off: dc * 513 + off + tlen],
                    start=(dc == 0), stop=(dc == 7)), inc=("t" if dc == 7 else None))
            for dc in range(8):
                off = 1 if dc < 4 else 0
                if dc > 0:
                    c.wait("vector", "t")
                c.op("vector", lambda e, dc=dc, off=off, tlen=tlen: e.tensor_tensor(
                    h2b[:, 0:tlen],
                    xmb[:, dc * 513 + off: dc * 513 + off + tlen],
                    xmb[:, dc * 513 + off: dc * 513 + off + tlen], ALU.mult), inc="v")
                h2_v = c.counts["v"]
                c.wait("tensor", "v", h2_v)
                c.op("tensor", lambda e, dc=dc, tlen=tlen: e.matmul(
                    p_s2[:, 0:tlen], self.onesb[:], h2b[:, 0:tlen],
                    start=(dc == 0), stop=(dc == 7)), inc="t")
            ln_chain(c, st0, st1, st2, self.s_bc, self.sh_bc, p_s1, p_s2)
            stat_s = c.counts["s"]
            for dc in range(8):
                off = 1 if dc < 4 else 0
                c.op("vector", lambda e, dc=dc, l=l: e.tensor_scalar(
                    shg[:], self.sh_bc[:],
                    g_[:, l * 8 + dc: l * 8 + dc + 1],
                    b_[:, l * 8 + dc: l * 8 + dc + 1],
                    ALU.mult, ALU.add), inc="v")
                c.op("vector", lambda e, dc=dc, off=off, l=l: e.scalar_tensor_tensor(
                    xnb[:, dc * 512:(dc + 1) * 512],
                    xmb[:, dc * 513 + off: dc * 513 + off + 512],
                    g_[:, l * 8 + dc: l * 8 + dc + 1], self.s_bc[:],
                    ALU.mult, ALU.mult), inc="v")
                c.op("vector", lambda e, dc=dc: e.tensor_tensor(
                    xnb[:, dc * 512:(dc + 1) * 512], xnb[:, dc * 512:(dc + 1) * 512],
                    shg[:], ALU.add), inc="v")
            xn_v = c.counts["v"]
            stat_v = xn_v
            tile_free = c.counts["v"]
            tile_free_t = c.counts["t"]
            c.wait("tensor", "v", xn_v)
            for pr in range(22):
                pa = pr % 2
                if ab_free[pa]:
                    c.wait("tensor", "s", gb_free[pa])
                    c.wait("tensor", "v", ab_free[pa])
                for dc in range(8):
                    c.op("tensor", lambda e, pr=pr, dc=dc, pa=pa, tlen=tlen: e.matmul(
                        p_a[:, pa * 512: pa * 512 + tlen],
                        w1[:, dc * 5632 + pr * 256: dc * 5632 + pr * 256 + 128],
                        xnb[:, dc * 512: dc * 512 + tlen],
                        start=(dc == 0), stop=(dc == 7)))
                for dc in range(8):
                    c.op("tensor", lambda e, pr=pr, dc=dc, pa=pa, tlen=tlen: e.matmul(
                        p_g[:, pa * 512: pa * 512 + tlen],
                        w1[:, dc * 5632 + pr * 256 + 128: dc * 5632 + pr * 256 + 256],
                        xnb[:, dc * 512: dc * 512 + tlen],
                        start=(dc == 0), stop=(dc == 7)),
                        inc=("t" if dc == 7 else None))
                mm_t = c.counts["t"]
                c.wait("scalar", "t", mm_t)
                c.wait("scalar", "v", last_h4_v)
                c.op("scalar", lambda e, pr=pr, pa=pa, tlen=tlen, l=l: e.activation(
                    gbuf[:, 0:tlen], p_g[:, pa * 512: pa * 512 + tlen], AF.Gelu,
                    bias=self.b1col[:, l * 44 + pr * 2 + 1: l * 44 + pr * 2 + 2]),
                    inc="s")
                gb_free[pa] = c.counts["s"]
                gl_s = c.counts["s"]
                c.wait("vector", "s", gl_s)
                c.op("vector", lambda e, pr=pr, pa=pa, tlen=tlen, l=l: e.scalar_tensor_tensor(
                    h4[:, pr * 512: pr * 512 + tlen],
                    p_a[:, pa * 512: pa * 512 + tlen],
                    self.b1col[:, l * 44 + pr * 2: l * 44 + pr * 2 + 1],
                    gbuf[:, 0:tlen],
                    ALU.add, ALU.mult), inc="v")
                ab_free[pa] = c.counts["v"]
                last_h4_v = c.counts["v"]
            h4_v = c.counts["v"]
            h4_v = c.counts["v"]
            c.wait("tensor", "v", h4_v)
            for dc in range(8):
                ob = dc % 2
                xs_sem = "e0" if ob == 0 else "e1"
                c.wait("sync", "v", xmf_free[0])
                c.op("sync", lambda e, dc=dc, toff=toff, tlen=tlen: e.dma_start(
                    out=xmf[:, 0:tlen],
                    in_=xmr[:, dc, toff: toff + tlen]), inc="d1", n=16)
                if o_free.get(ob):
                    c.wait("tensor", "v", o_free[ob])
                for kc in range(22):
                    c.op("tensor", lambda e, dc=dc, kc=kc, ob=ob, tlen=tlen: e.matmul(
                        p_o[:, ob * 512: ob * 512 + tlen],
                        w2[:, kc * 1024 + dc * 128: kc * 1024 + (dc + 1) * 128],
                        h4[:, kc * 512: kc * 512 + tlen],
                        start=(kc == 0), stop=(kc == 21)),
                        inc=("t" if kc == 21 else None))
                mm_t = c.counts["t"]
                c.wait("vector", "t", mm_t)
                c.wait("vector", "d1")
                c.wait("vector", "e0" if ob == 1 else "e1", xob_dma[1 - ob])
                c.op("vector", lambda e, dc=dc, ob=ob, tlen=tlen, l=l: e.scalar_tensor_tensor(
                    xob[:, 0:tlen],
                    p_o[:, ob * 512: ob * 512 + tlen],
                    self.b2col[:, l * 8 + dc: l * 8 + dc + 1],
                    xmf[:, 0:tlen], ALU.add, ALU.add),
                    inc="v")
                xmf_free[0] = c.counts["v"]
                o_free[ob] = c.counts["v"]
                if toff <= 1535 < toff + tlen:
                    fc = 1535 - toff
                    c.op("vector", lambda e, fc=fc: e.tensor_scalar(
                        xob[:, fc: fc + 1], xob[:, fc: fc + 1],
                        self.flag[:, 0:1], None, ALU.mult), inc="v")
                    xmf_free[0] = c.counts["v"]
                    o_free[ob] = c.counts["v"]
                c.wait("sync", "v")
                c.op("sync", lambda e, dc=dc, toff=toff, tlen=tlen, xs_sem=xs_sem: e.dma_start(
                    out=xor_[:, dc, toff: toff + tlen],
                    in_=xob[:, 0:tlen]), inc=xs_sem, n=16)
                xob_dma[ob] = c.counts[xs_sem]
        c.wait("sync", "e0", c.counts["e0"])
        c.wait("sync", "e1", c.counts["e1"])
        c.barrier()
        e2.close()

    def final_stage(self):
        nc, c = self.nc, self.c
        e2 = ExitStack()
        sb2 = lambda n, sh, dt: e2.enter_context(nc.sbuf_tensor(n, sh, dt))
        pb2 = lambda n, sh, dt: e2.enter_context(nc.psum_tensor(n, sh, dt))
        xo_t = [sb2(f"xoF{i}", [128, 8 * 512], F32) for i in range(2)]
        xt_t = [sb2(f"xtF{i}", [128, 8 * 512], F32) for i in range(2)]
        xb_ = sb2("xbF", [128, 8 * 512], BF16)
        x2_ = sb2("x2F", [128, 8 * 512], BF16)
        fo = sb2("foF", [128, 8 * 512], F32)
        p_s1 = pb2("p_s1F", [128, 512], F32)
        p_s2 = pb2("p_s2F", [128, 512], F32)
        st0, st1, st2 = self.st0, self.st1, self.st2
        xout_r = self.rr(self.sc_xout[1])
        xtld_r = self.rr(self.sc_xtld)
        out_r = self.outT.rearrange("(c p) n -> p c n", p=128)
        tile_free = [0, 0]
        stat_s = 0
        stat_v = 0
        fo_dma = [0, 0]
        for t in range(4):
            toff = 1536 + t * 512
            bf = t % 2
            dsem = "d0" if bf == 0 else "d1"
            c.wait("sync", "v", tile_free[bf])
            c.op("sync", lambda e, toff=toff, bf=bf: e.dma_start(
                out=xo_t[bf][:].rearrange("p (c n) -> p c n", n=512),
                in_=xout_r[:, :, toff: toff + 512]), inc=dsem, n=16)
            c.op("sync", lambda e, toff=toff, bf=bf: e.dma_start(
                out=xt_t[bf][:].rearrange("p (c n) -> p c n", n=512),
                in_=xtld_r[:, :, toff: toff + 512]), inc=dsem, n=16)
            c.wait("vector", dsem)
            c.op("vector", lambda e, bf=bf: e.tensor_copy(xb_[:], xo_t[bf][:]), inc="v")
            c.op("vector", lambda e: e.tensor_tensor(x2_[:], xb_[:], xb_[:],
                                                     ALU.mult), inc="v")
            x2_v = c.counts["v"]
            c.wait("tensor", "v", x2_v)
            c.wait("tensor", "s", stat_s)
            c.wait("tensor", "v", stat_v)
            for dc in range(8):
                c.op("tensor", lambda e, dc=dc: e.matmul(
                    p_s1[:], self.onesb[:], xb_[:, dc * 512:(dc + 1) * 512],
                    start=(dc == 0), stop=(dc == 7)))
            for dc in range(8):
                c.op("tensor", lambda e, dc=dc: e.matmul(
                    p_s2[:], self.onesb[:], x2_[:, dc * 512:(dc + 1) * 512],
                    start=(dc == 0), stop=(dc == 7)),
                    inc=("t" if dc == 7 else None))
            ln_chain(c, st0, st1, st2, self.s_bc, self.sh_bc, p_s1, p_s2)
            stat_s = c.counts["s"]
            stat_v = c.counts["v"]
            c.wait("vector", "o0" if bf == 1 else "o1", fo_dma[1 - bf])
            c.op("vector", lambda e, bf=bf: e.scalar_tensor_tensor(
                fo[:].rearrange("p (c n) -> p c n", n=512),
                xo_t[bf][:].rearrange("p (c n) -> p c n", n=512), 1.0,
                rep8(self.s_bc[:]), ALU.mult, ALU.mult), inc="v")
            c.op("vector", lambda e: e.tensor_tensor(
                fo[:].rearrange("p (c n) -> p c n", n=512),
                fo[:].rearrange("p (c n) -> p c n", n=512), rep8(self.sh_bc[:]),
                ALU.add), inc="v")
            c.op("vector", lambda e, bf=bf: e.tensor_tensor(
                fo[:], fo[:], xt_t[bf][:], ALU.add), inc="v")
            tile_free[bf] = c.counts["v"]
            c.wait("sync", "v")
            osem = "o0" if bf == 0 else "o1"
            c.op("sync", lambda e, t=t: e.dma_start(
                out=out_r[:, :, t * 512:(t + 1) * 512],
                in_=fo[:].rearrange("p (c n) -> p c n", n=512)), inc=osem, n=16)
            fo_dma[bf] = c.counts[osem]
        e2.close()


def build_fused():
    nc = bass.Bass("TRN2", target_bir_lowering=False, debug=False)
    es = ExitStack()
    sb = lambda n, shp, dt: es.enter_context(nc.sbuf_tensor(n, shp, dt))
    di = lambda n, shp, dt: nc.dram_tensor(n, shp, dt, kind="ExternalInput").ap()
    with es:
        c = Ctx(nc)
        for s in ("xc", "xw", "d0", "d1", "e0", "e1", "g", "g0", "g1", "t",
                  "s", "v", "o0", "o1", "cc", "cp"):
            c.sem(es, s)

        identb_i = di("identb", [128, 128], BF16)
        onesb_i = di("onesb", [128, 128], BF16)
        oneshh_i = di("oneshh", [128, 2], BF16)
        selT_i = di("selT", [2, 128], BF16)
        onesrow_i = di("onesrow", [1, 512], BF16)
        onesrowf_i = di("onesrow_f", [1, 64], F32)
        csts_i = di("csts", [128, 8], F32)

        sh = {
            "identb": sb("identb_s", [128, 128], BF16),
            "onesb": sb("onesb_s", [128, 128], BF16),
            "oneshh": sb("oneshh_s", [128, 2], BF16),
            "selT": sb("selT_s", [2, 128], BF16),
            "onesrow": sb("onesrow_s", [1, 512], BF16),
            "onesrow_f": sb("onesrowf_s", [1, 64], F32),
        }
        cst = sb("cst_s", [128, 8], F32)
        zf = sb("zf1536", [128, 1536], BF16)
        nc.const_aps.register(F32, 1.0 / 1024.0, cst[:, 0:1])
        nc.const_aps.register(F32, 1e-5, cst[:, 1:2])
        nc.const_aps.register(F32, -0.5, cst[:, 2:3])
        nc.const_aps.register(F32, -1.0, cst[:, 3:4])
        nc.const_aps.register(F32, 1e-30, cst[:, 4:5])
        for dst, src in [(sh["identb"], identb_i), (sh["onesb"], onesb_i),
                         (sh["oneshh"], oneshh_i), (sh["selT"], selT_i),
                         (sh["onesrow"], onesrow_i),
                         (sh["onesrow_f"], onesrowf_i), (cst, csts_i)]:
            c.op("sync", lambda e, d=dst, s=src: e.dma_start(out=d[:], in_=s),
                 inc="xc", n=16)
        c.op("vector", lambda e: e.memset(zf[:], 0.0), inc="v")
        zf_v = c.counts["v"]
        for eng in ("vector", "tensor", "scalar", "gpsimd"):
            c.wait(eng, "xc")

        agin = [nc.dram_tensor(f"agin{k}", [256, 2048], BF16, kind="Internal").ap()
                for k in range(4)]
        agout = [nc.dram_tensor(f"agout{k}", [1024, 2048], BF16,
                                kind="Internal").ap() for k in range(4)]
        attw = nc.dram_tensor("attw", [1024, TD], BF16, kind="Internal").ap()

        attw_r = attw.rearrange("(c p) n -> p c n", p=128)
        c.wait("sync", "v", zf_v)
        for ch in range(8):
            c.op("sync", lambda e, ch=ch: e.dma_start(
                out=attw_r[:, ch, 0:1536], in_=zf[:]), inc="cp", n=16)

        def on_chunk(k):
            c.wait("gpsimd", "o0", c.counts["o0"])
            c.wait("gpsimd", "o1", c.counts["o1"])
            c.op("gpsimd", lambda e, k=k: e.collective_compute(
                "AllGather", mybir.AluOpType.bypass,
                replica_groups=[[0, 1, 2, 3], [4, 5, 6, 7]],
                ins=[agin[k].opt()], outs=[agout[k].opt()]), inc="cc", n=1)

        phase1(nc, c, sh, agin, on_chunk)

        c.wait("gpsimd", "cp")
        cp_sem = c.sems["cp"]
        cc_sem = c.sems["cc"]

        def emit_switch(e):
            for arm in e.switch_core_id(n=128):
                w = (arm.logical % 8) % 4
                if w == 0:
                    e.wait_ge(cc_sem, 1)
                    e.dma_start(out=attw[:, 1536:2560],
                                in_=agout[0][:, 0:1024]).then_inc(cp_sem, 16)
                    e.dma_start(out=attw[:, 2560:3584],
                                in_=agout[0][:, 1024:2048]).then_inc(cp_sem, 16)
                else:
                    e.wait_ge(cc_sem, w)
                    e.dma_start(out=attw[:, 0:1536],
                                in_=agout[w - 1][:, 512:2048]).then_inc(cp_sem, 16)
                    e.wait_ge(cc_sem, w + 1)
                    e.dma_start(out=attw[:, 1536:3584],
                                in_=agout[w][:, :]).then_inc(cp_sem, 16)
        c.raw("gpsimd", emit_switch, inc="cp", n=32)
        c.barrier()

        l2 = L2(nc)
        l2.build(c, sh, attw)

        c.wait("sync", "o0", c.counts["o0"])
        c.wait("sync", "o1", c.counts["o1"])
        with nc.Block() as block:
            c.finalize(block)
    return nc


BF = ml_dtypes.bfloat16


def make_tables(S):
    t = (S - 1536 + np.arange(TD)).astype(np.float64)
    half_idx = np.arange(0, 64, 2, dtype=np.float64)
    inv_freq = 1.0 / 10000.0 ** (half_idx / 64.0)
    freqs = t[:, None] * inv_freq[None, :]
    freqs64 = np.concatenate([freqs, freqs], axis=1)
    base = (half_idx + 0.4 * 64) / (1.4 * 64)
    power = (t - 4096.0) / 256.0
    scale = base[None, :] ** power[:, None]
    scale64 = np.concatenate([scale, scale], axis=1)
    cos, sin = np.cos(freqs64), np.sin(freqs64)
    cs = (cos * scale64).T
    ss = (sin * scale64).T
    ci = (cos / scale64).T
    si = (sin / scale64).T
    out = np.zeros((128, 4 * TD), dtype=BF)
    for i, tab in enumerate([cs, ss, ci, si]):
        tt = np.tile(tab, (2, 1))
        out[:, i * TD:(i + 1) * TD] = tt.astype(BF)
    return out


def make_RT():
    R = np.zeros((128, 128), dtype=np.float32)
    for hb in (0, 64):
        for i in range(32):
            R[hb + i, hb + 32 + i] = -1.0
            R[hb + 32 + i, hb + i] = 1.0
    return np.ascontiguousarray(R.T).astype(BF)


def make_masks(S):
    big = np.float32(-1e30)
    row = np.arange(128)[:, None]
    qq = np.arange(128)[None, :]
    M0 = np.where(qq <= row, 0.0, big).astype(BF)
    M4 = np.where(qq >= row, 0.0, big).astype(BF)
    mreg = np.concatenate([M0, M4], axis=1)
    tiles = []
    for l in range(2):
        for qb in range(NBND[l]):
            qs_abs = S + (QS[l] - 1536) + 128 * qb
            for ko in range(5):
                j_abs = qs_abs - 512 + ko * 128 + row
                q_abs = qs_abs + qq
                valid = (j_abs >= 0) & (j_abs <= q_abs) & (q_abs <= j_abs + 512)
                valid = valid | (q_abs < 0)
                tiles.append(np.where(valid, 0.0, big).astype(BF))
    return mreg, np.stack(tiles)


def slice_pad(arr2d, S, dtype):
    out = np.zeros((1024, TD), dtype=dtype)
    lo, hi = S - 1536, S + 2048
    src_lo, src_hi = max(lo, 0), min(hi, 8192)
    out[:, src_lo - lo: src_hi - lo] = arr2d[:, src_lo:src_hi].astype(dtype)
    return out


def l2_in_maps(inp):
    x = np.asarray(inp["L0_attn_x"], dtype=np.float32)
    w_out = np.asarray(inp["L0_attn_w_out"], dtype=np.float32)
    b_out = np.asarray(inp["L0_attn_b_out"], dtype=np.float32)
    wqkv = np.asarray(inp["L0_ffn_wqkv"], dtype=np.float32)
    q_scale = np.asarray(inp["L0_ffn_q_scale"], dtype=np.float32)
    k_scale = np.asarray(inp["L0_ffn_k_scale"], dtype=np.float32)
    wo = np.asarray(inp["L0_ffn_wo"], dtype=np.float32)
    ln_g = np.asarray(inp["L0_ffn_ln_g"], dtype=np.float32)
    ln_b = np.asarray(inp["L0_ffn_ln_b"], dtype=np.float32)
    w1 = np.asarray(inp["L0_ffn_w1"], dtype=np.float32)
    b1 = np.asarray(inp["L0_ffn_b1"], dtype=np.float32)
    w2 = np.asarray(inp["L0_ffn_w2"], dtype=np.float32)
    b2 = np.asarray(inp["L0_ffn_b2"], dtype=np.float32)

    woutT = np.ascontiguousarray(w_out.T)
    w1r = np.zeros((2, 1024, 5632), dtype=np.float32)
    b1r = np.zeros((2, 1, 5632), dtype=np.float32)
    for l in range(2):
        a, g = w1[l][:, 0:2730], w1[l][:, 2730:5460]
        ba, bg = b1[l][0:2730], b1[l][2730:5460]
        for pr in range(22):
            n = min(128, 2730 - pr * 128)
            w1r[l][:, pr * 256: pr * 256 + n] = a[:, pr * 128: pr * 128 + n]
            w1r[l][:, pr * 256 + 128: pr * 256 + 128 + n] = g[:, pr * 128: pr * 128 + n]
            b1r[l][0, pr * 256: pr * 256 + n] = ba[pr * 128: pr * 128 + n]
            b1r[l][0, pr * 256 + 128: pr * 256 + 128 + n] = bg[pr * 128: pr * 128 + n]
    w2p = np.zeros((2, 2816, 1024), dtype=np.float32)
    w2p[:, 0:2730] = w2
    b2row = b2[:, None, :]
    b1col = np.zeros((128, 88), dtype=np.float32)
    b2col = np.zeros((128, 16), dtype=np.float32)
    for l in range(2):
        for pr in range(22):
            b1col[:, l * 44 + pr * 2] = b1r[l][0, pr * 256: pr * 256 + 128]
            b1col[:, l * 44 + pr * 2 + 1] = b1r[l][0, pr * 256 + 128: pr * 256 + 256]
        for dc in range(8):
            b2col[:, l * 8 + dc] = b2[l][dc * 128:(dc + 1) * 128]
    lng = np.zeros((128, 16), dtype=np.float32)
    lnb = np.zeros((128, 16), dtype=np.float32)
    for l in range(2):
        lng[:, l * 8:(l + 1) * 8] = ln_g[l].reshape(8, 128).T
        lnb[:, l * 8:(l + 1) * 8] = ln_b[l].reshape(8, 128).T
    qs8 = np.stack([np.tile(q_scale[l] * 8.0, 2) for l in range(2)], axis=1)
    ks1 = np.stack([np.tile(k_scale[l], 2) for l in range(2)], axis=1)
    RT = make_RT()
    identb = np.eye(128, dtype=BF)
    onesb = np.ones((128, 128), dtype=BF)
    oneshh = np.zeros((128, 2), dtype=BF)
    oneshh[0:64, 0] = 1; oneshh[64:128, 1] = 1
    selT = np.ascontiguousarray(oneshh.T)
    onesrow = np.ones((1, 512), dtype=BF)
    onesrow_f = np.ones((1, 64), dtype=np.float32)
    csts = np.broadcast_to(np.array(
        [1.0 / 1024.0, 1e-5, -0.5, -1.0, 1e-30, 0, 0, 0], dtype=np.float32),
        (128, 8)).copy()

    in_maps = []
    for core in range(8):
        b, r = core // 4, core % 4
        S = 2048 * r
        mreg, mbnd = make_masks(S)
        flag = np.full((128, 1), 0.0 if r == 0 else 1.0, dtype=np.float32)
        in_maps.append({
            "xTwin": slice_pad(np.ascontiguousarray(x[b].T), S, np.float32),
            "woutT": woutT, "boutrow": b_out[None, :],
            "wqkv": wqkv, "wo": wo, "w1r": w1r, "b1r": b1r, "w2p": w2p,
            "b2row": b2row, "b1col": b1col, "b2col": b2col,
            "lng": lng, "lnb": lnb, "qs8": qs8, "ks1": ks1,
            "tabs": make_tables(S), "RT": RT, "identb": identb, "onesb": onesb,
            "oneshh": oneshh, "selT": selT, "onesrow": onesrow,
            "onesrow_f": onesrow_f, "mreg": mreg, "mbnd": mbnd,
            "flag": flag, "csts": csts,
        })
    return in_maps




def kernel(**inputs):
    ims1 = l1_in_maps(inputs)
    ims2 = l2_in_maps(inputs)
    in_maps = []
    for core in range(8):
        m = dict(ims1[core])
        m.update(ims2[core])
        in_maps.append(m)
    nc = build_fused()
    res = run_bass_kernel_spmd(nc, in_maps, core_ids=list(range(8)))
    out = np.empty((2, 8192, 1024), dtype=np.float32)
    for core in range(8):
        b, r = core // 4, core % 4
        S = 2048 * r
        out[b, S:S + 2048, :] = np.asarray(res.results[core]["outT"]).T
    return out
